# version 18
# baseline (speedup 1.0000x reference)
import contextlib
import numpy as np
import ml_dtypes
import concourse.bass as bass
import concourse.mybir as mybir
from concourse.bass_utils import run_bass_kernel_spmd

F32 = mybir.dt.float32
BF16 = mybir.dt.bfloat16
AF = mybir.ActivationFunctionType
ALU = mybir.AluOpType
AX = mybir.AxisListType
BF = ml_dtypes.bfloat16

B, S, D = 2, 4096, 4096
NCORES = 8
NLOC = 1024
CH = 512
NHQ, NKV, HD = 32, 8, 128
KVD = NKV * HD
LN_EPS = 1e-5


def core_chunks(c):
    cc = c % 4
    return (cc, 7 - cc)


def chunk_owner(j):
    return (j, 0) if j < 4 else (7 - j, 1)


def core_positions(c):
    lo, hi = core_chunks(c)
    return np.concatenate([np.arange(lo * CH, (lo + 1) * CH),
                           np.arange(hi * CH, (hi + 1) * CH)])


def b_gather_specs():
    specs = []
    for j in range(8):
        specs.append(("k", j, 0))
        for kt in range(4):
            specs.append(("v", j, kt))
    for j2pair in ((0, 1), (2, 3)):
        for j in range(8):
            for j2 in j2pair:
                specs.append(("q", j, j2))
    return specs


def c_gather_specs():
    return [(g, half, a) for g in (0, 2, 4, 6, 1, 3, 5, 7)
            for half in range(2) for a in range(4)]


class Sems:
    def __init__(self, nc, stack):
        self.nc = nc
        self.stack = stack
        self.h = {}
        self.n = {}

    def __getitem__(self, name):
        if name not in self.h:
            self.h[name] = self.stack.enter_context(self.nc.semaphore(name))
            self.n[name] = 0
        return self.h[name]

    def inc(self, inst, name, k=1):
        inst.then_inc(self[name], k)
        self.n[name] += k

    def val(self, name):
        self[name]
        return self.n[name]

    def wait(self, e, name, v):
        e.wait_ge(self[name], v)


class TagSems:

    def __init__(self, base, prefix):
        self.base = base
        self.prefix = prefix

    def __getitem__(self, name):
        return self.base[self.prefix + name]

    def inc(self, inst, name, k=1):
        self.base.inc(inst, self.prefix + name, k)

    def val(self, name):
        return self.base.val(self.prefix + name)


_GP_SEM_TOKENS = ("ldc", "w_ld", "b_ld", "c_ot", "c_w_", "d_wc", "d_wbr",
                  "h_wc", "h_w_")


def _is_gp_sem(name):
    if "ldci" in name:
        return False
    return any(t in name for t in _GP_SEM_TOKENS)


class PoolSems:

    def __init__(self, pools, countss):
        self.pools = pools
        self.countss = countss
        self.map = {}
        self.base = {}
        self.n = {}
        self.nexts = [0, 0]

    def _h(self, name):
        w, s = self.map[name]
        return self.pools[w][s]

    def __getitem__(self, name):
        if name not in self.map:
            w = 0 if _is_gp_sem(name) else 1
            s = self.nexts[w]
            self.map[name] = (w, s)
            self.nexts[w] += 1
            self.base[name] = self.countss[w][s]
            self.n[name] = 0
        return self._h(name)

    def inc(self, inst, name, k=1):
        h = self[name]
        inst.then_inc(h, k)
        self.n[name] += k
        w, s = self.map[name]
        self.countss[w][s] += k

    def val(self, name):
        self[name]
        return self.n[name]

    def wait(self, e, name, v):
        e.wait_ge(self[name], self.base[name] + v)


def emit_gate(nc, blk, smbase, delta):
    for engname in ("sync", "vector", "scalar", "tensor", "gpsimd"):
        dec = getattr(blk, engname)

        def _body(e, d=tuple(delta.items())):
            for k, v in d:
                e.wait_ge(smbase[k], v)
        dec(_body)



def emit_launch_A(nc, blk, stack, sm, x_in, wq, wk, wv, cost, sint, ident,
                  qe_out, kt_out, v_out, tag="a", stage=2, gp_hook=None):
    sb = lambda name, shape, dt: stack.enter_context(nc.sbuf_tensor(tag + name, shape, dt))
    ps = lambda name, shape, dt: stack.enter_context(nc.psum_tensor(tag + name, shape, dt))

    xt = sb("xt", [128, 2, 4096], F32)
    xn_t = sb("xn_t", [128, 2, 4096], BF16)
    sq = sb("sq", [128, 4096], F32)
    xnT = sb("xnT", [128, 32, 1024], BF16)
    sums = sb("sums", [128, 8], F32)
    redt = sb("redt", [128, 1], F32)
    msqt = sb("msqt", [128, 1], F32)
    sumsq = sb("sumsq", [128, 8], F32)
    mvec = sb("mvec", [128, 8], F32)
    varv = sb("varv", [128, 8], F32)
    rstd = sb("rstd", [128, 8], F32)
    epsb = sb("epsb", [128, 1], F32)
    wbuf = sb("wbuf", [128, 2, 32, 256], BF16)
    qraw = sb("qraw", [128, 2, 2, 1024], BF16)
    qe_s = sb("qe_s", [128, 2, 2, 1024], BF16)
    rtmp = sb("rtmp", [128, 2, 1024], BF16)
    vstage = sb("vstage", [128, 8, 8, 129], BF16)
    cosb = sb("cosb", [128, 2, 1024], BF16)
    sinb = sb("sinb", [128, 2, 1024], BF16)
    idb = sb("idb", [128, 128], BF16)

    pst = [ps("pst0", [128, 1024], BF16), ps("pst1", [128, 1024], BF16)]
    pp = [ps("pp0", [128, 512], F32), ps("pp1", [128, 512], F32)]

    x_r = x_in.rearrange("(t p) d -> p t d", p=128)
    NHC = 24

    @blk.gpsimd
    def _(e):
        for half in (slice(0, 64), slice(64, 128)):
            sm.inc(e.dma_start(out=cosb[half, 0, :], in_=cost[:, :]), "ldc", 16)
            sm.inc(e.dma_start(out=cosb[half, 1, :], in_=cost[:, :]), "ldc", 16)
            sm.inc(e.dma_start(out=sinb[half, 0, :], in_=sint[:, :]), "ldc", 16)
            sm.inc(e.dma_start(out=sinb[half, 1, :], in_=sint[:, :]), "ldc", 16)

    @blk.sync
    def _(e):
        sm.inc(e.dma_start(out=idb[:, :], in_=ident[:, :]), "ldci", 16)
        for i in range(8):
            if i >= 2:
                sm.wait(e, "x1done", i - 1)
                sm.wait(e, "ssq", i - 1)
                sm.wait(e, "xn_done", i - 1)
            inst = e.dma_start(out=xt[:, i % 2, 0:2048], in_=x_r[:, i, 0:2048])
            sm.inc(inst, f"ldx1_{i % 2}", 16)

    @blk.vector
    def _(e):
        sm.inc(e.memset(epsb[:, :], LN_EPS), "veps", 1)
        sm.inc(e.memset(vstage[:, :, :, 128:129], 1.0), "vinit", 1)

        def cast(i):
            if i >= 2:
                sm.wait(e, "tr_mm", 32 * (i - 1))
            sm.wait(e, "rstd", i + 1)
            sm.wait(e, f"ldxs_{i % 2}", 16 * (i // 2 + 1))
            inst = e.tensor_scalar(xn_t[:, i % 2, :], xt[:, i % 2, :],
                                   mvec[:, i:i + 1], rstd[:, i:i + 1],
                                   ALU.subtract, ALU.mult)
            sm.inc(inst, "xn_done", 1)

        for i in range(8):
            sm.wait(e, f"ldx1_{i % 2}", 16 * (i // 2 + 1))
            sm.wait(e, f"ldxs_{i % 2}", 16 * (i // 2 + 1))
            if i >= 1:
                sm.wait(e, "vcp", 3 * i)
            inst = e.reduce_sum(redt[:, :], xt[:, i % 2, :], AX.X)
            sm.inc(inst, "x1done", 1)
            sm.wait(e, "x1done", i + 1)
            sm.inc(e.tensor_copy(sums[:, i:i + 1], redt[:, :]), "vcp", 1)
            sm.wait(e, "vcp", 3 * i + 1)
            sm.inc(e.tensor_scalar_mul(mvec[:, i:i + 1], sums[:, i:i + 1],
                                       1.0 / 4096.0), "vcp", 1)
            sm.wait(e, "vcp", 3 * i + 2)
            sm.inc(e.tensor_mul(msqt[:, :], mvec[:, i:i + 1], mvec[:, i:i + 1]),
                   "vcp", 1)
            if i >= 1:
                cast(i - 1)
            sm.wait(e, "vcp", 3 * i + 3)
            sm.wait(e, "ssq", i + 1)
            inst = e.scalar_tensor_tensor(varv[:, i:i + 1], sumsq[:, i:i + 1],
                                          1.0 / 4096.0, msqt[:, :],
                                          ALU.mult, ALU.subtract)
            sm.inc(inst, "vvar", 1)
        cast(7)

    @blk.scalar
    def _(e):
        def evac(g_idx):
            i_, grp = divmod(g_idx, 4)
            pbank = pst[g_idx % 2]
            sm.wait(e, "tr_mm", 8 * (g_idx + 1))
            inst = e.copy(
                xnT[:, grp * 8:(grp + 1) * 8, i_ * 128:(i_ + 1) * 128],
                pbank[:, :].rearrange("p (a b) -> p a b", b=128))
            sm.inc(inst, "tr_ev", 1)

        for i in range(2):
            inst = e.dma_start(out=xt[:, i % 2, 2048:4096],
                               in_=x_r[:, i, 2048:4096])
            sm.inc(inst, f"ldxs_{i % 2}", 16)
        sm.wait(e, "veps", 1)
        for i in range(8):
            sm.wait(e, f"ldx1_{i % 2}", 16 * (i // 2 + 1))
            sm.wait(e, f"ldxs_{i % 2}", 16 * (i // 2 + 1))
            if i >= 1:
                sm.wait(e, "ssq", i)
            inst = e.activation(sq[:, :], xt[:, i % 2, :], AF.Square,
                                accum_out=sumsq[:, i:i + 1])
            sm.inc(inst, "ssq", 1)
            sm.wait(e, "vvar", i + 1)
            sm.inc(e.activation(varv[:, i:i + 1], varv[:, i:i + 1], AF.Ln,
                                bias=epsb[:, 0:1]), "sln", 1)
            sm.wait(e, "sln", i + 1)
            inst = e.activation(rstd[:, i:i + 1], varv[:, i:i + 1],
                                AF.Exp, scale=-0.5)
            sm.inc(inst, "rstd", 1)
            if i >= 2:
                for g in range(4):
                    evac(4 * (i - 2) + g)
            if i + 2 < 8:
                sm.wait(e, "x1done", i + 1)
                sm.wait(e, "xn_done", i + 1)
                sm.wait(e, "ssq", i + 1)
                inst = e.dma_start(out=xt[:, i % 2, 2048:4096],
                                   in_=x_r[:, i + 2, 2048:4096])
                sm.inc(inst, f"ldxs_{i % 2}", 16)
        for g_idx in range(24, 32):
            evac(g_idx)

    if stage < 1:
        @blk.gpsimd
        def _(e):
            sm.wait(e, "rstd", 8)
            sm.wait(e, "vvar", 8)
            sm.wait(e, "vcp", 24)
            sm.wait(e, "ssq", 8)
            qe_r = qe_out.rearrange("(hh p) t -> p hh t", p=128)
            dbg = [sums, sumsq, mvec, varv, rstd]
            if stage >= 0.5:
                pass
            for j, tsr in enumerate(dbg):
                inst = e.dma_start(out=qe_r[:, j, 0:8], in_=tsr[:, :])
                sm.inc(inst, "qe_dma", 16)
            sm.wait(e, "qe_dma", sm.val("qe_dma"))
        return

    @blk.tensor
    def _(e):
        sm.wait(e, "ldci", 16)
        for i in range(8):
            for grp in range(4):
                g_idx = i * 4 + grp
                pbank = pst[g_idx % 2]
                sm.wait(e, "xn_done", i + 1)
                if g_idx >= 2:
                    sm.wait(e, "tr_ev", g_idx - 1)
                for k in range(8):
                    dt = grp * 8 + k
                    inst = e.transpose(pbank[:, k * 128:(k + 1) * 128],
                                       xn_t[:, i % 2, dt * 128:(dt + 1) * 128], idb[:, :])
                    sm.inc(inst, "tr_mm", 1)

    if stage < 2:
        @blk.sync
        def _(e):
            sm.wait(e, "tr_ev", 32)
            qe_r = qe_out.rearrange("(hh p) t -> p hh t", p=128)
            inst = e.dma_start(out=qe_r[:, 0:32, :], in_=xnT[:, :, :])
            sm.inc(inst, "qe_dma", 16)
            sm.wait(e, "qe_dma", sm.val("qe_dma"))
        return

    HCS = [("k", wk, i) for i in range(4)] + [("v", wv, i) for i in range(4)] \
        + [("q", wq, i) for i in list(range(0, 16, 2)) + list(range(1, 16, 2))]
    groups = []
    hc_g0 = []
    rope_of_hc = {}
    rn = 0
    for n, (kind, w, i) in enumerate(HCS):
        hc_g0.append(len(groups))
        if kind == "v":
            for tt in range(8):
                groups.append((n, "v", tt))
        else:
            rope_of_hc[n] = rn
            rn += 1
            for ot in range(2):
                for tc in range(2):
                    groups.append((n, kind, (ot, tc)))
    hc_g0.append(len(groups))
    ev_after = []
    ec_ = 0
    for (n, kind, pl) in groups:
        ec_ += 2 if kind == "v" else 1
        ev_after.append(ec_)
    rope_hcs = [n for n, (kind, w, i) in enumerate(HCS) if kind != "v"]

    @blk.gpsimd
    def _(e):
        for n, (kind, w, i) in enumerate(HCS):
            if n >= 2:
                sm.wait(e, "p_mm", hc_g0[n - 1])
            inst = e.dma_start(out=wbuf[:, n % 2, :, :], in_=w[i, :, :, :])
            sm.inc(inst, f"w_ld_{n % 2}", 16)
            if gp_hook is not None:
                gp_hook(e, n)

    @blk.tensor
    def _(e):
        sm.wait(e, "tr_ev", 32)
        for m, (n, kind, pl) in enumerate(groups):
            if m == hc_g0[n]:
                sm.wait(e, f"w_ld_{n % 2}", 16 * (n // 2 + 1))
            pb = pp[m % 2]
            if m >= 2:
                sm.wait(e, "p_ev", ev_after[m - 2])
            if kind == "v":
                tt = pl
                for dm in range(32):
                    inst = e.matmul(pb[:, 0:256],
                                    xnT[:, dm, tt * 128:(tt + 1) * 128],
                                    wbuf[:, n % 2, dm, :],
                                    start=(dm == 0), stop=(dm == 31))
            else:
                ot, tc = pl
                for dm in range(32):
                    inst = e.matmul(pb[:, :],
                                    wbuf[:, n % 2, dm, ot * 128:(ot + 1) * 128],
                                    xnT[:, dm, tc * 512:(tc + 1) * 512],
                                    start=(dm == 0), stop=(dm == 31))
            sm.inc(inst, "p_mm", 1)

    @blk.scalar
    def _(e):
        for m, (n, kind, pl) in enumerate(groups):
            pb = pp[m % 2]
            sm.wait(e, "p_mm", m + 1)
            if kind == "v":
                tt = pl
                vc = HCS[n][2]
                for gg in range(2):
                    g = vc * 2 + gg
                    inst = e.copy(vstage[:, tt, g, 0:128],
                                  pb[:, gg * 128:(gg + 1) * 128])
                    sm.inc(inst, "p_ev", 1)
            else:
                ot, tc = pl
                r = rope_of_hc[n]
                if pl == (0, 0) and r >= 2:
                    sm.wait(e, "rope_done", r - 1)
                inst = e.copy(qraw[:, r % 2, ot, tc * 512:(tc + 1) * 512],
                              pb[:, :])
                sm.inc(inst, "p_ev", 1)

    @blk.vector
    def _(e):
        sm.wait(e, "ldc", 128)
        for r, n in enumerate(rope_hcs):
            sm.wait(e, "p_ev", ev_after[hc_g0[n + 1] - 1])
            if r >= 2:
                sm.wait(e, f"qe_dma_{r % 2}", 16 * (r // 2))
            qr = qraw[0:64, r % 2, :, :]
            qi = qraw[64:128, r % 2, :, :]
            qeE = qe_s[0:64, r % 2, :, :]
            qeO = qe_s[64:128, r % 2, :, :]
            t_lo = rtmp[0:64, :, :]
            t_hi = rtmp[64:128, :, :]
            if r >= 1:
                sm.wait(e, "vr3", r)
                sm.wait(e, "rope_done", r)
            sm.inc(e.tensor_mul(t_lo, qi, sinb[64:128, :, :]), "vr1", 1)
            sm.inc(e.tensor_mul(qeE, qr, cosb[0:64, :, :]), "vr2", 1)
            sm.wait(e, "vr1", r + 1)
            sm.wait(e, "vr2", r + 1)
            sm.inc(e.tensor_sub(qeE, qeE, t_lo), "vr3", 1)
            sm.inc(e.tensor_mul(t_hi, qr, sinb[0:64, :, :]), "vr4", 1)
            sm.inc(e.tensor_mul(qeO, qi, cosb[64:128, :, :]), "vr5", 1)
            sm.wait(e, "vr4", r + 1)
            sm.wait(e, "vr5", r + 1)
            inst = e.tensor_add(qeO, qeO, t_hi)
            sm.inc(inst, "rope_done", 1)

    @blk.sync
    def _(e):
        qe_r = qe_out.rearrange("(hh p) t -> p hh t", p=128)
        kt_r = kt_out.rearrange("(hh p) t -> p hh t", p=128)
        sm.wait(e, "p_ev", ev_after[hc_g0[8] - 1])
        sm.wait(e, "vinit", 1)
        v_r = v_out.rearrange("(t p) c -> p t c", p=128)
        inst = e.dma_start(out=v_r[:, :, :],
                           in_=vstage[:, :, :, :].rearrange("p t g c -> p t (g c)"))
        sm.inc(inst, "v_dma", 16)
        for r, n in enumerate(rope_hcs):
            kind, w, i = HCS[n]
            sm.wait(e, "rope_done", r + 1)
            sm.wait(e, "vr3", r + 1)
            if kind == "q":
                hh = (i % 2) * 16 + (i // 2) * 2
                dst = qe_r[:, hh:hh + 2, :]
            else:
                dst = kt_r[:, i * 2:(i + 1) * 2, :]
            inst = e.dma_start(out=dst, in_=qe_s[:, r % 2, :, :])
            sm.inc(inst, f"qe_dma_{r % 2}", 16)
        sm.wait(e, "qe_dma_0", sm.val("qe_dma_0"))
        sm.wait(e, "qe_dma_1", sm.val("qe_dma_1"))
        sm.wait(e, "v_dma", 16)


def rope_tables():
    f = np.arange(0, HD, 2, dtype=np.float64) / HD
    inv = 1.0 / (10000.0 ** f)
    t = np.arange(S, dtype=np.float64)
    ang = np.outer(inv, t)
    return (np.cos(ang).astype(np.float32), np.sin(ang).astype(np.float32))


def perm_interleave():
    return np.concatenate([np.arange(0, HD, 2), np.arange(1, HD, 2)])



def emit_B_shared(nc, blk, stack, sm, trimask, ident, tag="bs"):
    sb = lambda name, shape, dt: stack.enter_context(nc.sbuf_tensor(tag + name, shape, dt))
    ps = lambda name, shape, dt: stack.enter_context(nc.psum_tensor(tag + name, shape, dt))
    sh = {}
    sh["msk"] = sb("msk", [128, 128], BF16)
    sh["idb"] = sb("idb", [128, 128], BF16)
    sh["zeros"] = sb("zeros", [128, 128], BF16)
    sh["p_sb"] = sb("p_sb", [128, 4, 2, 512], BF16)
    sh["rl_t"] = sb("rl_t", [128, 1], F32)
    sh["rl_t8"] = sb("rl_t8", [128, 8], F32)
    sh["rl"] = sb("rl", [128, 8, 1], F32)
    sh["o_sb"] = sb("o_sb", [128, 8, 128], BF16)
    sh["psT"] = [ps("psT0", [128, 1024], F32), ps("psT1", [128, 1024], F32)]
    sh["pOl"] = ps("pOl", [128, 1536], F32)
    sh["ptr"] = ps("ptr", [128, 1024], BF16)

    @blk.sync
    def _(e):
        sm.inc(e.dma_start(out=sh["msk"][:, :], in_=trimask[:, :]), "sh_ld", 16)
        sm.inc(e.dma_start(out=sh["idb"][:, :], in_=ident[:, :]), "sh_ld", 16)

    @blk.vector
    def _(e):
        sm.inc(e.memset(sh["zeros"][:, :], 0.0), "sh_ld", 1)
    return sh


def emit_launch_B(nc, blk, stack, sm, sh, q_in, k_in, v_in, ot_out, tag, prev=None, fused=None):
    sb = lambda name, shape, dt: stack.enter_context(nc.sbuf_tensor(tag + name, shape, dt))
    qsb = sb("qsb", [128, 4, 4096], BF16)
    ksb = sb("ksb", [128, 4096], BF16)
    vsb = sb("vsb", [128, 32, 129], BF16)
    ot_sb = sb("ot_sb", [128, 4, 4096], BF16)
    msk, idb, zeros = sh["msk"], sh["idb"], sh["zeros"]
    p_sb, rl_t, rl, o_sb = sh["p_sb"], sh["rl_t"], sh["rl"], sh["o_sb"]
    rl_t8 = sh["rl_t8"]
    psT, pOl, ptr = sh["psT"], sh["pOl"], sh["ptr"]

    def ol_ap(a):
        off = (a // 3) * 512 + (a % 3) * 129
        return pOl[:, off:off + 129]

    SCHED = []
    for hp in range(2):
        for qc in range(8):
            group = [(kc, kt, kc == qc, (kt * 128 if kc == qc else 0))
                     for kc in range(qc + 1) for kt in range(4)]
            SCHED.append((qc, hp, group))

    exp_after, msk_after, pv_after, mm_after = [], [], [], []
    ec = mc = pc = mmc = 0
    for gi, (qc, hp, group) in enumerate(SCHED):
        pc += 3
        for (kc, kt, diag, qoff) in group:
            mmc += 2
            ec += 2 if diag else 1
            mc += 2 if diag else 0
            pc += sum(1 for hh in range(2) for qt in range((kt if diag else 0), 4))
            exp_after.append(ec)
            msk_after.append(mc)
            pv_after.append(pc)
            mm_after.append(mmc)
    NGRP = len(SCHED)
    TOT = dict(b_exp=ec, b_msk=mc, b_pv=pc, b_mm=mmc, b_oev=3 * NGRP,
               b_rc=4 * NGRP, b_te=NGRP, b_tr=8 * NGRP)

    def entry_waits(e):
        if prev is not None:
            for s, v in TOT.items():
                sm.wait(e, prev + s, v)

    @blk.sync
    def _(e):
        if fused is None:
            sm.inc(e.dma_start(out=qsb[:, :, :],
                               in_=q_in.rearrange("(j p) t -> p j t", p=128)),
                   tag + "b_ld", 16)
            sm.inc(e.dma_start(out=ksb[:, :], in_=k_in[:, :]), tag + "b_ld", 16)
            sm.inc(e.dma_start(out=vsb[:, :, :],
                               in_=v_in.rearrange("(a p) c -> p a c", p=128)),
                   tag + "b_ld", 16)
        else:
            pass

    if fused is not None:
        @blk.gpsimd
        def _(e):
            import bass_rust as _br
            ktA, vA, u = fused["ktA"], fused["vA"], fused["u"]
            ixs = fused["ixs"]
            cc_hdl, cc_v0, cc_v1 = fused["cc"]
            e.wait_ge(fused["ix_hdl"], fused["ix_val"])
            fl = lambda t: t.rearrange("r t -> (r t)").rearrange("(r o) -> r o", o=1)
            qflat = [fl(fused["qeA0"]), fl(fused["qeA1"])]
            kflat = fl(ktA)
            vflat = fl(vA)
            e.wait_ge(cc_hdl, cc_v0)
            waited1 = [False]
            for n, (kind, j, j2) in enumerate(b_gather_specs()):
                if kind == "q":
                    if j2 >= 2 and not waited1[0]:
                        e.wait_ge(cc_hdl, cc_v1)
                        waited1[0] = True
                    out = qsb[:, j2, j * 512:(j + 1) * 512]
                    flat = qflat[j2 // 2]
                elif kind == "k":
                    out = ksb[:, j * 512:(j + 1) * 512]
                    flat = kflat
                else:
                    out = vsb[:, j * 4 + j2, :]
                    flat = vflat
                inst = e.indirect_dma_start(
                    out, None, flat, _br.IndirectOffsetOnAxis(ixs[:, u, n:n + 1], 0))
                if kind == "q":
                    sm.inc(inst, tag + f"b_ldq{j2 // 2}", 16)
                else:
                    sm.inc(inst, tag + "b_ldkv", 16)

    @blk.vector
    def _(e):
        entry_waits(e)
        s_i = 0
        for g_i, (qc, hp, group) in enumerate(SCHED):
            for (kc, kt, diag, qoff) in group:
                par = s_i % 4
                if diag:
                    sm.wait(e, tag + "b_exp", exp_after[s_i])
                    for hh in range(2):
                        inst = e.tensor_mul(p_sb[:, par, hh, kt * 128:(kt + 1) * 128],
                                            p_sb[:, par, hh, kt * 128:(kt + 1) * 128],
                                            msk[:, :])
                        sm.inc(inst, tag + "b_msk", 1)
                s_i += 1
            sm.wait(e, tag + "b_pv", pv_after[s_i - 1])
            def bank_view(bk, cnt):
                return pOl[:, bk * 512:bk * 512 + cnt * 129] \
                    .rearrange("p (r s) -> p r s", s=129)
            rc = 4 * g_i
            for bk, cnt, a0 in ((0, 3, 0), (1, 3, 3), (2, 2, 6)):
                inst = e.tensor_copy(rl[:, a0:a0 + cnt, :],
                                     bank_view(bk, cnt)[:, :, 128:129])
                sm.inc(inst, tag + "b_rc", 1)
                rc += 1
            sm.wait(e, tag + "b_rc", rc)
            inst = e.reciprocal(rl_t8[:, :], rl[:, :, 0])
            sm.inc(inst, tag + "b_rc", 1)
            sm.wait(e, tag + "b_rc", rc + 1)
            if g_i >= 1:
                sm.wait(e, tag + "b_tr", 8 * g_i)
            for bk, cnt, a0 in ((0, 3, 0), (1, 3, 3), (2, 2, 6)):
                inst = e.tensor_mul(
                    o_sb[:, a0:a0 + cnt, :],
                    bank_view(bk, cnt)[:, :, 0:128],
                    rl_t8[:, a0:a0 + cnt].rearrange("p (r o) -> p r o", o=1)
                        .broadcast_to((128, cnt, 128)))
                sm.inc(inst, tag + "b_oev", 1)

    @blk.tensor
    def _(e):
        sm.wait(e, "sh_ld", 33)
        if fused is None:
            sm.wait(e, tag + "b_ld", 48)
        entry_waits(e)
        s_i = 0
        for g_i, (qc, hp, group) in enumerate(SCHED):
            if fused is not None and qc == 0:
                if hp == 0:
                    sm.wait(e, tag + "b_ldkv", 16 * 40)
                sm.wait(e, tag + f"b_ldq{hp}", 16 * 16)
            if g_i >= 1:
                sm.wait(e, tag + "b_oev", 3 * g_i)
                sm.wait(e, tag + "b_rc", 4 * g_i)
            for bank in range(3):
                inst = e.matmul(pOl[:, bank * 512:(bank + 1) * 512],
                                zeros[:, :], ksb[:, 0:512], start=True, stop=False)
                sm.inc(inst, tag + "b_pv", 1)
            last_touch = {}
            for li, (kc, kt, diag, qoff) in enumerate(group):
                for hh in range(2):
                    for qt in range((kt if diag else 0), 4):
                        last_touch[(hh * 4 + qt) // 3] = (li, hh, qt)
            base_s = s_i

            def emit_sT(li2):
                s2 = base_s + li2
                (kc2, kt2, diag2, qoff2) = group[li2]
                if s2 >= 2:
                    sm.wait(e, tag + "b_exp", exp_after[s2 - 2])
                for hh in range(2):
                    inst = e.matmul(
                        psT[s2 % 2][:, hh * 512 + qoff2:(hh + 1) * 512],
                        ksb[:, (kc2 * 4 + kt2) * 128:(kc2 * 4 + kt2 + 1) * 128],
                        qsb[:, hp * 2 + hh, qc * 512 + qoff2:(qc + 1) * 512],
                        start=True, stop=True)
                    sm.inc(inst, tag + "b_mm", 1)

            emit_sT(0)
            for li, (kc, kt, diag, qoff) in enumerate(group):
                par = s_i % 4
                if li + 1 < len(group):
                    emit_sT(li + 1)
                sm.wait(e, tag + "b_exp", exp_after[s_i])
                if diag:
                    sm.wait(e, tag + "b_msk", msk_after[s_i])
                for hh in range(2):
                    for qt in range((kt if diag else 0), 4):
                        a = hh * 4 + qt
                        is_last = last_touch[a // 3] == (li, hh, qt)
                        inst = e.matmul(ol_ap(a),
                                        p_sb[:, par, hh, qt * 128:(qt + 1) * 128],
                                        vsb[:, kc * 4 + kt, :],
                                        start=False, stop=is_last)
                        sm.inc(inst, tag + "b_pv", 1)
                s_i += 1
            sm.wait(e, tag + "b_oev", 3 * (g_i + 1))
            if g_i >= 1:
                sm.wait(e, tag + "b_te", g_i)
            for a in range(8):
                inst = e.transpose(ptr[:, a * 128:(a + 1) * 128], o_sb[:, a, :],
                                   idb[:, :])
                sm.inc(inst, tag + "b_tr", 1)

    @blk.scalar
    def _(e):
        entry_waits(e)
        s_i = 0
        for g_i, (qc, hp, group) in enumerate(SCHED):
            for (kc, kt, diag, qoff) in group:
                par = s_i % 4
                pb = psT[s_i % 2]
                sm.wait(e, tag + "b_mm", mm_after[s_i])
                if s_i >= 4:
                    sm.wait(e, tag + "b_pv", pv_after[s_i - 4])
                if not diag:
                    inst = e.activation(p_sb[:, par, :, :], pb[:, :], AF.Exp,
                                        scale=1.0 / np.sqrt(128.0))
                    sm.inc(inst, tag + "b_exp", 1)
                else:
                    for hh in range(2):
                        inst = e.activation(p_sb[:, par, hh, qoff:512],
                                            pb[:, hh * 512 + qoff:(hh + 1) * 512],
                                            AF.Exp, scale=1.0 / np.sqrt(128.0))
                        sm.inc(inst, tag + "b_exp", 1)
                s_i += 1
            sm.wait(e, tag + "b_tr", 8 * (g_i + 1))
            inst = e.copy(ot_sb[:, hp * 2:hp * 2 + 2, qc * 512:(qc + 1) * 512],
                          ptr[:, :].rearrange("p (j q) -> p j q", q=512))
            sm.inc(inst, tag + "b_te", 1)

    @blk.sync
    def _(e):
        sm.wait(e, tag + "b_te", NGRP)
        inst = e.dma_start(out=ot_out.rearrange("(j p) t -> p j t", p=128),
                           in_=ot_sb[:, :, :])
        sm.inc(inst, tag + "b_odma", 16)
        sm.wait(e, tag + "b_odma", 16)


def build_nc_B():
    nc = bass.Bass("TRN2", target_bir_lowering=False)
    ins = {}
    for u in range(2):
        ins[f"q_in{u}"] = nc.dram_tensor(f"q_in{u}", [512, 4096], BF16, kind="ExternalInput").ap()
        ins[f"k_in{u}"] = nc.dram_tensor(f"k_in{u}", [128, 4096], BF16, kind="ExternalInput").ap()
        ins[f"v_in{u}"] = nc.dram_tensor(f"v_in{u}", [4096, 129], BF16, kind="ExternalInput").ap()
    trimask = nc.dram_tensor("trimask", [128, 128], BF16, kind="ExternalInput").ap()
    ident = nc.dram_tensor("ident", [128, 128], BF16, kind="ExternalInput").ap()
    outs = {}
    for u in range(2):
        outs[f"ot_out{u}"] = nc.dram_tensor(f"ot_out{u}", [512, 4096], BF16, kind="ExternalOutput").ap()
    with contextlib.ExitStack() as stack:
        sm = Sems(nc, stack)
        blk = stack.enter_context(nc.Block())
        sh = emit_B_shared(nc, blk, stack, sm, trimask, ident)
        emit_launch_B(nc, blk, stack, sm, sh, ins["q_in0"], ins["k_in0"],
                      ins["v_in0"], outs["ot_out0"], tag="u0")
        emit_launch_B(nc, blk, stack, sm, sh, ins["q_in1"], ins["k_in1"],
                      ins["v_in1"], outs["ot_out1"], tag="u1", prev="u0")
    return nc



def emit_launch_C(nc, blk, stack, sm, ot_in, wo, x_in, xo_out, tag="c", fused_ot=None):
    sb = lambda name, shape, dt: stack.enter_context(nc.sbuf_tensor(tag + name, shape, dt))
    ps = lambda name, shape, dt: stack.enter_context(nc.psum_tensor(tag + name, shape, dt))

    otb = sb("otb", [128, 32, 1024], BF16)
    wbuf = sb("wbuf", [128, 2, 32, 512], BF16)
    xpc = sb("xpc", [128, 2, 512], F32)
    xo = sb("xo", [128, 2, 512], F32)
    pp = [ps("pp0", [128, 512], F32), ps("pp1", [128, 512], F32)]

    x_r = x_in.rearrange("(t p) d -> p t d", p=128)
    xo_r = xo_out.rearrange("(t p) d -> p t d", p=128)

    @blk.sync
    def _(e):
        if fused_ot is None:
            sm.inc(e.dma_start(out=otb[:, :, :],
                               in_=ot_in.rearrange("(a p) t -> p a t", p=128)),
                   "c_ot", 16)

        def store(j):
            sm.wait(e, "c_ev", j + 1)
            inst = e.dma_start(out=xo_r[:, j % 8, (j // 8) * 512:(j // 8 + 1) * 512],
                               in_=xo[:, j % 2, :])
            sm.inc(inst, f"c_o_{j % 2}", 16)
        for k in range(64):
            ec, tt = divmod(k, 8)
            if k >= 2:
                sm.wait(e, "c_ev", k - 1)
            inst = e.dma_start(out=xpc[:, k % 2, :],
                               in_=x_r[:, tt, ec * 512:(ec + 1) * 512])
            sm.inc(inst, f"c_x_{k % 2}", 16)
            if k >= 2:
                store(k - 2)
        store(62)
        store(63)

    if fused_ot is not None:
        @blk.gpsimd
        def _(e):
            import bass_rust as _br
            otA_aps, ixs, ix_hdl, ix_val, cco_hdl, cco_base = fused_ot
            e.wait_ge(ix_hdl, ix_val)
            oflats = [a_.rearrange("r t -> (r t)").rearrange("(r o) -> r o", o=1)
                      for a_ in otA_aps]
            def wld(ec):
                if ec >= 2:
                    sm.wait(e, "c_mm", 8 * (ec - 1))
                inst = e.dma_start(out=wbuf[:, ec % 2, :, :], in_=wo[ec, :, :, :])
                sm.inc(inst, f"c_w_{ec % 2}", 16)

            e.wait_ge(cco_hdl, cco_base + 1)
            specs = c_gather_specs()
            for n, (g, half, a) in enumerate(specs[:32]):
                inst = e.indirect_dma_start(
                    otb[:, g * 4 + a, half * 512:(half + 1) * 512], None,
                    oflats[g % 2],
                    _br.IndirectOffsetOnAxis(ixs[:, n:n + 1], 0))
                sm.inc(inst, f"c_ot{g % 2}", 16)
            wld(0)
            wld(1)
            e.wait_ge(cco_hdl, cco_base + 2)
            for n, (g, half, a) in enumerate(specs[32:]):
                inst = e.indirect_dma_start(
                    otb[:, g * 4 + a, half * 512:(half + 1) * 512], None,
                    oflats[g % 2],
                    _br.IndirectOffsetOnAxis(ixs[:, n + 32:n + 33], 0))
                sm.inc(inst, f"c_ot{g % 2}", 16)
            for ec in range(2, 8):
                wld(ec)

    if fused_ot is None:
        @blk.gpsimd
        def _(e):
            for ec in range(8):
                if ec >= 2:
                    sm.wait(e, "c_mm", 8 * (ec - 1))
                inst = e.dma_start(out=wbuf[:, ec % 2, :, :], in_=wo[ec, :, :, :])
                sm.inc(inst, f"c_w_{ec % 2}", 16)

    EVENS = [g * 4 + a for g in (0, 2, 4, 6) for a in range(4)]
    ODDS = [g * 4 + a for g in (1, 3, 5, 7) for a in range(4)]

    @blk.tensor
    def _(e):
        if fused_ot is None:
            sm.wait(e, "c_ot", 16)
        else:
            sm.wait(e, "c_ot0", 16 * 32)

        def evens(k):
            ec, tt = divmod(k, 8)
            if tt == 0:
                sm.wait(e, f"c_w_{ec % 2}", 16 * (ec // 2 + 1))
            pb = pp[k % 2]
            if k >= 2:
                sm.wait(e, "c_ev", k - 1)
            for i_, dm in enumerate(EVENS):
                e.matmul(pb[:, :], otb[:, dm, tt * 128:(tt + 1) * 128],
                         wbuf[:, ec % 2, dm, :],
                         start=(i_ == 0), stop=False)

        def odds(k):
            ec, tt = divmod(k, 8)
            pb = pp[k % 2]
            for i_, dm in enumerate(ODDS):
                inst = e.matmul(pb[:, :], otb[:, dm, tt * 128:(tt + 1) * 128],
                                wbuf[:, ec % 2, dm, :],
                                start=False, stop=(i_ == 15))
            sm.inc(inst, "c_mm", 1)

        for k in range(64):
            evens(k)
            if k == 0 and fused_ot is not None:
                sm.wait(e, "c_ot1", 16 * 32)
            if k >= 1:
                odds(k - 1)
        odds(63)

    @blk.vector
    def _(e):
        for k in range(64):
            sm.wait(e, "c_mm", k + 1)
            sm.wait(e, f"c_x_{k % 2}", 16 * (k // 2 + 1))
            if k >= 2:
                sm.wait(e, f"c_o_{k % 2}", 16 * (k // 2))
            inst = e.tensor_add(xo[:, k % 2, :], pp[k % 2][:, :], xpc[:, k % 2, :])
            sm.inc(inst, "c_ev", 1)

    @blk.sync
    def _(e):
        sm.wait(e, "c_o_0", 16 * 32)
        sm.wait(e, "c_o_1", 16 * 32)


def build_nc_C():
    nc = bass.Bass("TRN2", target_bir_lowering=False)
    ot_in = nc.dram_tensor("ot_in", [4096, 1024], BF16, kind="ExternalInput").ap()
    wo = nc.dram_tensor("wo", [4096, 4096], F32, kind="ExternalInput").ap()
    x_in = nc.dram_tensor("x_in", [1024, 4096], F32, kind="ExternalInput").ap()
    xo_out = nc.dram_tensor("xo_out", [1024, 4096], F32, kind="ExternalOutput").ap()
    with contextlib.ExitStack() as stack:
        sm = Sems(nc, stack)
        blk = stack.enter_context(nc.Block())
        emit_launch_C(nc, blk, stack, sm, ot_in, wo, x_in, xo_out)
    return nc
RMS_EPS = 1.1920929e-07


def emit_launch_D(nc, blk, stack, sm, x_in, wcat, b24, gate_in, wbr, ident,
                  xf_out, tag="d"):
    sb = lambda name, shape, dt: stack.enter_context(nc.sbuf_tensor(tag + name, shape, dt))
    ps = lambda name, shape, dt: stack.enter_context(nc.psum_tensor(tag + name, shape, dt))

    xt = sb("xt", [128, 2, 4096], F32)
    scratch = sb("scratch", [128, 2, 4096], F32)
    xn_tok = sb("xn_tok", [128, 8, 4096], BF16)
    sums = sb("sums", [128, 8], F32)
    redt = sb("redt", [128, 1], F32)
    msqt = sb("msqt", [128, 1], F32)
    sumsq = sb("sumsq", [128, 8], F32)
    mvec = sb("mvec", [128, 8], F32)
    varv = sb("varv", [128, 8], F32)
    rstd = sb("rstd", [128, 8], F32)
    rsum = sb("rsum", [128, 8], F32)
    r2 = sb("r2", [128, 8], F32)
    epsb = sb("epsb", [128, 1], F32)
    eps2b = sb("eps2b", [128, 1], F32)
    xnT_g = sb("xnT_g", [128, 2, 8, 128], BF16)
    wcatb = sb("wcatb", [128, 32, 24], BF16)
    b24b = sb("b24b", [128, 24], F32)
    gateb = sb("gateb", [128, 1], F32)
    zer5 = sb("zer5", [128, 512], BF16)
    lgf = sb("lgf", [128, 1024], BF16)
    Lt = sb("Lt", [128, 8, 24], F32)
    LL = sb("LL", [128, 8, 24], F32)
    EZ = sb("EZ", [128, 8, 16], F32)
    EZT = sb("EZT", [128, 8, 16], F32)
    e44 = sb("e44", [128, 8, 16], F32)
    s4u = sb("s4u", [128, 8, 4], F32)
    s4v = sb("s4v", [128, 8, 4], F32)
    ru = sb("ru", [128, 8, 4], F32)
    cv = sb("cv", [128, 8, 4], F32)
    H = sb("H", [128, 8, 16], F32)
    Hpre = sb("Hpre", [128, 8, 4], F32)
    Hpost = sb("Hpost", [128, 8, 4], F32)
    sred = sb("sred", [128, 8], F32)
    srec = sb("srec", [128, 8], F32)
    g_s = sb("g_s", [128, 1], F32)
    g_t = sb("g_t", [128, 1], F32)
    omg = sb("omg", [128, 1], F32)
    xp = sb("xp", [128, 8, 1024], BF16)
    xpT = sb("xpT", [128, 8, 1024], BF16)
    wbrb = sb("wbrb", [128, 8, 1024], BF16)
    y_f = sb("y_f", [128, 8, 1024], BF16)
    y_tok = xp
    idb = sb("idb", [128, 128], BF16)

    ptr = [ps("ptr0", [128, 1024], BF16), ps("ptr1", [128, 1024], BF16)]
    pl = ps("pl", [24, 1024], F32)
    py = [ps("py0", [128, 512], F32), ps("py1", [128, 512], F32)]

    x_r = x_in.rearrange("(t p) d -> p t d", p=128)
    xf_r = xf_out.rearrange("(t p) d -> p t d", p=128)

    def v44(t):
        return t[:, :, :].rearrange("p s (i j) -> p s i j", j=4)

    def b_2(ap3):
        return ap3.rearrange("p s (o j) -> p s o j", o=1).broadcast_to((128, 8, 4, 4))

    def b_3(ap3):
        return ap3.rearrange("p s (i o) -> p s i o", o=1).broadcast_to((128, 8, 4, 4))

    @blk.gpsimd
    def _(e):
        sm.inc(e.dma_start(out=wcatb[:, :, :], in_=wcat[:, :, :]), "d_wc", 16)
        sm.inc(e.dma_start(out=b24b[:, :], in_=b24[:, :]), "d_wc", 16)
        sm.inc(e.dma_start(out=gateb[:, :], in_=gate_in[:, :]), "d_wc", 16)
        sm.inc(e.dma_start(out=idb[:, :], in_=ident[:, :]), "d_wc", 16)
        sm.inc(e.dma_start(out=wbrb[:, :, :], in_=wbr[:, :, :]), "d_wbr", 16)

    @blk.sync
    def _(e):
        for i in range(8):
            if i >= 2:
                sm.wait(e, "d_red", i - 1)
                sm.wait(e, "d_ssq", i - 1)
                sm.wait(e, "d_xn", i - 1)
            sm.inc(e.dma_start(out=xt[:, i % 2, 0:2048], in_=x_r[:, i, 0:2048]),
                   f"d_x1_{i % 2}", 16)

    @blk.vector
    def _(e):
        sm.inc(e.memset(epsb[:, :], LN_EPS), "d_eps", 1)
        sm.inc(e.memset(eps2b[:, :], RMS_EPS), "d_eps", 1)
        sm.inc(e.memset(lgf[:, :], 0.0), "d_eps", 1)
        sm.inc(e.memset(zer5[:, :], 0.0), "d_eps", 1)
        sm.inc(e.memset(cv[:, :, :], 1.0), "d_eps", 1)

        def cast(i):
            sm.wait(e, "d_rstd", i + 1)
            sm.wait(e, f"d_xs_{i % 2}", 16 * (i // 2 + 1))
            inst = e.tensor_scalar(xn_tok[:, i, :], xt[:, i % 2, :],
                                   mvec[:, i:i + 1], rstd[:, i:i + 1],
                                   ALU.subtract, ALU.mult)
            sm.inc(inst, "d_xn", 1)

        for i in range(8):
            sm.wait(e, f"d_x1_{i % 2}", 16 * (i // 2 + 1))
            sm.wait(e, f"d_xs_{i % 2}", 16 * (i // 2 + 1))
            if i >= 1:
                sm.wait(e, "d_cp", 3 * i)
            inst = e.reduce_sum(redt[:, :], xt[:, i % 2, :], AX.X)
            sm.inc(inst, "d_red", 1)
            sm.wait(e, "d_red", i + 1)
            sm.inc(e.tensor_copy(sums[:, i:i + 1], redt[:, :]), "d_cp", 1)
            sm.wait(e, "d_cp", 3 * i + 1)
            sm.inc(e.tensor_scalar_mul(mvec[:, i:i + 1], sums[:, i:i + 1],
                                       1.0 / 4096.0), "d_cp", 1)
            sm.wait(e, "d_cp", 3 * i + 2)
            sm.inc(e.tensor_mul(msqt[:, :], mvec[:, i:i + 1], mvec[:, i:i + 1]),
                   "d_cp", 1)
            if i >= 1:
                cast(i - 1)
            sm.wait(e, "d_cp", 3 * i + 3)
            sm.wait(e, "d_ssq", i + 1)
            inst = e.scalar_tensor_tensor(varv[:, i:i + 1], sumsq[:, i:i + 1],
                                          1.0 / 4096.0, msqt[:, :],
                                          ALU.mult, ALU.subtract)
            sm.inc(inst, "d_var", 1)
        cast(7)

    @blk.scalar
    def _(e):
        def evac(g_idx):
            par = g_idx % 2
            sm.wait(e, "d_tr", 8 * (g_idx + 1))
            if g_idx >= 2:
                sm.wait(e, "d_lg", 2 + 8 * (g_idx - 1))
            inst = e.copy(xnT_g[:, par, :, :],
                          ptr[par][:, :].rearrange("p (a b) -> p a b", b=128))
            sm.inc(inst, "d_tev", 1)

        for i in range(2):
            inst = e.dma_start(out=xt[:, i % 2, 2048:4096],
                               in_=x_r[:, i, 2048:4096])
            sm.inc(inst, f"d_xs_{i % 2}", 16)
        sm.wait(e, "d_eps", 5)
        for i in range(8):
            sm.wait(e, f"d_x1_{i % 2}", 16 * (i // 2 + 1))
            sm.wait(e, f"d_xs_{i % 2}", 16 * (i // 2 + 1))
            if i >= 1:
                sm.wait(e, "d_ssq", i)
            inst = e.activation(scratch[:, 0, :], xt[:, i % 2, :], AF.Square,
                                accum_out=sumsq[:, i:i + 1])
            sm.inc(inst, "d_ssq", 1)
            sm.wait(e, "d_var", i + 1)
            sm.inc(e.activation(varv[:, i:i + 1], varv[:, i:i + 1], AF.Ln,
                                bias=epsb[:, 0:1]), "d_ln", 1)
            sm.wait(e, "d_ln", i + 1)
            sm.inc(e.activation(rstd[:, i:i + 1], varv[:, i:i + 1],
                                AF.Exp, scale=-0.5), "d_rstd", 1)
            if i >= 2:
                for g in range(4):
                    evac(4 * (i - 2) + g)
            if i + 2 < 8:
                sm.wait(e, "d_red", i + 1)
                sm.wait(e, "d_xn", i + 1)
                sm.wait(e, "d_ssq", i + 1)
                inst = e.dma_start(out=xt[:, i % 2, 2048:4096],
                                   in_=x_r[:, i + 2, 2048:4096])
                sm.inc(inst, f"d_xs_{i % 2}", 16)
        for g_idx in range(24, 32):
            evac(g_idx)

    @blk.scalar
    def _(e):
        for i in range(8):
            sm.wait(e, "d_xn", i + 1)
            if i == 0:
                sm.wait(e, "d_ssq", 8)
            else:
                sm.wait(e, "d_rsq", i)
            inst = e.activation(scratch[:, 0, :], xn_tok[:, i, :], AF.Square,
                                accum_out=rsum[:, i:i + 1])
            sm.inc(inst, "d_rsq", 1)
        sm.wait(e, "d_rsq", 8)
        sm.wait(e, "d_eps", 5)
        sm.inc(e.activation(rsum[:, :], rsum[:, :], AF.Ln, scale=1.0 / 4096.0,
                            bias=eps2b[:, 0:1]), "d_rln", 1)
        sm.wait(e, "d_rln", 1)
        sm.inc(e.activation(r2[:, :], rsum[:, :], AF.Exp, scale=-0.5), "d_r2", 1)

    @blk.tensor
    def _(e):
        sm.wait(e, "d_wc", 64)
        sm.wait(e, "d_eps", 5)
        for bank in range(2):
            inst = e.matmul(pl[:, bank * 512:(bank + 1) * 512], zer5[:, 0:24],
                            zer5[:, :], start=True, stop=False)
            sm.inc(inst, "d_lg", 1)
        g_idx = 0
        for i in range(8):
            for grp in range(4):
                par = g_idx % 2
                pbank = ptr[par]
                sm.wait(e, "d_xn", i + 1)
                if g_idx >= 2:
                    sm.wait(e, "d_tev", g_idx - 1)
                for k in range(8):
                    dt_ = grp * 8 + k
                    inst = e.transpose(pbank[:, k * 128:(k + 1) * 128],
                                       xn_tok[:, i, dt_ * 128:(dt_ + 1) * 128],
                                       idb[:, :])
                    sm.inc(inst, "d_tr", 1)
                sm.wait(e, "d_tev", g_idx + 1)
                for k in range(8):
                    dm = grp * 8 + k
                    stop = (grp == 3 and k == 7 and i in (3, 7))
                    inst = e.matmul(pl[:, i * 128:(i + 1) * 128], wcatb[:, dm, :],
                                    xnT_g[:, par, k, :], start=False, stop=stop)
                    sm.inc(inst, "d_lg", 1)
                g_idx += 1

    @blk.scalar
    def _(e):
        sm.wait(e, "d_lg", 2 + 256)
        inst = e.copy(lgf[0:24, :], pl[:, :])
        sm.inc(inst, "d_lgf", 1)

    @blk.tensor
    def _(e):
        sm.wait(e, "d_lgf", 1)
        sm.wait(e, "d_tev", 32)
        for i in range(8):
            inst = e.transpose(ptr[i % 2][:, (i // 2) * 128:(i // 2 + 1) * 128],
                               lgf[:, i * 128:(i + 1) * 128], idb[:, :])
            sm.inc(inst, "d_ltr", 1)

    @blk.scalar
    def _(e):
        for i in range(8):
            sm.wait(e, "d_ltr", i + 1)
            inst = e.copy(Lt[:, i, :], ptr[i % 2][:, (i // 2) * 128:(i // 2) * 128 + 24])
            sm.inc(inst, "d_lte", 1)
        sm.wait(e, "d_LL", 8)
        sm.inc(e.activation(EZ[:, :, :], LL[:, :, 0:16], AF.Exp, scale=10.0), "d_EZ", 1)
        sm.wait(e, "d_wc", 64)
        sm.inc(e.activation(g_t[:, :], gateb[:, :], AF.Exp, scale=-1.0), "d_gate", 1)
        sm.inc(e.activation(Hpre[:, :, :], LL[:, :, 16:20], AF.Exp), "d_sfe", 1)
        sm.inc(e.activation(Hpost[:, :, :], LL[:, :, 20:24], AF.Exp), "d_sfe", 1)

    @blk.vector
    def _(e):
        sm.wait(e, "d_lte", 8)
        sm.wait(e, "d_r2", 1)
        sm.wait(e, "d_wc", 64)
        for i in range(8):
            inst = e.scalar_tensor_tensor(LL[:, i, :], Lt[:, i, :], r2[:, i:i + 1],
                                          b24b[:, :], ALU.mult, ALU.add)
            sm.inc(inst, "d_LL", 1)
        sm.wait(e, "d_EZ", 1)
        inst = e.tensor_copy(v44(EZT),
                             v44(EZ).rearrange("p s i j -> p s j i"))
        sm.inc(inst, "d_sk", 1)
        sm.wait(e, "d_sk", 1)
        vc = 1
        for it in range(50):
            inst = e.tensor_mul(v44(e44), v44(EZ), b_2(cv))
            sm.inc(inst, "d_sk", 1); vc += 1
            sm.wait(e, "d_sk", vc)
            inst = e.reduce_sum(s4u[:, :, :], v44(e44), AX.X)
            sm.inc(inst, "d_sk", 1); vc += 1
            sm.wait(e, "d_sk", vc)
            inst = e.reciprocal(ru[:, :, :], s4u[:, :, :])
            sm.inc(inst, "d_sk", 1); vc += 1
            sm.wait(e, "d_sk", vc)
            inst = e.tensor_mul(v44(e44), v44(EZT), b_2(ru))
            sm.inc(inst, "d_sk", 1); vc += 1
            sm.wait(e, "d_sk", vc)
            inst = e.reduce_sum(s4v[:, :, :], v44(e44), AX.X)
            sm.inc(inst, "d_sk", 1); vc += 1
            sm.wait(e, "d_sk", vc)
            inst = e.reciprocal(cv[:, :, :], s4v[:, :, :])
            sm.inc(inst, "d_sk", 1); vc += 1
            sm.wait(e, "d_sk", vc)
        inst = e.tensor_mul(v44(H), v44(EZ), b_3(ru))
        sm.inc(inst, "d_sk", 1); vc += 1
        sm.wait(e, "d_sk", vc)
        inst = e.tensor_mul(v44(H), v44(H), b_2(cv))
        sm.inc(inst, "d_sk", 1); vc += 1
        sm.wait(e, "d_sk", vc)
        sm.wait(e, "d_gate", 1)
        sm.inc(e.tensor_scalar_add(g_t[:, :], g_t[:, :], 1.0), "d_g", 1)
        sm.wait(e, "d_g", 1)
        sm.inc(e.reciprocal(g_s[:, :], g_t[:, :]), "d_g", 1)
        sm.wait(e, "d_g", 2)
        sm.inc(e.tensor_scalar(omg[:, :], g_s[:, :], -1.0, 1.0, ALU.mult, ALU.add),
               "d_g", 1)
        sm.wait(e, "d_g", 3)
        sm.inc(e.tensor_scalar_mul(H[:, :, :], H[:, :, :], g_s[:, 0:1]), "d_H", 1)
        sm.wait(e, "d_H", 1)
        for ii in range(4):
            inst = e.tensor_scalar_add(H[:, :, ii * 5:ii * 5 + 1],
                                       H[:, :, ii * 5:ii * 5 + 1], omg[:, 0:1])
            sm.inc(inst, "d_H", 1)
            sm.wait(e, "d_H", 2 + ii)
        for n_, dst in ((1, Hpre), (2, Hpost)):
            sm.wait(e, "d_sfe", n_)
            inst = e.reduce_sum(sred[:, :], dst[:, :, :], AX.X)
            sm.inc(inst, "d_sf", 1)
            sm.wait(e, "d_sf", sm.val("d_sf"))
            inst = e.reciprocal(srec[:, :], sred[:, :])
            sm.inc(inst, "d_sf", 1)
            sm.wait(e, "d_sf", sm.val("d_sf"))
            inst = e.tensor_mul(dst[:, :, :], dst[:, :, :],
                                srec[:, :].rearrange("p (s o) -> p s o", o=1)
                                          .broadcast_to((128, 8, 4)))
            sm.inc(inst, "d_sf", 1)
            sm.wait(e, "d_sf", sm.val("d_sf"))

    @blk.vector
    def _(e):
        sm.wait(e, "d_sf", 3)
        for i in range(8):
            for n in range(4):
                if n == 0:
                    inst = e.tensor_scalar_mul(xp[:, i, :],
                                               xn_tok[:, i, 0:1024],
                                               Hpre[:, i, 0:1])
                else:
                    inst = e.scalar_tensor_tensor(xp[:, i, :],
                                                  xn_tok[:, i, n * 1024:(n + 1) * 1024],
                                                  Hpre[:, i, n:n + 1],
                                                  xp[:, i, :], ALU.mult, ALU.add)
                sm.inc(inst, "d_xp", 1)
                sm.wait(e, "d_xp", sm.val("d_xp"))

    @blk.tensor
    def _(e):
        sm.wait(e, "d_lte", 8)
        for i in range(8):
            par = i % 2
            sm.wait(e, "d_xp", 4 * (i + 1))
            if i >= 2:
                sm.wait(e, "d_xpte", i - 1)
            for ct in range(8):
                inst = e.transpose(ptr[par][:, ct * 128:(ct + 1) * 128],
                                   xp[:, i, ct * 128:(ct + 1) * 128], idb[:, :])
                sm.inc(inst, "d_xpt", 1)

    @blk.scalar
    def _(e):
        for i in range(8):
            par = i % 2
            sm.wait(e, "d_xpt", 8 * (i + 1))
            inst = e.copy(xpT[:, :, i * 128:(i + 1) * 128],
                          ptr[par][:, :].rearrange("p (a b) -> p a b", b=128))
            sm.inc(inst, "d_xpte", 1)

    @blk.tensor
    def _(e):
        sm.wait(e, "d_wbr", 16)
        sm.wait(e, "d_xpte", 8)
        k = 0
        for ct in range(8):
            for tc in range(2):
                pb = py[k % 2]
                if k >= 2:
                    sm.wait(e, "d_yev", k - 1)
                for cdm in range(8):
                    inst = e.matmul(pb[:, :], wbrb[:, cdm, ct * 128:(ct + 1) * 128],
                                    xpT[:, cdm, tc * 512:(tc + 1) * 512],
                                    start=(cdm == 0), stop=(cdm == 7))
                sm.inc(inst, "d_ymm", 1)
                k += 1

    @blk.scalar
    def _(e):
        for k in range(16):
            ct, tc = divmod(k, 2)
            sm.wait(e, "d_ymm", k + 1)
            inst = e.copy(y_f[:, ct, tc * 512:(tc + 1) * 512], py[k % 2][:, :])
            sm.inc(inst, "d_yev", 1)

    @blk.tensor
    def _(e):
        sm.wait(e, "d_yev", 16)
        sm.wait(e, "d_xpte", 8)
        for i in range(8):
            par = i % 2
            if i >= 2:
                sm.wait(e, "d_yte", i - 1)
            for ct in range(8):
                inst = e.transpose(ptr[par][:, ct * 128:(ct + 1) * 128],
                                   y_f[:, ct, i * 128:(i + 1) * 128], idb[:, :])
                sm.inc(inst, "d_ytr", 1)

    @blk.scalar
    def _(e):
        sm.wait(e, "d_xpt", 64)
        for i in range(8):
            par = i % 2
            sm.wait(e, "d_ytr", 8 * (i + 1))
            inst = e.copy(y_tok[:, i, :],
                          ptr[par][:, :].rearrange("p (a b) -> p a b", b=128))
            sm.inc(inst, "d_yte", 1)

    @blk.sync
    def _(e):
        def store(j):
            sm.wait(e, "d_mx", j + 1)
            inst = e.dma_start(out=xf_r[:, j, :], in_=scratch[:, j % 2, :])
            sm.inc(inst, f"d_out_{j % 2}", 16)
        for i in range(8):
            if i < 2:
                sm.wait(e, "d_xn", 8)
            else:
                sm.wait(e, "d_mx", i - 1)
            sm.inc(e.dma_start(out=xt[:, i % 2, :], in_=x_r[:, i, :]), f"d_x3_{i % 2}", 16)
            if i >= 2:
                store(i - 2)
        store(6)
        store(7)
        sm.wait(e, "d_out_0", 64)
        sm.wait(e, "d_out_1", 64)

    @blk.vector
    def _(e):
        sm.wait(e, "d_H", 5)
        sm.wait(e, "d_sf", 6)
        sm.wait(e, "d_yte", 8)
        sm.wait(e, "d_rsq", 8)
        for i in range(8):
            par = i % 2
            acc = scratch[:, par, :]
            sm.wait(e, f"d_x3_{par}", 16 * (i // 2 + 1))
            if i >= 2:
                sm.wait(e, f"d_out_{par}", 16 * (i // 2))
            for ii in range(4):
                a_sl = acc.rearrange("p (n c) -> p n c", c=1024)[:, ii, :]
                for j in range(4):
                    if j == 0:
                        inst = e.tensor_scalar_mul(a_sl, xn_tok[:, i, 0:1024],
                                                   H[:, i, ii * 4:ii * 4 + 1])
                    else:
                        inst = e.scalar_tensor_tensor(
                            a_sl, xn_tok[:, i, j * 1024:(j + 1) * 1024],
                            H[:, i, ii * 4 + j:ii * 4 + j + 1],
                            a_sl, ALU.mult, ALU.add)
                    sm.inc(inst, "d_mv", 1)
                    sm.wait(e, "d_mv", sm.val("d_mv"))
                inst = e.scalar_tensor_tensor(a_sl, y_tok[:, i, :],
                                              Hpost[:, i, ii:ii + 1],
                                              a_sl, ALU.mult, ALU.add)
                sm.inc(inst, "d_mv", 1)
                sm.wait(e, "d_mv", sm.val("d_mv"))
            inst = e.tensor_add(acc, acc, xt[:, par, :])
            sm.inc(inst, "d_mx", 1)
            sm.wait(e, "d_mx", sm.val("d_mx"))


def build_nc_D():
    nc = bass.Bass("TRN2", target_bir_lowering=False)
    x_in = nc.dram_tensor("x_in", [NLOC, D], F32, kind="ExternalInput").ap()
    wcat = nc.dram_tensor("wcat", [128, 32, 24], F32, kind="ExternalInput").ap()
    b24 = nc.dram_tensor("b24", [128, 24], F32, kind="ExternalInput").ap()
    gate_in = nc.dram_tensor("gate_in", [128, 1], F32, kind="ExternalInput").ap()
    wbr = nc.dram_tensor("wbr", [128, 8, 1024], F32, kind="ExternalInput").ap()
    ident = nc.dram_tensor("ident", [128, 128], BF16, kind="ExternalInput").ap()
    xf_out = nc.dram_tensor("xf_out", [NLOC, D], F32, kind="ExternalOutput").ap()
    with contextlib.ExitStack() as stack:
        sm = Sems(nc, stack)
        blk = stack.enter_context(nc.Block())
        emit_launch_D(nc, blk, stack, sm, x_in, wcat, b24, gate_in, wbr, ident, xf_out)
    return nc


def emit_launch_H(nc, blk, stack, sm, x_in, wbase, bbase, wgt, bthr, acat, bcat,
                  selmat, ident, xf_out, tag="h"):
    sb = lambda name, shape, dt: stack.enter_context(nc.sbuf_tensor(tag + name, shape, dt))
    ps = lambda name, shape, dt: stack.enter_context(nc.psum_tensor(tag + name, shape, dt))

    xt = sb("xt", [128, 2, 4096], F32)
    sq = sb("sq", [128, 4096], F32)
    xn_t = sb("xn_t", [128, 2, 4096], BF16)
    xnT = sb("xnT", [128, 32, 1024], BF16)
    sums = sb("sums", [128, 8], F32)
    redt = sb("redt", [128, 1], F32)
    msqt = sb("msqt", [128, 1], F32)
    sumsq = sb("sumsq", [128, 8], F32)
    mvec = sb("mvec", [128, 8], F32)
    varv = sb("varv", [128, 8], F32)
    rstd = sb("rstd", [128, 8], F32)
    epsb = sb("epsb", [128, 1], F32)
    wgtb = sb("wgtb", [128, 32, 9], BF16)
    bthrn = sb("bthrn", [128, 1], F32)
    bthrb = sb("bthrb", [128, 1], F32)
    acatb = sb("acatb", [128, 32, 128], BF16)
    bcatb = sb("bcatb", [128, 4096], BF16)
    selb = sb("selb", [8, 128], BF16)
    onesc = sb("onesc", [1, 128], BF16)
    bbaseb = sb("bbaseb", [1, 4096], BF16)
    lgf = sb("lgf", [128, 1024], BF16)
    G = sb("G", [128, 8, 9], F32)
    thr = sb("thr", [128, 8], F32)
    sred = sb("sred", [128, 8], F32)
    srec = sb("srec", [128, 8], F32)
    gsm = sb("gsm", [128, 8, 8], F32)
    wts = sb("wts", [8, 1024], BF16)
    wtile = sb("wtile", [128, 128], BF16)
    w_exp = sb("w_exp", [128, 1024], BF16)
    hw = sb("hw", [128, 1024], BF16)
    wb = sb("wb", [128, 2, 32, 256], BF16)
    xpc = sb("xpc", [128, 2, 256], F32)
    xo = sb("xo", [128, 2, 256], F32)
    idb = sb("idb", [128, 128], BF16)

    ptr = [ps("ptr0", [128, 1024], BF16), ps("ptr1", [128, 1024], BF16)]
    plg = ps("plg", [9, 1024], F32)
    pb = [ps("pb0", [128, 512], F32), ps("pb1", [128, 512], F32)]

    x_r = x_in.rearrange("(t p) d -> p t d", p=128)
    xf_r = xf_out.rearrange("(t p) d -> p t d", p=128)

    @blk.gpsimd
    def _(e):
        sm.inc(e.dma_start(out=wgtb[:, :, :], in_=wgt[:, :, :]), "h_wc", 16)
        sm.inc(e.dma_start(out=bthrb[:, :], in_=bthr[:, :]), "h_wc", 16)
        sm.inc(e.dma_start(out=acatb[:, :, :], in_=acat[:, :, :]), "h_wc", 16)
        sm.inc(e.dma_start(out=bcatb[:, :], in_=bcat[:, :]), "h_wc", 16)
        sm.inc(e.dma_start(out=selb[:, :], in_=selmat[:, :]), "h_wc", 16)
        sm.inc(e.dma_start(out=bbaseb[:, :], in_=bbase[:, :]), "h_wc", 16)
        sm.inc(e.dma_start(out=idb[:, :], in_=ident[:, :]), "h_wc", 16)
        for oc in range(16):
            if oc >= 2:
                sm.wait(e, "h_bmm", 8 * (oc - 1))
            inst = e.dma_start(out=wb[:, oc % 2, :, :], in_=wbase[oc, :, :, :])
            sm.inc(inst, f"h_w_{oc % 2}", 16)

    @blk.sync
    def _(e):
        for i in range(8):
            if i >= 2:
                sm.wait(e, "h_red", i - 1)
                sm.wait(e, "h_ssq", i - 1)
                sm.wait(e, "h_xn", i - 1)
            sm.inc(e.dma_start(out=xt[:, i % 2, 0:2048], in_=x_r[:, i, 0:2048]),
                   f"h_x1_{i % 2}", 16)

    @blk.vector
    def _(e):
        sm.inc(e.memset(epsb[:, :], LN_EPS), "h_eps", 1)
        sm.inc(e.memset(lgf[:, :], 0.0), "h_eps", 1)
        sm.inc(e.memset(wtile[:, :], 0.0), "h_eps", 1)
        sm.inc(e.memset(onesc[:, :], 1.0), "h_eps", 1)

        def cast(i):
            if i >= 2:
                sm.wait(e, "h_tr", 32 * (i - 1))
            sm.wait(e, "h_rstd", i + 1)
            sm.wait(e, f"h_xs_{i % 2}", 16 * (i // 2 + 1))
            inst = e.tensor_scalar(xn_t[:, i % 2, :], xt[:, i % 2, :],
                                   mvec[:, i:i + 1], rstd[:, i:i + 1],
                                   ALU.subtract, ALU.mult)
            sm.inc(inst, "h_xn", 1)

        for i in range(8):
            sm.wait(e, f"h_x1_{i % 2}", 16 * (i // 2 + 1))
            sm.wait(e, f"h_xs_{i % 2}", 16 * (i // 2 + 1))
            if i >= 1:
                sm.wait(e, "h_cp", 3 * i)
            inst = e.reduce_sum(redt[:, :], xt[:, i % 2, :], AX.X)
            sm.inc(inst, "h_red", 1)
            sm.wait(e, "h_red", i + 1)
            sm.inc(e.tensor_copy(sums[:, i:i + 1], redt[:, :]), "h_cp", 1)
            sm.wait(e, "h_cp", 3 * i + 1)
            sm.inc(e.tensor_scalar_mul(mvec[:, i:i + 1], sums[:, i:i + 1],
                                       1.0 / 4096.0), "h_cp", 1)
            sm.wait(e, "h_cp", 3 * i + 2)
            sm.inc(e.tensor_mul(msqt[:, :], mvec[:, i:i + 1], mvec[:, i:i + 1]),
                   "h_cp", 1)
            if i >= 1:
                cast(i - 1)
            sm.wait(e, "h_cp", 3 * i + 3)
            sm.wait(e, "h_ssq", i + 1)
            inst = e.scalar_tensor_tensor(varv[:, i:i + 1], sumsq[:, i:i + 1],
                                          1.0 / 4096.0, msqt[:, :],
                                          ALU.mult, ALU.subtract)
            sm.inc(inst, "h_var", 1)
        cast(7)

    @blk.scalar
    def _(e):
        def evac(g_idx):
            i_, grp = divmod(g_idx, 4)
            par = g_idx % 2
            sm.wait(e, "h_tr", 8 * (g_idx + 1))
            inst = e.copy(xnT[:, grp * 8:(grp + 1) * 8, i_ * 128:(i_ + 1) * 128],
                          ptr[par][:, :].rearrange("p (a b) -> p a b", b=128))
            sm.inc(inst, "h_tev", 1)

        for i in range(2):
            inst = e.dma_start(out=xt[:, i % 2, 2048:4096],
                               in_=x_r[:, i, 2048:4096])
            sm.inc(inst, f"h_xs_{i % 2}", 16)
        sm.wait(e, "h_eps", 4)
        for i in range(8):
            sm.wait(e, f"h_x1_{i % 2}", 16 * (i // 2 + 1))
            sm.wait(e, f"h_xs_{i % 2}", 16 * (i // 2 + 1))
            if i >= 1:
                sm.wait(e, "h_ssq", i)
            inst = e.activation(sq[:, :], xt[:, i % 2, :], AF.Square,
                                accum_out=sumsq[:, i:i + 1])
            sm.inc(inst, "h_ssq", 1)
            sm.wait(e, "h_var", i + 1)
            sm.inc(e.activation(varv[:, i:i + 1], varv[:, i:i + 1], AF.Ln,
                                bias=epsb[:, 0:1]), "h_ln", 1)
            sm.wait(e, "h_ln", i + 1)
            sm.inc(e.activation(rstd[:, i:i + 1], varv[:, i:i + 1],
                                AF.Exp, scale=-0.5), "h_rstd", 1)
            if i >= 2:
                for g in range(4):
                    evac(4 * (i - 2) + g)
            if i + 2 < 8:
                sm.wait(e, "h_red", i + 1)
                sm.wait(e, "h_xn", i + 1)
                sm.wait(e, "h_ssq", i + 1)
                inst = e.dma_start(out=xt[:, i % 2, 2048:4096],
                                   in_=x_r[:, i + 2, 2048:4096])
                sm.inc(inst, f"h_xs_{i % 2}", 16)
        for g_idx in range(24, 32):
            evac(g_idx)

    @blk.tensor
    def _(e):
        sm.wait(e, "h_wc", 112)
        sm.wait(e, "h_eps", 4)
        for bank in range(2):
            inst = e.matmul(plg[:, bank * 512:(bank + 1) * 512], lgf[:, 0:9],
                            lgf[:, 0:512], start=True, stop=False)
            sm.inc(inst, "h_lg", 1)
        g_idx = 0
        for i in range(8):
            for grp in range(4):
                par = g_idx % 2
                pbank = ptr[par]
                sm.wait(e, "h_xn", i + 1)
                if g_idx >= 2:
                    sm.wait(e, "h_tev", g_idx - 1)
                for k in range(8):
                    dt_ = grp * 8 + k
                    inst = e.transpose(pbank[:, k * 128:(k + 1) * 128],
                                       xn_t[:, i % 2, dt_ * 128:(dt_ + 1) * 128],
                                       idb[:, :])
                    sm.inc(inst, "h_tr", 1)
                g_idx += 1
        sm.wait(e, "h_tev", 32)
        for i in range(8):
            for dm in range(32):
                stop = (dm == 31 and i in (3, 7))
                inst = e.matmul(plg[:, i * 128:(i + 1) * 128], wgtb[:, dm, :],
                                xnT[:, dm, i * 128:(i + 1) * 128],
                                start=False, stop=stop)
                sm.inc(inst, "h_lg", 1)

    @blk.scalar
    def _(e):
        sm.wait(e, "h_lg", 2 + 256)
        inst = e.copy(lgf[0:9, :], plg[:, :])
        sm.inc(inst, "h_lgf", 1)

    @blk.tensor
    def _(e):
        sm.wait(e, "h_lgf", 1)
        sm.wait(e, "h_tev", 32)
        for i in range(8):
            inst = e.transpose(ptr[i % 2][:, (i // 2) * 128:(i // 2 + 1) * 128],
                               lgf[:, i * 128:(i + 1) * 128], idb[:, :])
            sm.inc(inst, "h_ltr", 1)

    @blk.scalar
    def _(e):
        for i in range(8):
            sm.wait(e, "h_ltr", i + 1)
            inst = e.copy(G[:, i, :], ptr[i % 2][:, (i // 2) * 128:(i // 2) * 128 + 9])
            sm.inc(inst, "h_lte", 1)
        sm.wait(e, "h_lte", 8)
        sm.wait(e, "h_negb", 1)
        inst = e.activation(thr[:, :], G[:, :, 8:9].rearrange("p s o -> p (s o)"),
                            AF.Exp, scale=-1.0, bias=bthrn[:, 0:1])
        sm.inc(inst, "h_sg", 1)
        inst = e.activation(gsm[:, :, :], G[:, :, 0:8], AF.Exp)
        sm.inc(inst, "h_ge", 1)

    @blk.vector
    def _(e):
        sm.wait(e, "h_wc", 112)
        sm.inc(e.tensor_scalar_mul(bthrn[:, :], bthrb[:, :], -1.0), "h_negb", 1)
        sm.wait(e, "h_sg", 1)
        sm.inc(e.tensor_scalar_add(thr[:, :], thr[:, :], 1.0), "h_th", 1)
        sm.wait(e, "h_th", 1)
        sm.inc(e.reciprocal(sred[:, :], thr[:, :]), "h_th", 1)
        sm.wait(e, "h_th", 2)
        sm.inc(e.tensor_scalar_mul(thr[:, :], sred[:, :], 0.125), "h_th", 1)
        sm.wait(e, "h_ge", 1)
        sm.wait(e, "h_th", 3)
        sm.inc(e.reduce_sum(sred[:, :], gsm[:, :, :], AX.X), "h_gs", 1)
        sm.wait(e, "h_gs", 1)
        sm.inc(e.reciprocal(srec[:, :], sred[:, :]), "h_gs", 1)
        sm.wait(e, "h_gs", 2)
        inst = e.tensor_mul(gsm[:, :, :], gsm[:, :, :],
                            srec[:, :].rearrange("p (s o) -> p s o", o=1)
                                      .broadcast_to((128, 8, 8)))
        sm.inc(inst, "h_gs", 1)
        sm.wait(e, "h_gs", 3)
        sm.wait(e, "h_th", 3)
        inst = e.tensor_sub(gsm[:, :, :], gsm[:, :, :],
                            thr[:, :].rearrange("p (s o) -> p s o", o=1)
                                     .broadcast_to((128, 8, 8)))
        sm.inc(inst, "h_gs", 1)
        sm.wait(e, "h_gs", 4)
        sm.inc(e.tensor_relu(gsm[:, :, :], gsm[:, :, :]), "h_gs", 1)
        sm.wait(e, "h_gs", 5)
        sm.inc(e.reduce_sum(sred[:, :], gsm[:, :, :], AX.X), "h_gs", 1)
        sm.wait(e, "h_gs", 6)
        sm.inc(e.tensor_single_scalar(srec[:, :], sred[:, :], 0.0, ALU.is_equal),
               "h_gs", 1)
        sm.wait(e, "h_gs", 7)
        sm.inc(e.tensor_add(sred[:, :], sred[:, :], srec[:, :]), "h_gs", 1)
        sm.wait(e, "h_gs", 8)
        sm.inc(e.reciprocal(srec[:, :], sred[:, :]), "h_gs", 1)
        sm.wait(e, "h_gs", 9)
        sm.inc(e.tensor_scalar_mul(srec[:, :], srec[:, :], 2.0), "h_gs", 1)
        sm.wait(e, "h_gs", 10)
        inst = e.tensor_mul(gsm[:, :, :], gsm[:, :, :],
                            srec[:, :].rearrange("p (s o) -> p s o", o=1)
                                      .broadcast_to((128, 8, 8)))
        sm.inc(inst, "h_gs", 1)

    @blk.vector
    def _(e):
        sm.wait(e, "h_gs", 11)
        for i in range(8):
            if i >= 1:
                sm.wait(e, "h_wtr", i)
            inst = e.tensor_copy(wtile[:, 0:8],
                                 gsm[:, i, :].rearrange("p (o c) -> p (o c)", o=1)
                                 if False else gsm[:, i, :])
            sm.inc(inst, "h_wt", 1)
            sm.wait(e, "h_wt", i + 1)

    @blk.tensor
    def _(e):
        sm.wait(e, "h_ltr", 8)
        sm.wait(e, "h_lte", 8)
        for i in range(8):
            sm.wait(e, "h_wt", i + 1)
            inst = e.transpose(ptr[i % 2][:, (i // 2) * 128:(i // 2 + 1) * 128],
                               wtile[:, :], idb[:, :])
            sm.inc(inst, "h_wtr", 1)

    @blk.scalar
    def _(e):
        for i in range(8):
            sm.wait(e, "h_wtr", i + 1)
            inst = e.copy(wts[:, i * 128:(i + 1) * 128],
                          ptr[i % 2][:, (i // 2) * 128:(i // 2) * 128 + 8]
                          .rearrange("p c -> p c") if False else
                          ptr[i % 2][0:8, (i // 2) * 128:(i // 2 + 1) * 128])
            sm.inc(inst, "h_wte", 1)

    @blk.tensor
    def _(e):
        sm.wait(e, "h_wte", 8)
        for tc in range(2):
            inst = e.matmul(pb[tc][:, :], selb[:, :], wts[:, tc * 512:(tc + 1) * 512],
                            start=True, stop=True)
            sm.inc(inst, "h_we", 1)
        sm.wait(e, "h_wee", 2)
        for tc in range(2):
            for dm in range(32):
                inst = e.matmul(pb[tc][:, :], acatb[:, dm, :],
                                xnT[:, dm, tc * 512:(tc + 1) * 512],
                                start=(dm == 0), stop=(dm == 31))
            sm.inc(inst, "h_hmm", 1)

    @blk.scalar
    def _(e):
        for tc in range(2):
            sm.wait(e, "h_we", tc + 1)
            inst = e.copy(w_exp[:, tc * 512:(tc + 1) * 512], pb[tc][:, :])
            sm.inc(inst, "h_wee", 1)

    @blk.vector
    def _(e):
        for tc in range(2):
            sm.wait(e, "h_hmm", tc + 1)
            inst = e.tensor_mul(hw[:, tc * 512:(tc + 1) * 512], pb[tc][:, :],
                                w_exp[:, tc * 512:(tc + 1) * 512])
            sm.inc(inst, "h_hw", 1)

    @blk.tensor
    def _(e):
        sm.wait(e, "h_hw", 2)
        sm.wait(e, "h_eps", 4)
        for k in range(128):
            oc, tt = divmod(k, 8)
            if tt == 0:
                sm.wait(e, f"h_w_{oc % 2}", 16 * (oc // 2 + 1))
            pbk = pb[k % 2]
            if k >= 2:
                sm.wait(e, "h_ev", k - 1)
            for dm in range(32):
                e.matmul(pbk[:, 0:256], xnT[:, dm, tt * 128:(tt + 1) * 128],
                         wb[:, oc % 2, dm, :], start=(dm == 0), stop=False)
            e.matmul(pbk[:, 0:256], onesc[:, :], bbaseb[:, oc * 256:(oc + 1) * 256],
                     start=False, stop=False)
            inst = e.matmul(pbk[:, 0:256], hw[:, tt * 128:(tt + 1) * 128],
                            bcatb[:, oc * 256:(oc + 1) * 256],
                            start=False, stop=True)
            sm.inc(inst, "h_bmm", 1)

    @blk.sync
    def _(e):
        def store(j):
            sm.wait(e, "h_ev", j + 1)
            inst = e.dma_start(out=xf_r[:, j % 8, (j // 8) * 256:(j // 8 + 1) * 256],
                               in_=xo[:, j % 2, :])
            sm.inc(inst, f"h_o_{j % 2}", 16)
        for k in range(128):
            oc, tt = divmod(k, 8)
            if k < 2:
                sm.wait(e, "h_xn", 8)
            else:
                sm.wait(e, "h_ev", k - 1)
            inst = e.dma_start(out=xpc[:, k % 2, :],
                               in_=x_r[:, tt, oc * 256:(oc + 1) * 256])
            sm.inc(inst, f"h_xp_{k % 2}", 16)
            if k >= 2:
                store(k - 2)
        store(126)
        store(127)
        sm.wait(e, "h_o_0", 16 * 64)
        sm.wait(e, "h_o_1", 16 * 64)

    @blk.vector
    def _(e):
        for k in range(128):
            sm.wait(e, "h_bmm", k + 1)
            sm.wait(e, f"h_xp_{k % 2}", 16 * (k // 2 + 1))
            if k >= 2:
                sm.wait(e, f"h_o_{k % 2}", 16 * (k // 2))
            inst = e.tensor_add(xo[:, k % 2, :], pb[k % 2][:, 0:256], xpc[:, k % 2, :])
            sm.inc(inst, "h_ev", 1)


def build_nc_H():
    nc = bass.Bass("TRN2", target_bir_lowering=False)
    x_in = nc.dram_tensor("x_in", [NLOC, D], F32, kind="ExternalInput").ap()
    wbase = nc.dram_tensor("wbase", [16, 128, 32, 256], F32, kind="ExternalInput").ap()
    bbase = nc.dram_tensor("bbase", [1, D], F32, kind="ExternalInput").ap()
    wgt = nc.dram_tensor("wgt", [128, 32, 9], F32, kind="ExternalInput").ap()
    bthr = nc.dram_tensor("bthr", [128, 1], F32, kind="ExternalInput").ap()
    acat = nc.dram_tensor("acat", [128, 32, 128], F32, kind="ExternalInput").ap()
    bcat = nc.dram_tensor("bcat", [128, D], F32, kind="ExternalInput").ap()
    selmat = nc.dram_tensor("selmat", [8, 128], BF16, kind="ExternalInput").ap()
    ident = nc.dram_tensor("ident", [128, 128], BF16, kind="ExternalInput").ap()
    xf_out = nc.dram_tensor("xf_out", [NLOC, D], F32, kind="ExternalOutput").ap()
    with contextlib.ExitStack() as stack:
        sm = Sems(nc, stack)
        blk = stack.enter_context(nc.Block())
        emit_launch_H(nc, blk, stack, sm, x_in, wbase, bbase, wgt, bthr, acat,
                      bcat, selmat, ident, xf_out)
    return nc



def build_nc_fused():
    nc = bass.Bass("TRN2", target_bir_lowering=False)
    ap_ = lambda name, shape, dt, kind: nc.dram_tensor(name, shape, dt, kind=kind).ap()
    EI, EO, IN = "ExternalInput", "ExternalOutput", "Internal"

    x_in = ap_("x_in", [NLOC, D], F32, EI)
    wq0 = ap_("wq0", [16, 128, 32, 256], F32, EI)
    wk0 = ap_("wk0", [4, 128, 32, 256], F32, EI)
    wv0 = ap_("wv0", [4, 128, 32, 256], F32, EI)
    wo0 = ap_("wo0", [8, 128, 32, 512], F32, EI)
    wq1 = ap_("wq1", [16, 128, 32, 256], F32, EI)
    wk1 = ap_("wk1", [4, 128, 32, 256], F32, EI)
    wv1 = ap_("wv1", [4, 128, 32, 256], F32, EI)
    wo1 = ap_("wo1", [8, 128, 32, 512], F32, EI)
    cost = ap_("cost", [64, NLOC], F32, EI)
    sint = ap_("sint", [64, NLOC], F32, EI)
    ident = ap_("ident", [128, 128], BF16, EI)
    trimask = ap_("trimask", [128, 128], BF16, EI)
    wcat = ap_("wcat", [128, 32, 24], F32, EI)
    b24 = ap_("b24", [128, 24], F32, EI)
    gate_in = ap_("gate_in", [128, 1], F32, EI)
    wbr = ap_("wbr", [128, 8, 1024], F32, EI)
    wbase = ap_("wbase", [16, 128, 32, 256], F32, EI)
    bbase = ap_("bbase", [1, D], F32, EI)
    wgt = ap_("wgt", [128, 32, 9], F32, EI)
    bthr = ap_("bthr", [128, 1], F32, EI)
    acat = ap_("acat", [128, 32, 128], F32, EI)
    bcat = ap_("bcat", [128, D], F32, EI)
    selmat = ap_("selmat", [8, 128], BF16, EI)
    bidx = ap_("bidx", [128, 2, 72], mybir.dt.uint32, EI)
    cidx = ap_("cidx", [128, 64], mybir.dt.uint32, EI)
    xf_out = ap_("xf_out", [NLOC, D], F32, EO)

    qe_d = [ap_(f"qe{l}_d", [D, NLOC], BF16, IN) for l in range(2)]
    kt_d = [ap_(f"kt{l}_d", [KVD, NLOC], BF16, IN) for l in range(2)]
    v_d = [ap_(f"v{l}_d", [NLOC, 8 * 129], BF16, IN) for l in range(2)]
    aps_ = lambda name, shape, dt: nc.dram_tensor(name, shape, dt, kind=IN,
                                                  addr_space="Shared").ap()
    qeA = [[aps_(f"qeA{l}h{h_}", [8 * 2048, NLOC], BF16) for h_ in range(2)]
           for l in range(2)]
    ktA = [aps_(f"ktA{l}", [8 * KVD, NLOC], BF16) for l in range(2)]
    vA = [aps_(f"vA{l}", [8 * NLOC, 8 * 129], BF16) for l in range(2)]
    ot_d = [ap_(f"ot{l}_d", [NLOC, D], BF16, IN) for l in range(2)]
    otA_u = [[aps_(f"otA{l}u{u}", [8 * 512, D], BF16) for u in range(2)]
             for l in range(2)]
    x1_d = ap_("x1_d", [NLOC, D], F32, IN)
    x2_d = ap_("x2_d", [NLOC, D], F32, IN)
    x3_d = ap_("x3_d", [NLOC, D], F32, IN)

    with contextlib.ExitStack() as gstack:
        smb = Sems(nc, gstack)
        gp_pool = [gstack.enter_context(nc.semaphore(f"gp{i}")) for i in range(8)]
        ot_pool = [gstack.enter_context(nc.semaphore(f"pl{i}")) for i in range(48)]
        pools = (gp_pool, ot_pool)
        countss = ([0] * 8, [0] * 48)
        blk = gstack.enter_context(nc.Block())
        bixs = gstack.enter_context(nc.sbuf_tensor("bixs", [128, 2, 72], mybir.dt.uint32))
        cixs = gstack.enter_context(nc.sbuf_tensor("cixs", [128, 64], mybir.dt.uint32))

        @blk.gpsimd
        def _(e):
            smb.inc(e.dma_start(out=bixs[:, :, :], in_=bidx[:, :, :]), "ixld", 16)
            smb.inc(e.dma_start(out=cixs[:, :], in_=cidx[:, :]), "ixld", 16)
        ix_hdl = smb["ixld"]

        psm_box = [PoolSems(pools, countss)]

        def psm():
            return psm_box[0]

        def gate():
            cur = psm_box[0]
            waits = [(cur._h(k), cur.base[k] + v)
                     for k, v in cur.n.items() if v > 0]
            for engname in ("sync", "vector", "scalar", "tensor", "gpsimd"):
                dec = getattr(blk, engname)

                def _body(e, w=tuple(waits)):
                    for h, v in w:
                        e.wait_ge(h, v)
                dec(_body)
            psm_box[0] = PoolSems(pools, countss)

        def ag_phase(tagname, pairs):
            @blk.gpsimd
            def _(e):
                for (src, dst) in pairs:
                    inst = e.collective_compute(
                        "AllGather", ALU.bypass,
                        replica_groups=[list(range(8))],
                        ins=[src.opt()], outs=[dst.opt()])
                    smb.inc(inst, tagname, 1)
                e.wait_ge(smb[tagname], smb.val(tagname))
            gate()

        layer_x = [x_in, x2_d]
        layer_w = [(wq0, wk0, wv0, wo0), (wq1, wk1, wv1, wo1)]
        layer_xo = [x1_d, x3_d]

        for l in range(2):
            a_sm = psm()

            def mk_hook(a_sm, l):
                def hook(e, n):
                    if n == 10:
                        e.wait_ge(a_sm["qe_dma_0"], a_sm.base["qe_dma_0"] + 32)
                        e.wait_ge(a_sm["qe_dma_1"], a_sm.base["qe_dma_1"] + 32)
                        inst = e.collective_compute(
                            "AllGather", ALU.bypass,
                            replica_groups=[list(range(8))],
                            ins=[kt_d[l].opt()], outs=[ktA[l].opt()])
                        smb.inc(inst, f"cc{l}", 1)
                    elif n == 12:
                        e.wait_ge(a_sm["v_dma"], a_sm.base["v_dma"] + 16)
                        inst = e.collective_compute(
                            "AllGather", ALU.bypass,
                            replica_groups=[list(range(8))],
                            ins=[v_d[l].opt()], outs=[vA[l].opt()])
                        smb.inc(inst, f"cc{l}", 1)
                    elif n == 18:
                        e.wait_ge(a_sm["qe_dma_0"], a_sm.base["qe_dma_0"] + 96)
                        e.wait_ge(a_sm["qe_dma_1"], a_sm.base["qe_dma_1"] + 96)
                        inst = e.collective_compute(
                            "AllGather", ALU.bypass,
                            replica_groups=[list(range(8))],
                            ins=[qe_d[l][0:2048, :].opt()],
                            outs=[qeA[l][0].opt()])
                        smb.inc(inst, f"cc{l}", 1)
                return hook

            with contextlib.ExitStack() as ps:
                emit_launch_A(nc, blk, ps, a_sm, layer_x[l],
                              layer_w[l][0], layer_w[l][1], layer_w[l][2],
                              cost, sint, ident, qe_d[l], kt_d[l], v_d[l],
                              tag=f"a{l}", gp_hook=mk_hook(a_sm, l))

            @blk.gpsimd
            def _(e, a_sm=a_sm, l=l):
                e.wait_ge(a_sm["qe_dma_0"], a_sm.base["qe_dma_0"] + 160)
                e.wait_ge(a_sm["qe_dma_1"], a_sm.base["qe_dma_1"] + 160)
                inst = e.collective_compute(
                    "AllGather", ALU.bypass,
                    replica_groups=[list(range(8))],
                    ins=[qe_d[l][2048:4096, :].opt()], outs=[qeA[l][1].opt()])
                smb.inc(inst, f"cc{l}", 1)
            gate()
            cc_hdl = smb[f"cc{l}"]
            cc_v1 = smb.val(f"cc{l}")
            cc_v0 = cc_v1 - 1
            with contextlib.ExitStack() as ps:
                smB = psm()
                sh = emit_B_shared(nc, blk, ps, smB, trimask, ident, tag=f"bs{l}")
                emit_launch_B(nc, blk, ps, smB, sh, None, None, None,
                              ot_d[l][0:512, :], tag=f"u0l{l}",
                              fused=dict(qeA0=qeA[l][0], qeA1=qeA[l][1],
                                         ktA=ktA[l], vA=vA[l], u=0,
                                         cc=(cc_hdl, cc_v0, cc_v1),
                                         ixs=bixs, ix_hdl=ix_hdl, ix_val=32))
                emit_launch_B(nc, blk, ps, smB, sh, None, None, None,
                              ot_d[l][512:1024, :], tag=f"u1l{l}", prev=f"u0l{l}",
                              fused=dict(qeA0=qeA[l][0], qeA1=qeA[l][1],
                                         ktA=ktA[l], vA=vA[l], u=1,
                                         cc=(cc_hdl, cc_v0, cc_v1),
                                         ixs=bixs, ix_hdl=ix_hdl, ix_val=32))
            b_sm = psm()

            cco_base = smb.val(f"cco{l}")

            @blk.gpsimd
            def _(e, b_sm=b_sm, l=l):
                e.wait_ge(b_sm[f"u0l{l}b_odma"], b_sm.base[f"u0l{l}b_odma"] + 16)
                inst = e.collective_compute(
                    "AllGather", ALU.bypass, replica_groups=[list(range(8))],
                    ins=[ot_d[l][0:512, :].opt()], outs=[otA_u[l][0].opt()])
                smb.inc(inst, f"cco{l}", 1)
                e.wait_ge(b_sm[f"u1l{l}b_odma"], b_sm.base[f"u1l{l}b_odma"] + 16)
                inst = e.collective_compute(
                    "AllGather", ALU.bypass, replica_groups=[list(range(8))],
                    ins=[ot_d[l][512:1024, :].opt()], outs=[otA_u[l][1].opt()])
                smb.inc(inst, f"cco{l}", 1)
            gate()
            with contextlib.ExitStack() as ps:
                emit_launch_C(nc, blk, ps, psm(), None,
                              layer_w[l][3], layer_x[l], layer_xo[l],
                              tag=f"c{l}",
                              fused_ot=(otA_u[l], cixs, ix_hdl, 32,
                                        smb[f"cco{l}"], cco_base))
            gate()
            if l == 0:
                with contextlib.ExitStack() as ps:
                    emit_launch_D(nc, blk, ps, psm(), x1_d, wcat,
                                  b24, gate_in, wbr, ident, x2_d, tag="d")
                gate()
            else:
                with contextlib.ExitStack() as ps:
                    emit_launch_H(nc, blk, ps, psm(), x3_d, wbase,
                                  bbase, wgt, bthr, acat, bcat, selmat, ident,
                                  xf_out, tag="h")
                gate()
    return nc


def _wshape(w, hc_cols):
    din, dout = w.shape
    n = dout // hc_cols
    return np.ascontiguousarray(
        w.reshape(32, 128, n, hc_cols).transpose(2, 1, 0, 3))


def _cshape(w):
    return np.ascontiguousarray(w.reshape(32, 128, -1).transpose(1, 0, 2))


def kernel_fused(inputs):
    tri, ident, sel = _consts()
    cosf, sinf = rope_tables()
    x = np.ascontiguousarray(np.asarray(inputs["L0_attn_x"], dtype=np.float32))
    p = perm_interleave()
    wq0 = np.ascontiguousarray(np.asarray(inputs["L0_attn_wq"])[:, np.concatenate([h * HD + p for h in range(NHQ)])])
    wk0 = np.ascontiguousarray(np.asarray(inputs["L0_attn_wk"])[:, np.concatenate([g * HD + p for g in range(NKV)])])
    wcat = _cshape(np.concatenate([np.asarray(inputs["L0_ffn_W_res"]),
                                   np.asarray(inputs["L0_ffn_W_pre"]),
                                   np.asarray(inputs["L0_ffn_W_post"])],
                                  0).T.astype(np.float32))
    b24 = np.concatenate([np.asarray(inputs["L0_ffn_H_res_base"]).reshape(16),
                          np.asarray(inputs["L0_ffn_H_pre_base"]),
                          np.asarray(inputs["L0_ffn_H_post_base"])]).astype(np.float32)
    b24r = np.ascontiguousarray(np.broadcast_to(b24, (128, 24)))
    gater = np.full((128, 1), np.asarray(inputs["L0_ffn_gate"])[0], np.float32)
    wbr0 = np.asarray(inputs["L0_ffn_W_branch"], dtype=np.float32)
    wbr = np.ascontiguousarray(wbr0.reshape(8, 128, 1024).transpose(1, 0, 2))
    wbase = _wshape(np.asarray(inputs["L1_ffn_W_base"]).T.astype(np.float32), 256)
    bbase = np.ascontiguousarray(np.asarray(inputs["L1_ffn_b_base"])[None, :].astype(np.float32))
    wgt = _cshape(np.concatenate([np.asarray(inputs["L1_ffn_W_gate"]).T,
                                  np.asarray(inputs["L1_ffn_W_thr"])[:, None]],
                                 1).astype(np.float32))
    bthr = np.full((128, 1), np.asarray(inputs["L1_ffn_b_thr"])[0], np.float32)
    acat = _cshape(np.asarray(inputs["L1_ffn_A_w"]).reshape(128, D).T.astype(np.float32))
    bcat = np.ascontiguousarray(
        np.asarray(inputs["L1_ffn_B_w"]).transpose(0, 2, 1).reshape(128, D).astype(np.float32))

    def build_bidx(c):
        bq = c // 4
        arr = np.zeros((128, 2, 72), np.uint32)
        p = np.arange(128, dtype=np.uint32)
        for u in range(2):
            gq = (c % 4) * 2 + u
            for n, (kind, j, j2) in enumerate(b_gather_specs()):
                cw, h = chunk_owner(j)
                if kind == "q":
                    row = (bq * 4 + cw) * 2048 + gq * 256 + (j2 % 2) * 128 + p
                    arr[:, u, n] = row * 1024 + h * 512
                elif kind == "k":
                    row = bq * 4096 + cw * 1024 + gq * 128 + p
                    arr[:, u, n] = row * 1024 + h * 512
                else:
                    row = bq * 4096 + cw * 1024 + h * 512 + j2 * 128 + p
                    arr[:, u, n] = row * 1032 + gq * 129
        return arr

    def build_cidx(c):
        bq = c // 4
        cc = c % 4
        arr = np.zeros((128, 64), np.uint32)
        p = np.arange(128, dtype=np.uint32)
        for n, (g, half, a) in enumerate(c_gather_specs()):
            row = (bq * 4 + g // 2) * 512 + a * 128 + p
            col = cc * 512 if half == 0 else 3584 - cc * 512
            arr[:, n] = row * 4096 + col
        return arr

    wq0s = _wshape(wq0, 256)
    wk0s = _wshape(wk0, 256)
    wv0s = _wshape(np.asarray(inputs["L0_attn_wv"], dtype=np.float32), 256)
    wo0s = _wshape(np.asarray(inputs["L0_attn_wo"], dtype=np.float32), 512)
    wq1s = _wshape(np.asarray(inputs["L1_attn_Wq"], dtype=np.float32), 256)
    wk1s = _wshape(np.asarray(inputs["L1_attn_Wk"], dtype=np.float32), 256)
    wv1s = _wshape(np.asarray(inputs["L1_attn_Wv"], dtype=np.float32), 256)
    wo1s = _wshape(np.asarray(inputs["L1_attn_Wo"], dtype=np.float32), 512)

    nc = build_nc_fused()
    in_maps = []
    for c in range(NCORES):
        b = c // 4
        pos = core_positions(c)
        in_maps.append({
            "bidx": build_bidx(c), "cidx": build_cidx(c),
            "x_in": np.ascontiguousarray(x[b, pos, :]),
            "wq0": wq0s, "wk0": wk0s, "wv0": wv0s, "wo0": wo0s,
            "wq1": wq1s, "wk1": wk1s, "wv1": wv1s, "wo1": wo1s,
            "cost": np.ascontiguousarray(cosf[:, pos]).astype(np.float32),
            "sint": np.ascontiguousarray(sinf[:, pos]).astype(np.float32),
            "ident": ident, "trimask": tri,
            "wcat": wcat, "b24": b24r, "gate_in": gater, "wbr": wbr,
            "wbase": wbase, "bbase": bbase, "wgt": wgt, "bthr": bthr,
            "acat": acat, "bcat": bcat, "selmat": sel,
        })
    res = run_bass_kernel_spmd(nc, in_maps, core_ids=list(range(NCORES))).results
    out = np.zeros((B, S, D), np.float32)
    for c in range(NCORES):
        b = c // 4
        pos = core_positions(c)
        out[b, pos, :] = res[c]["xf_out"]
    return out


_TRI = None


def _consts():
    global _TRI
    if _TRI is None:
        tri = np.triu(np.ones((128, 128), np.float32)).astype(BF)
        ident = np.eye(128, dtype=BF)
        sel = np.zeros((8, 128), np.float32)
        for e_ in range(8):
            sel[e_, e_ * 16:(e_ + 1) * 16] = 1.0
        _TRI = (tri, ident, sel.astype(BF))
    return _TRI


def kernel(**inputs):
    return kernel_fused(inputs)


# version 19
# speedup vs baseline: 1.0141x; 1.0141x over previous
import contextlib
import numpy as np
import ml_dtypes
import concourse.bass as bass
import concourse.mybir as mybir
from concourse.bass_utils import run_bass_kernel_spmd

F32 = mybir.dt.float32
BF16 = mybir.dt.bfloat16
AF = mybir.ActivationFunctionType
ALU = mybir.AluOpType
AX = mybir.AxisListType
BF = ml_dtypes.bfloat16

B, S, D = 2, 4096, 4096
NCORES = 8
NLOC = 1024
CH = 512
NHQ, NKV, HD = 32, 8, 128
KVD = NKV * HD
LN_EPS = 1e-5


def core_chunks(c):
    cc = c % 4
    return (cc, 7 - cc)


def chunk_owner(j):
    return (j, 0) if j < 4 else (7 - j, 1)


def core_positions(c):
    lo, hi = core_chunks(c)
    return np.concatenate([np.arange(lo * CH, (lo + 1) * CH),
                           np.arange(hi * CH, (hi + 1) * CH)])


def b_gather_specs():
    specs = []
    for j in range(8):
        specs.append(("k", j, 0))
        for kt in range(4):
            specs.append(("v", j, kt))
    for j2pair in ((0, 1), (2, 3)):
        for j in range(8):
            for j2 in j2pair:
                specs.append(("q", j, j2))
    return specs


def c_gather_specs():
    return [(g, half, a) for g in range(8) for half in range(2) for a in range(4)]


class Sems:
    def __init__(self, nc, stack):
        self.nc = nc
        self.stack = stack
        self.h = {}
        self.n = {}

    def __getitem__(self, name):
        if name not in self.h:
            self.h[name] = self.stack.enter_context(self.nc.semaphore(name))
            self.n[name] = 0
        return self.h[name]

    def inc(self, inst, name, k=1):
        inst.then_inc(self[name], k)
        self.n[name] += k

    def val(self, name):
        self[name]
        return self.n[name]

    def wait(self, e, name, v):
        e.wait_ge(self[name], v)


class TagSems:

    def __init__(self, base, prefix):
        self.base = base
        self.prefix = prefix

    def __getitem__(self, name):
        return self.base[self.prefix + name]

    def inc(self, inst, name, k=1):
        self.base.inc(inst, self.prefix + name, k)

    def val(self, name):
        return self.base.val(self.prefix + name)


_GP_SEM_TOKENS = ("ldc", "w_ld", "b_ld", "c_ot", "c_w_", "d_wc", "d_wbr",
                  "h_wc", "h_w_")


def _is_gp_sem(name):
    if "ldci" in name:
        return False
    return any(t in name for t in _GP_SEM_TOKENS)


class PoolSems:

    def __init__(self, pools, countss):
        self.pools = pools
        self.countss = countss
        self.map = {}
        self.base = {}
        self.n = {}
        self.nexts = [0, 0]

    def _h(self, name):
        w, s = self.map[name]
        return self.pools[w][s]

    def __getitem__(self, name):
        if name not in self.map:
            w = 0 if _is_gp_sem(name) else 1
            s = self.nexts[w]
            self.map[name] = (w, s)
            self.nexts[w] += 1
            self.base[name] = self.countss[w][s]
            self.n[name] = 0
        return self._h(name)

    def inc(self, inst, name, k=1):
        h = self[name]
        inst.then_inc(h, k)
        self.n[name] += k
        w, s = self.map[name]
        self.countss[w][s] += k

    def val(self, name):
        self[name]
        return self.n[name]

    def wait(self, e, name, v):
        e.wait_ge(self[name], self.base[name] + v)


def emit_gate(nc, blk, smbase, delta):
    for engname in ("sync", "vector", "scalar", "tensor", "gpsimd"):
        dec = getattr(blk, engname)

        def _body(e, d=tuple(delta.items())):
            for k, v in d:
                e.wait_ge(smbase[k], v)
        dec(_body)



def emit_launch_A(nc, blk, stack, sm, x_in, wq, wk, wv, cost, sint, ident,
                  qe_out, kt_out, v_out, tag="a", stage=2, gp_hook=None):
    sb = lambda name, shape, dt: stack.enter_context(nc.sbuf_tensor(tag + name, shape, dt))
    ps = lambda name, shape, dt: stack.enter_context(nc.psum_tensor(tag + name, shape, dt))

    xt = sb("xt", [128, 2, 4096], F32)
    xn_t = sb("xn_t", [128, 2, 4096], BF16)
    sq = sb("sq", [128, 4096], F32)
    xnT = sb("xnT", [128, 32, 1024], BF16)
    sums = sb("sums", [128, 8], F32)
    redt = sb("redt", [128, 1], F32)
    msqt = sb("msqt", [128, 1], F32)
    sumsq = sb("sumsq", [128, 8], F32)
    mvec = sb("mvec", [128, 8], F32)
    varv = sb("varv", [128, 8], F32)
    rstd = sb("rstd", [128, 8], F32)
    epsb = sb("epsb", [128, 1], F32)
    wbuf = sb("wbuf", [128, 2, 32, 256], BF16)
    qraw = sb("qraw", [128, 2, 2, 1024], BF16)
    qe_s = sb("qe_s", [128, 2, 2, 1024], BF16)
    rtmp = sb("rtmp", [128, 2, 1024], BF16)
    vstage = sb("vstage", [128, 8, 8, 129], BF16)
    cosb = sb("cosb", [128, 2, 1024], BF16)
    sinb = sb("sinb", [128, 2, 1024], BF16)
    idb = sb("idb", [128, 128], BF16)

    pst = [ps("pst0", [128, 1024], BF16), ps("pst1", [128, 1024], BF16)]
    pp = [ps("pp0", [128, 512], F32), ps("pp1", [128, 512], F32)]

    x_r = x_in.rearrange("(t p) d -> p t d", p=128)
    NHC = 24

    @blk.gpsimd
    def _(e):
        for half in (slice(0, 64), slice(64, 128)):
            sm.inc(e.dma_start(out=cosb[half, 0, :], in_=cost[:, :]), "ldc", 16)
            sm.inc(e.dma_start(out=cosb[half, 1, :], in_=cost[:, :]), "ldc", 16)
            sm.inc(e.dma_start(out=sinb[half, 0, :], in_=sint[:, :]), "ldc", 16)
            sm.inc(e.dma_start(out=sinb[half, 1, :], in_=sint[:, :]), "ldc", 16)

    @blk.sync
    def _(e):
        sm.inc(e.dma_start(out=idb[:, :], in_=ident[:, :]), "ldci", 16)
        for i in range(8):
            if i >= 2:
                sm.wait(e, "x1done", i - 1)
                sm.wait(e, "ssq", i - 1)
                sm.wait(e, "xn_done", i - 1)
            inst = e.dma_start(out=xt[:, i % 2, 0:2048], in_=x_r[:, i, 0:2048])
            sm.inc(inst, f"ldx1_{i % 2}", 16)

    @blk.vector
    def _(e):
        sm.inc(e.memset(epsb[:, :], LN_EPS), "veps", 1)
        sm.inc(e.memset(vstage[:, :, :, 128:129], 1.0), "vinit", 1)

        def cast(i):
            if i >= 2:
                sm.wait(e, "tr_mm", 32 * (i - 1))
            sm.wait(e, "rstd", i + 1)
            sm.wait(e, f"ldxs_{i % 2}", 16 * (i // 2 + 1))
            inst = e.tensor_scalar(xn_t[:, i % 2, :], xt[:, i % 2, :],
                                   mvec[:, i:i + 1], rstd[:, i:i + 1],
                                   ALU.subtract, ALU.mult)
            sm.inc(inst, "xn_done", 1)

        for i in range(8):
            sm.wait(e, f"ldx1_{i % 2}", 16 * (i // 2 + 1))
            sm.wait(e, f"ldxs_{i % 2}", 16 * (i // 2 + 1))
            if i >= 1:
                sm.wait(e, "vcp", 3 * i)
            inst = e.reduce_sum(redt[:, :], xt[:, i % 2, :], AX.X)
            sm.inc(inst, "x1done", 1)
            sm.wait(e, "x1done", i + 1)
            sm.inc(e.tensor_copy(sums[:, i:i + 1], redt[:, :]), "vcp", 1)
            sm.wait(e, "vcp", 3 * i + 1)
            sm.inc(e.tensor_scalar_mul(mvec[:, i:i + 1], sums[:, i:i + 1],
                                       1.0 / 4096.0), "vcp", 1)
            sm.wait(e, "vcp", 3 * i + 2)
            sm.inc(e.tensor_mul(msqt[:, :], mvec[:, i:i + 1], mvec[:, i:i + 1]),
                   "vcp", 1)
            if i >= 1:
                cast(i - 1)
            sm.wait(e, "vcp", 3 * i + 3)
            sm.wait(e, "ssq", i + 1)
            inst = e.scalar_tensor_tensor(varv[:, i:i + 1], sumsq[:, i:i + 1],
                                          1.0 / 4096.0, msqt[:, :],
                                          ALU.mult, ALU.subtract)
            sm.inc(inst, "vvar", 1)
        cast(7)

    @blk.scalar
    def _(e):
        def evac(g_idx):
            i_, grp = divmod(g_idx, 4)
            pbank = pst[g_idx % 2]
            sm.wait(e, "tr_mm", 8 * (g_idx + 1))
            inst = e.copy(
                xnT[:, grp * 8:(grp + 1) * 8, i_ * 128:(i_ + 1) * 128],
                pbank[:, :].rearrange("p (a b) -> p a b", b=128))
            sm.inc(inst, "tr_ev", 1)

        for i in range(2):
            inst = e.dma_start(out=xt[:, i % 2, 2048:4096],
                               in_=x_r[:, i, 2048:4096])
            sm.inc(inst, f"ldxs_{i % 2}", 16)
        sm.wait(e, "veps", 1)
        for i in range(8):
            sm.wait(e, f"ldx1_{i % 2}", 16 * (i // 2 + 1))
            sm.wait(e, f"ldxs_{i % 2}", 16 * (i // 2 + 1))
            if i >= 1:
                sm.wait(e, "ssq", i)
            inst = e.activation(sq[:, :], xt[:, i % 2, :], AF.Square,
                                accum_out=sumsq[:, i:i + 1])
            sm.inc(inst, "ssq", 1)
            sm.wait(e, "vvar", i + 1)
            sm.inc(e.activation(varv[:, i:i + 1], varv[:, i:i + 1], AF.Ln,
                                bias=epsb[:, 0:1]), "sln", 1)
            sm.wait(e, "sln", i + 1)
            inst = e.activation(rstd[:, i:i + 1], varv[:, i:i + 1],
                                AF.Exp, scale=-0.5)
            sm.inc(inst, "rstd", 1)
            if i >= 2:
                for g in range(4):
                    evac(4 * (i - 2) + g)
            if i + 2 < 8:
                sm.wait(e, "x1done", i + 1)
                sm.wait(e, "xn_done", i + 1)
                sm.wait(e, "ssq", i + 1)
                inst = e.dma_start(out=xt[:, i % 2, 2048:4096],
                                   in_=x_r[:, i + 2, 2048:4096])
                sm.inc(inst, f"ldxs_{i % 2}", 16)
        for g_idx in range(24, 32):
            evac(g_idx)

    if stage < 1:
        @blk.gpsimd
        def _(e):
            sm.wait(e, "rstd", 8)
            sm.wait(e, "vvar", 8)
            sm.wait(e, "vcp", 24)
            sm.wait(e, "ssq", 8)
            qe_r = qe_out.rearrange("(hh p) t -> p hh t", p=128)
            dbg = [sums, sumsq, mvec, varv, rstd]
            if stage >= 0.5:
                pass
            for j, tsr in enumerate(dbg):
                inst = e.dma_start(out=qe_r[:, j, 0:8], in_=tsr[:, :])
                sm.inc(inst, "qe_dma", 16)
            sm.wait(e, "qe_dma", sm.val("qe_dma"))
        return

    @blk.tensor
    def _(e):
        sm.wait(e, "ldci", 16)
        for i in range(8):
            for grp in range(4):
                g_idx = i * 4 + grp
                pbank = pst[g_idx % 2]
                sm.wait(e, "xn_done", i + 1)
                if g_idx >= 2:
                    sm.wait(e, "tr_ev", g_idx - 1)
                for k in range(8):
                    dt = grp * 8 + k
                    inst = e.transpose(pbank[:, k * 128:(k + 1) * 128],
                                       xn_t[:, i % 2, dt * 128:(dt + 1) * 128], idb[:, :])
                    sm.inc(inst, "tr_mm", 1)

    if stage < 2:
        @blk.sync
        def _(e):
            sm.wait(e, "tr_ev", 32)
            qe_r = qe_out.rearrange("(hh p) t -> p hh t", p=128)
            inst = e.dma_start(out=qe_r[:, 0:32, :], in_=xnT[:, :, :])
            sm.inc(inst, "qe_dma", 16)
            sm.wait(e, "qe_dma", sm.val("qe_dma"))
        return

    HCS = [("k", wk, i) for i in range(4)] + [("v", wv, i) for i in range(4)] \
        + [("q", wq, i) for i in list(range(0, 16, 2)) + list(range(1, 16, 2))]
    groups = []
    hc_g0 = []
    rope_of_hc = {}
    rn = 0
    for n, (kind, w, i) in enumerate(HCS):
        hc_g0.append(len(groups))
        if kind == "v":
            for tt in range(8):
                groups.append((n, "v", tt))
        else:
            rope_of_hc[n] = rn
            rn += 1
            for ot in range(2):
                for tc in range(2):
                    groups.append((n, kind, (ot, tc)))
    hc_g0.append(len(groups))
    ev_after = []
    ec_ = 0
    for (n, kind, pl) in groups:
        ec_ += 2 if kind == "v" else 1
        ev_after.append(ec_)
    rope_hcs = [n for n, (kind, w, i) in enumerate(HCS) if kind != "v"]

    @blk.gpsimd
    def _(e):
        for n, (kind, w, i) in enumerate(HCS):
            if n >= 2:
                sm.wait(e, "p_mm", hc_g0[n - 1])
            inst = e.dma_start(out=wbuf[:, n % 2, :, :], in_=w[i, :, :, :])
            sm.inc(inst, f"w_ld_{n % 2}", 16)
            if gp_hook is not None:
                gp_hook(e, n)

    @blk.tensor
    def _(e):
        sm.wait(e, "tr_ev", 32)
        for m, (n, kind, pl) in enumerate(groups):
            if m == hc_g0[n]:
                sm.wait(e, f"w_ld_{n % 2}", 16 * (n // 2 + 1))
            pb = pp[m % 2]
            if m >= 2:
                sm.wait(e, "p_ev", ev_after[m - 2])
            if kind == "v":
                tt = pl
                for dm in range(32):
                    inst = e.matmul(pb[:, 0:256],
                                    xnT[:, dm, tt * 128:(tt + 1) * 128],
                                    wbuf[:, n % 2, dm, :],
                                    start=(dm == 0), stop=(dm == 31))
            else:
                ot, tc = pl
                for dm in range(32):
                    inst = e.matmul(pb[:, :],
                                    wbuf[:, n % 2, dm, ot * 128:(ot + 1) * 128],
                                    xnT[:, dm, tc * 512:(tc + 1) * 512],
                                    start=(dm == 0), stop=(dm == 31))
            sm.inc(inst, "p_mm", 1)

    @blk.scalar
    def _(e):
        for m, (n, kind, pl) in enumerate(groups):
            pb = pp[m % 2]
            sm.wait(e, "p_mm", m + 1)
            if kind == "v":
                tt = pl
                vc = HCS[n][2]
                for gg in range(2):
                    g = vc * 2 + gg
                    inst = e.copy(vstage[:, tt, g, 0:128],
                                  pb[:, gg * 128:(gg + 1) * 128])
                    sm.inc(inst, "p_ev", 1)
            else:
                ot, tc = pl
                r = rope_of_hc[n]
                if pl == (0, 0) and r >= 2:
                    sm.wait(e, "rope_done", r - 1)
                inst = e.copy(qraw[:, r % 2, ot, tc * 512:(tc + 1) * 512],
                              pb[:, :])
                sm.inc(inst, "p_ev", 1)

    @blk.vector
    def _(e):
        sm.wait(e, "ldc", 128)
        for r, n in enumerate(rope_hcs):
            sm.wait(e, "p_ev", ev_after[hc_g0[n + 1] - 1])
            if r >= 2:
                sm.wait(e, f"qe_dma_{r % 2}", 16 * (r // 2))
            qr = qraw[0:64, r % 2, :, :]
            qi = qraw[64:128, r % 2, :, :]
            qeE = qe_s[0:64, r % 2, :, :]
            qeO = qe_s[64:128, r % 2, :, :]
            t_lo = rtmp[0:64, :, :]
            t_hi = rtmp[64:128, :, :]
            if r >= 1:
                sm.wait(e, "vr3", r)
                sm.wait(e, "rope_done", r)
            sm.inc(e.tensor_mul(t_lo, qi, sinb[64:128, :, :]), "vr1", 1)
            sm.inc(e.tensor_mul(qeE, qr, cosb[0:64, :, :]), "vr2", 1)
            sm.wait(e, "vr1", r + 1)
            sm.wait(e, "vr2", r + 1)
            sm.inc(e.tensor_sub(qeE, qeE, t_lo), "vr3", 1)
            sm.inc(e.tensor_mul(t_hi, qr, sinb[0:64, :, :]), "vr4", 1)
            sm.inc(e.tensor_mul(qeO, qi, cosb[64:128, :, :]), "vr5", 1)
            sm.wait(e, "vr4", r + 1)
            sm.wait(e, "vr5", r + 1)
            inst = e.tensor_add(qeO, qeO, t_hi)
            sm.inc(inst, "rope_done", 1)

    @blk.sync
    def _(e):
        qe_r = qe_out.rearrange("(hh p) t -> p hh t", p=128)
        kt_r = kt_out.rearrange("(hh p) t -> p hh t", p=128)
        sm.wait(e, "p_ev", ev_after[hc_g0[8] - 1])
        sm.wait(e, "vinit", 1)
        v_r = v_out.rearrange("(t p) c -> p t c", p=128)
        inst = e.dma_start(out=v_r[:, :, :],
                           in_=vstage[:, :, :, :].rearrange("p t g c -> p t (g c)"))
        sm.inc(inst, "v_dma", 16)
        for r, n in enumerate(rope_hcs):
            kind, w, i = HCS[n]
            sm.wait(e, "rope_done", r + 1)
            sm.wait(e, "vr3", r + 1)
            if kind == "q":
                hh = (i % 2) * 16 + (i // 2) * 2
                dst = qe_r[:, hh:hh + 2, :]
            else:
                dst = kt_r[:, i * 2:(i + 1) * 2, :]
            inst = e.dma_start(out=dst, in_=qe_s[:, r % 2, :, :])
            sm.inc(inst, f"qe_dma_{r % 2}", 16)
        sm.wait(e, "qe_dma_0", sm.val("qe_dma_0"))
        sm.wait(e, "qe_dma_1", sm.val("qe_dma_1"))
        sm.wait(e, "v_dma", 16)


def rope_tables():
    f = np.arange(0, HD, 2, dtype=np.float64) / HD
    inv = 1.0 / (10000.0 ** f)
    t = np.arange(S, dtype=np.float64)
    ang = np.outer(inv, t)
    return (np.cos(ang).astype(np.float32), np.sin(ang).astype(np.float32))


def perm_interleave():
    return np.concatenate([np.arange(0, HD, 2), np.arange(1, HD, 2)])



def emit_B_shared(nc, blk, stack, sm, trimask, ident, tag="bs"):
    sb = lambda name, shape, dt: stack.enter_context(nc.sbuf_tensor(tag + name, shape, dt))
    ps = lambda name, shape, dt: stack.enter_context(nc.psum_tensor(tag + name, shape, dt))
    sh = {}
    sh["msk"] = sb("msk", [128, 128], BF16)
    sh["idb"] = sb("idb", [128, 128], BF16)
    sh["zeros"] = sb("zeros", [128, 128], BF16)
    sh["p_sb"] = sb("p_sb", [128, 4, 2, 512], BF16)
    sh["rl_t"] = sb("rl_t", [128, 1], F32)
    sh["rl_t8"] = sb("rl_t8", [128, 8], F32)
    sh["rl"] = sb("rl", [128, 8, 1], F32)
    sh["o_sb"] = sb("o_sb", [128, 8, 128], BF16)
    sh["psT"] = [ps("psT0", [128, 1024], F32), ps("psT1", [128, 1024], F32)]
    sh["pOl"] = ps("pOl", [128, 1536], F32)
    sh["ptr"] = ps("ptr", [128, 1024], BF16)

    @blk.sync
    def _(e):
        sm.inc(e.dma_start(out=sh["msk"][:, :], in_=trimask[:, :]), "sh_ld", 16)
        sm.inc(e.dma_start(out=sh["idb"][:, :], in_=ident[:, :]), "sh_ld", 16)

    @blk.vector
    def _(e):
        sm.inc(e.memset(sh["zeros"][:, :], 0.0), "sh_ld", 1)
    return sh


def emit_launch_B(nc, blk, stack, sm, sh, q_in, k_in, v_in, ot_out, tag, prev=None, fused=None):
    sb = lambda name, shape, dt: stack.enter_context(nc.sbuf_tensor(tag + name, shape, dt))
    qsb = sb("qsb", [128, 4, 4096], BF16)
    ksb = sb("ksb", [128, 4096], BF16)
    vsb = sb("vsb", [128, 32, 129], BF16)
    ot_sb = sb("ot_sb", [128, 4, 4096], BF16)
    msk, idb, zeros = sh["msk"], sh["idb"], sh["zeros"]
    p_sb, rl_t, rl, o_sb = sh["p_sb"], sh["rl_t"], sh["rl"], sh["o_sb"]
    rl_t8 = sh["rl_t8"]
    psT, pOl, ptr = sh["psT"], sh["pOl"], sh["ptr"]

    def ol_ap(a):
        off = (a // 3) * 512 + (a % 3) * 129
        return pOl[:, off:off + 129]

    SCHED = []
    for hp in range(2):
        for qc in range(8):
            group = [(kc, kt, kc == qc, (kt * 128 if kc == qc else 0))
                     for kc in range(qc + 1) for kt in range(4)]
            SCHED.append((qc, hp, group))

    exp_after, msk_after, pv_after, mm_after = [], [], [], []
    ec = mc = pc = mmc = 0
    for gi, (qc, hp, group) in enumerate(SCHED):
        pc += 3
        for (kc, kt, diag, qoff) in group:
            mmc += 2
            ec += 2 if diag else 1
            mc += 2 if diag else 0
            pc += sum(1 for hh in range(2) for qt in range((kt if diag else 0), 4))
            exp_after.append(ec)
            msk_after.append(mc)
            pv_after.append(pc)
            mm_after.append(mmc)
    NGRP = len(SCHED)
    TOT = dict(b_exp=ec, b_msk=mc, b_pv=pc, b_mm=mmc, b_oev=3 * NGRP,
               b_rc=4 * NGRP, b_te=NGRP, b_tr=8 * NGRP)

    def entry_waits(e):
        if prev is not None:
            for s, v in TOT.items():
                sm.wait(e, prev + s, v)

    @blk.sync
    def _(e):
        if fused is None:
            sm.inc(e.dma_start(out=qsb[:, :, :],
                               in_=q_in.rearrange("(j p) t -> p j t", p=128)),
                   tag + "b_ld", 16)
            sm.inc(e.dma_start(out=ksb[:, :], in_=k_in[:, :]), tag + "b_ld", 16)
            sm.inc(e.dma_start(out=vsb[:, :, :],
                               in_=v_in.rearrange("(a p) c -> p a c", p=128)),
                   tag + "b_ld", 16)
        else:
            pass

    if fused is not None:
        @blk.gpsimd
        def _(e):
            import bass_rust as _br
            ktA, vA, u = fused["ktA"], fused["vA"], fused["u"]
            ixs = fused["ixs"]
            cc_hdl, cc_v0, cc_v1 = fused["cc"]
            e.wait_ge(fused["ix_hdl"], fused["ix_val"])
            fl = lambda t: t.rearrange("r t -> (r t)").rearrange("(r o) -> r o", o=1)
            qflat = [fl(fused["qeA0"]), fl(fused["qeA1"])]
            kflat = fl(ktA)
            vflat = fl(vA)
            e.wait_ge(cc_hdl, cc_v0)
            waited1 = [False]
            for n, (kind, j, j2) in enumerate(b_gather_specs()):
                if kind == "q":
                    if j2 >= 2 and not waited1[0]:
                        e.wait_ge(cc_hdl, cc_v1)
                        waited1[0] = True
                    out = qsb[:, j2, j * 512:(j + 1) * 512]
                    flat = qflat[j2 // 2]
                elif kind == "k":
                    out = ksb[:, j * 512:(j + 1) * 512]
                    flat = kflat
                else:
                    out = vsb[:, j * 4 + j2, :]
                    flat = vflat
                inst = e.indirect_dma_start(
                    out, None, flat, _br.IndirectOffsetOnAxis(ixs[:, u, n:n + 1], 0))
                if kind == "q":
                    sm.inc(inst, tag + f"b_ldq{j2 // 2}", 16)
                else:
                    sm.inc(inst, tag + "b_ldkv", 16)

    @blk.vector
    def _(e):
        entry_waits(e)
        s_i = 0
        for g_i, (qc, hp, group) in enumerate(SCHED):
            for (kc, kt, diag, qoff) in group:
                par = s_i % 4
                if diag:
                    sm.wait(e, tag + "b_exp", exp_after[s_i])
                    for hh in range(2):
                        inst = e.tensor_mul(p_sb[:, par, hh, kt * 128:(kt + 1) * 128],
                                            p_sb[:, par, hh, kt * 128:(kt + 1) * 128],
                                            msk[:, :])
                        sm.inc(inst, tag + "b_msk", 1)
                s_i += 1
            sm.wait(e, tag + "b_pv", pv_after[s_i - 1])
            def bank_view(bk, cnt):
                return pOl[:, bk * 512:bk * 512 + cnt * 129] \
                    .rearrange("p (r s) -> p r s", s=129)
            rc = 4 * g_i
            for bk, cnt, a0 in ((0, 3, 0), (1, 3, 3), (2, 2, 6)):
                inst = e.tensor_copy(rl[:, a0:a0 + cnt, :],
                                     bank_view(bk, cnt)[:, :, 128:129])
                sm.inc(inst, tag + "b_rc", 1)
                rc += 1
            sm.wait(e, tag + "b_rc", rc)
            inst = e.reciprocal(rl_t8[:, :], rl[:, :, 0])
            sm.inc(inst, tag + "b_rc", 1)
            sm.wait(e, tag + "b_rc", rc + 1)
            if g_i >= 1:
                sm.wait(e, tag + "b_tr", 8 * g_i)
            for bk, cnt, a0 in ((0, 3, 0), (1, 3, 3), (2, 2, 6)):
                inst = e.tensor_mul(
                    o_sb[:, a0:a0 + cnt, :],
                    bank_view(bk, cnt)[:, :, 0:128],
                    rl_t8[:, a0:a0 + cnt].rearrange("p (r o) -> p r o", o=1)
                        .broadcast_to((128, cnt, 128)))
                sm.inc(inst, tag + "b_oev", 1)

    @blk.tensor
    def _(e):
        sm.wait(e, "sh_ld", 33)
        if fused is None:
            sm.wait(e, tag + "b_ld", 48)
        entry_waits(e)
        s_i = 0
        for g_i, (qc, hp, group) in enumerate(SCHED):
            if fused is not None and qc == 0:
                if hp == 0:
                    sm.wait(e, tag + "b_ldkv", 16 * 40)
                sm.wait(e, tag + f"b_ldq{hp}", 16 * 16)
            if g_i >= 1:
                sm.wait(e, tag + "b_oev", 3 * g_i)
                sm.wait(e, tag + "b_rc", 4 * g_i)
            for bank in range(3):
                inst = e.matmul(pOl[:, bank * 512:(bank + 1) * 512],
                                zeros[:, :], ksb[:, 0:512], start=True, stop=False)
                sm.inc(inst, tag + "b_pv", 1)
            last_touch = {}
            for li, (kc, kt, diag, qoff) in enumerate(group):
                for hh in range(2):
                    for qt in range((kt if diag else 0), 4):
                        last_touch[(hh * 4 + qt) // 3] = (li, hh, qt)
            base_s = s_i

            def emit_sT(li2):
                s2 = base_s + li2
                (kc2, kt2, diag2, qoff2) = group[li2]
                if s2 >= 2:
                    sm.wait(e, tag + "b_exp", exp_after[s2 - 2])
                for hh in range(2):
                    inst = e.matmul(
                        psT[s2 % 2][:, hh * 512 + qoff2:(hh + 1) * 512],
                        ksb[:, (kc2 * 4 + kt2) * 128:(kc2 * 4 + kt2 + 1) * 128],
                        qsb[:, hp * 2 + hh, qc * 512 + qoff2:(qc + 1) * 512],
                        start=True, stop=True)
                    sm.inc(inst, tag + "b_mm", 1)

            emit_sT(0)
            for li, (kc, kt, diag, qoff) in enumerate(group):
                par = s_i % 4
                if li + 1 < len(group):
                    emit_sT(li + 1)
                sm.wait(e, tag + "b_exp", exp_after[s_i])
                if diag:
                    sm.wait(e, tag + "b_msk", msk_after[s_i])
                for hh in range(2):
                    for qt in range((kt if diag else 0), 4):
                        a = hh * 4 + qt
                        is_last = last_touch[a // 3] == (li, hh, qt)
                        inst = e.matmul(ol_ap(a),
                                        p_sb[:, par, hh, qt * 128:(qt + 1) * 128],
                                        vsb[:, kc * 4 + kt, :],
                                        start=False, stop=is_last)
                        sm.inc(inst, tag + "b_pv", 1)
                s_i += 1
            sm.wait(e, tag + "b_oev", 3 * (g_i + 1))
            if g_i >= 1:
                sm.wait(e, tag + "b_te", g_i)
            for a in range(8):
                inst = e.transpose(ptr[:, a * 128:(a + 1) * 128], o_sb[:, a, :],
                                   idb[:, :])
                sm.inc(inst, tag + "b_tr", 1)

    @blk.scalar
    def _(e):
        entry_waits(e)
        s_i = 0
        for g_i, (qc, hp, group) in enumerate(SCHED):
            for (kc, kt, diag, qoff) in group:
                par = s_i % 4
                pb = psT[s_i % 2]
                sm.wait(e, tag + "b_mm", mm_after[s_i])
                if s_i >= 4:
                    sm.wait(e, tag + "b_pv", pv_after[s_i - 4])
                if not diag:
                    inst = e.activation(p_sb[:, par, :, :], pb[:, :], AF.Exp,
                                        scale=1.0 / np.sqrt(128.0))
                    sm.inc(inst, tag + "b_exp", 1)
                else:
                    for hh in range(2):
                        inst = e.activation(p_sb[:, par, hh, qoff:512],
                                            pb[:, hh * 512 + qoff:(hh + 1) * 512],
                                            AF.Exp, scale=1.0 / np.sqrt(128.0))
                        sm.inc(inst, tag + "b_exp", 1)
                s_i += 1
            sm.wait(e, tag + "b_tr", 8 * (g_i + 1))
            inst = e.copy(ot_sb[:, hp * 2:hp * 2 + 2, qc * 512:(qc + 1) * 512],
                          ptr[:, :].rearrange("p (j q) -> p j q", q=512))
            sm.inc(inst, tag + "b_te", 1)

    @blk.sync
    def _(e):
        sm.wait(e, tag + "b_te", NGRP)
        inst = e.dma_start(out=ot_out.rearrange("(j p) t -> p j t", p=128),
                           in_=ot_sb[:, :, :])
        sm.inc(inst, tag + "b_odma", 16)
        sm.wait(e, tag + "b_odma", 16)


def build_nc_B():
    nc = bass.Bass("TRN2", target_bir_lowering=False)
    ins = {}
    for u in range(2):
        ins[f"q_in{u}"] = nc.dram_tensor(f"q_in{u}", [512, 4096], BF16, kind="ExternalInput").ap()
        ins[f"k_in{u}"] = nc.dram_tensor(f"k_in{u}", [128, 4096], BF16, kind="ExternalInput").ap()
        ins[f"v_in{u}"] = nc.dram_tensor(f"v_in{u}", [4096, 129], BF16, kind="ExternalInput").ap()
    trimask = nc.dram_tensor("trimask", [128, 128], BF16, kind="ExternalInput").ap()
    ident = nc.dram_tensor("ident", [128, 128], BF16, kind="ExternalInput").ap()
    outs = {}
    for u in range(2):
        outs[f"ot_out{u}"] = nc.dram_tensor(f"ot_out{u}", [512, 4096], BF16, kind="ExternalOutput").ap()
    with contextlib.ExitStack() as stack:
        sm = Sems(nc, stack)
        blk = stack.enter_context(nc.Block())
        sh = emit_B_shared(nc, blk, stack, sm, trimask, ident)
        emit_launch_B(nc, blk, stack, sm, sh, ins["q_in0"], ins["k_in0"],
                      ins["v_in0"], outs["ot_out0"], tag="u0")
        emit_launch_B(nc, blk, stack, sm, sh, ins["q_in1"], ins["k_in1"],
                      ins["v_in1"], outs["ot_out1"], tag="u1", prev="u0")
    return nc



def emit_launch_C(nc, blk, stack, sm, ot_in, wo, x_in, xo_out, tag="c", fused_ot=None):
    sb = lambda name, shape, dt: stack.enter_context(nc.sbuf_tensor(tag + name, shape, dt))
    ps = lambda name, shape, dt: stack.enter_context(nc.psum_tensor(tag + name, shape, dt))

    otb = sb("otb", [128, 32, 1024], BF16)
    wbuf = sb("wbuf", [128, 2, 32, 512], BF16)
    xpc = sb("xpc", [128, 2, 512], F32)
    xo = sb("xo", [128, 2, 512], F32)
    pp = [ps("pp0", [128, 512], F32), ps("pp1", [128, 512], F32)]

    x_r = x_in.rearrange("(t p) d -> p t d", p=128)
    xo_r = xo_out.rearrange("(t p) d -> p t d", p=128)

    @blk.sync
    def _(e):
        if fused_ot is None:
            sm.inc(e.dma_start(out=otb[:, :, :],
                               in_=ot_in.rearrange("(a p) t -> p a t", p=128)),
                   "c_ot", 16)

        def store(j):
            sm.wait(e, "c_ev", j + 1)
            inst = e.dma_start(out=xo_r[:, j % 8, (j // 8) * 512:(j // 8 + 1) * 512],
                               in_=xo[:, j % 2, :])
            sm.inc(inst, f"c_o_{j % 2}", 16)
        for k in range(64):
            ec, tt = divmod(k, 8)
            if k >= 2:
                sm.wait(e, "c_ev", k - 1)
            inst = e.dma_start(out=xpc[:, k % 2, :],
                               in_=x_r[:, tt, ec * 512:(ec + 1) * 512])
            sm.inc(inst, f"c_x_{k % 2}", 16)
            if k >= 2:
                store(k - 2)
        store(62)
        store(63)

    if fused_ot is not None:
        @blk.gpsimd
        def _(e):
            import bass_rust as _br
            otA_aps, ixs, ix_hdl, ix_val = fused_ot
            e.wait_ge(ix_hdl, ix_val)
            oflats = [a_.rearrange("r t -> (r t)").rearrange("(r o) -> r o", o=1)
                      for a_ in otA_aps]
            for n, (g, half, a) in enumerate(c_gather_specs()):
                inst = e.indirect_dma_start(
                    otb[:, g * 4 + a, half * 512:(half + 1) * 512], None,
                    oflats[g % 2],
                    _br.IndirectOffsetOnAxis(ixs[:, n:n + 1], 0))
                sm.inc(inst, "c_ot", 16)

    @blk.gpsimd
    def _(e):
        for ec in range(8):
            if ec >= 2:
                sm.wait(e, "c_mm", 8 * (ec - 1))
            inst = e.dma_start(out=wbuf[:, ec % 2, :, :], in_=wo[ec, :, :, :])
            sm.inc(inst, f"c_w_{ec % 2}", 16)

    @blk.tensor
    def _(e):
        sm.wait(e, "c_ot", 16 if fused_ot is None else 16 * 64)
        for k in range(64):
            ec, tt = divmod(k, 8)
            if tt == 0:
                sm.wait(e, f"c_w_{ec % 2}", 16 * (ec // 2 + 1))
            pb = pp[k % 2]
            if k >= 2:
                sm.wait(e, "c_ev", k - 1)
            for dm in range(32):
                inst = e.matmul(pb[:, :], otb[:, dm, tt * 128:(tt + 1) * 128],
                                wbuf[:, ec % 2, dm, :],
                                start=(dm == 0), stop=(dm == 31))
            sm.inc(inst, "c_mm", 1)

    @blk.vector
    def _(e):
        for k in range(64):
            sm.wait(e, "c_mm", k + 1)
            sm.wait(e, f"c_x_{k % 2}", 16 * (k // 2 + 1))
            if k >= 2:
                sm.wait(e, f"c_o_{k % 2}", 16 * (k // 2))
            inst = e.tensor_add(xo[:, k % 2, :], pp[k % 2][:, :], xpc[:, k % 2, :])
            sm.inc(inst, "c_ev", 1)

    @blk.sync
    def _(e):
        sm.wait(e, "c_o_0", 16 * 32)
        sm.wait(e, "c_o_1", 16 * 32)


def build_nc_C():
    nc = bass.Bass("TRN2", target_bir_lowering=False)
    ot_in = nc.dram_tensor("ot_in", [4096, 1024], BF16, kind="ExternalInput").ap()
    wo = nc.dram_tensor("wo", [4096, 4096], F32, kind="ExternalInput").ap()
    x_in = nc.dram_tensor("x_in", [1024, 4096], F32, kind="ExternalInput").ap()
    xo_out = nc.dram_tensor("xo_out", [1024, 4096], F32, kind="ExternalOutput").ap()
    with contextlib.ExitStack() as stack:
        sm = Sems(nc, stack)
        blk = stack.enter_context(nc.Block())
        emit_launch_C(nc, blk, stack, sm, ot_in, wo, x_in, xo_out)
    return nc
RMS_EPS = 1.1920929e-07


def emit_launch_D(nc, blk, stack, sm, x_in, wcat, b24, gate_in, wbr, ident,
                  xf_out, tag="d"):
    sb = lambda name, shape, dt: stack.enter_context(nc.sbuf_tensor(tag + name, shape, dt))
    ps = lambda name, shape, dt: stack.enter_context(nc.psum_tensor(tag + name, shape, dt))

    xt = sb("xt", [128, 2, 4096], F32)
    scratch = sb("scratch", [128, 2, 4096], F32)
    xn_tok = sb("xn_tok", [128, 8, 4096], BF16)
    sums = sb("sums", [128, 8], F32)
    redt = sb("redt", [128, 1], F32)
    msqt = sb("msqt", [128, 1], F32)
    sumsq = sb("sumsq", [128, 8], F32)
    mvec = sb("mvec", [128, 8], F32)
    varv = sb("varv", [128, 8], F32)
    rstd = sb("rstd", [128, 8], F32)
    rsum = sb("rsum", [128, 8], F32)
    r2 = sb("r2", [128, 8], F32)
    epsb = sb("epsb", [128, 1], F32)
    eps2b = sb("eps2b", [128, 1], F32)
    xnT_g = sb("xnT_g", [128, 2, 8, 128], BF16)
    wcatb = sb("wcatb", [128, 32, 24], BF16)
    b24b = sb("b24b", [128, 24], F32)
    gateb = sb("gateb", [128, 1], F32)
    zer5 = sb("zer5", [128, 512], BF16)
    lgf = sb("lgf", [128, 1024], BF16)
    Lt = sb("Lt", [128, 8, 24], F32)
    LL = sb("LL", [128, 8, 24], F32)
    EZ = sb("EZ", [128, 8, 16], F32)
    EZT = sb("EZT", [128, 8, 16], F32)
    e44 = sb("e44", [128, 8, 16], F32)
    s4u = sb("s4u", [128, 8, 4], F32)
    s4v = sb("s4v", [128, 8, 4], F32)
    s4t = sb("s4t", [128, 8, 4], F32)
    s4t2 = sb("s4t2", [128, 8, 4], F32)
    ru = sb("ru", [128, 8, 4], F32)
    cv = sb("cv", [128, 8, 4], F32)
    H = sb("H", [128, 8, 16], F32)
    Hpre = sb("Hpre", [128, 8, 4], F32)
    Hpost = sb("Hpost", [128, 8, 4], F32)
    sred = sb("sred", [128, 8], F32)
    srec = sb("srec", [128, 8], F32)
    g_s = sb("g_s", [128, 1], F32)
    g_t = sb("g_t", [128, 1], F32)
    omg = sb("omg", [128, 1], F32)
    xp = sb("xp", [128, 8, 1024], BF16)
    xpT = sb("xpT", [128, 8, 1024], BF16)
    wbrb = sb("wbrb", [128, 8, 1024], BF16)
    y_f = sb("y_f", [128, 8, 1024], BF16)
    y_tok = xp
    idb = sb("idb", [128, 128], BF16)

    ptr = [ps("ptr0", [128, 1024], BF16), ps("ptr1", [128, 1024], BF16)]
    pl = ps("pl", [24, 1024], F32)
    py = [ps("py0", [128, 512], F32), ps("py1", [128, 512], F32)]

    x_r = x_in.rearrange("(t p) d -> p t d", p=128)
    xf_r = xf_out.rearrange("(t p) d -> p t d", p=128)

    def v44(t):
        return t[:, :, :].rearrange("p s (i j) -> p s i j", j=4)

    def b_2(ap3):
        return ap3.rearrange("p s (o j) -> p s o j", o=1).broadcast_to((128, 8, 4, 4))

    def b_3(ap3):
        return ap3.rearrange("p s (i o) -> p s i o", o=1).broadcast_to((128, 8, 4, 4))

    @blk.gpsimd
    def _(e):
        sm.inc(e.dma_start(out=wcatb[:, :, :], in_=wcat[:, :, :]), "d_wc", 16)
        sm.inc(e.dma_start(out=b24b[:, :], in_=b24[:, :]), "d_wc", 16)
        sm.inc(e.dma_start(out=gateb[:, :], in_=gate_in[:, :]), "d_wc", 16)
        sm.inc(e.dma_start(out=idb[:, :], in_=ident[:, :]), "d_wc", 16)
        sm.inc(e.dma_start(out=wbrb[:, :, :], in_=wbr[:, :, :]), "d_wbr", 16)

    @blk.sync
    def _(e):
        for i in range(8):
            if i >= 2:
                sm.wait(e, "d_red", i - 1)
                sm.wait(e, "d_ssq", i - 1)
                sm.wait(e, "d_xn", i - 1)
            sm.inc(e.dma_start(out=xt[:, i % 2, 0:2048], in_=x_r[:, i, 0:2048]),
                   f"d_x1_{i % 2}", 16)

    @blk.vector
    def _(e):
        sm.inc(e.memset(epsb[:, :], LN_EPS), "d_eps", 1)
        sm.inc(e.memset(eps2b[:, :], RMS_EPS), "d_eps", 1)
        sm.inc(e.memset(lgf[:, :], 0.0), "d_eps", 1)
        sm.inc(e.memset(zer5[:, :], 0.0), "d_eps", 1)
        sm.inc(e.memset(cv[:, :, :], 1.0), "d_eps", 1)

        def cast(i):
            sm.wait(e, "d_rstd", i + 1)
            sm.wait(e, f"d_xs_{i % 2}", 16 * (i // 2 + 1))
            inst = e.tensor_scalar(xn_tok[:, i, :], xt[:, i % 2, :],
                                   mvec[:, i:i + 1], rstd[:, i:i + 1],
                                   ALU.subtract, ALU.mult)
            sm.inc(inst, "d_xn", 1)

        for i in range(8):
            sm.wait(e, f"d_x1_{i % 2}", 16 * (i // 2 + 1))
            sm.wait(e, f"d_xs_{i % 2}", 16 * (i // 2 + 1))
            if i >= 1:
                sm.wait(e, "d_cp", 3 * i)
            inst = e.reduce_sum(redt[:, :], xt[:, i % 2, :], AX.X)
            sm.inc(inst, "d_red", 1)
            sm.wait(e, "d_red", i + 1)
            sm.inc(e.tensor_copy(sums[:, i:i + 1], redt[:, :]), "d_cp", 1)
            sm.wait(e, "d_cp", 3 * i + 1)
            sm.inc(e.tensor_scalar_mul(mvec[:, i:i + 1], sums[:, i:i + 1],
                                       1.0 / 4096.0), "d_cp", 1)
            sm.wait(e, "d_cp", 3 * i + 2)
            sm.inc(e.tensor_mul(msqt[:, :], mvec[:, i:i + 1], mvec[:, i:i + 1]),
                   "d_cp", 1)
            if i >= 1:
                cast(i - 1)
            sm.wait(e, "d_cp", 3 * i + 3)
            sm.wait(e, "d_ssq", i + 1)
            inst = e.scalar_tensor_tensor(varv[:, i:i + 1], sumsq[:, i:i + 1],
                                          1.0 / 4096.0, msqt[:, :],
                                          ALU.mult, ALU.subtract)
            sm.inc(inst, "d_var", 1)
        cast(7)

    @blk.scalar
    def _(e):
        def evac(g_idx):
            par = g_idx % 2
            sm.wait(e, "d_tr", 8 * (g_idx + 1))
            if g_idx >= 2:
                sm.wait(e, "d_lg", 2 + 8 * (g_idx - 1))
            inst = e.copy(xnT_g[:, par, :, :],
                          ptr[par][:, :].rearrange("p (a b) -> p a b", b=128))
            sm.inc(inst, "d_tev", 1)

        for i in range(2):
            inst = e.dma_start(out=xt[:, i % 2, 2048:4096],
                               in_=x_r[:, i, 2048:4096])
            sm.inc(inst, f"d_xs_{i % 2}", 16)
        sm.wait(e, "d_eps", 5)
        for i in range(8):
            sm.wait(e, f"d_x1_{i % 2}", 16 * (i // 2 + 1))
            sm.wait(e, f"d_xs_{i % 2}", 16 * (i // 2 + 1))
            if i >= 1:
                sm.wait(e, "d_ssq", i)
            inst = e.activation(scratch[:, 0, :], xt[:, i % 2, :], AF.Square,
                                accum_out=sumsq[:, i:i + 1])
            sm.inc(inst, "d_ssq", 1)
            sm.wait(e, "d_var", i + 1)
            sm.inc(e.activation(varv[:, i:i + 1], varv[:, i:i + 1], AF.Ln,
                                bias=epsb[:, 0:1]), "d_ln", 1)
            sm.wait(e, "d_ln", i + 1)
            sm.inc(e.activation(rstd[:, i:i + 1], varv[:, i:i + 1],
                                AF.Exp, scale=-0.5), "d_rstd", 1)
            if i >= 2:
                for g in range(4):
                    evac(4 * (i - 2) + g)
            if i + 2 < 8:
                sm.wait(e, "d_red", i + 1)
                sm.wait(e, "d_xn", i + 1)
                sm.wait(e, "d_ssq", i + 1)
                inst = e.dma_start(out=xt[:, i % 2, 2048:4096],
                                   in_=x_r[:, i + 2, 2048:4096])
                sm.inc(inst, f"d_xs_{i % 2}", 16)
        for g_idx in range(24, 32):
            evac(g_idx)

    @blk.scalar
    def _(e):
        for i in range(8):
            sm.wait(e, "d_xn", i + 1)
            if i == 0:
                sm.wait(e, "d_ssq", 8)
            else:
                sm.wait(e, "d_rsq", i)
            inst = e.activation(scratch[:, 0, :], xn_tok[:, i, :], AF.Square,
                                accum_out=rsum[:, i:i + 1])
            sm.inc(inst, "d_rsq", 1)
        sm.wait(e, "d_rsq", 8)
        sm.wait(e, "d_eps", 5)
        sm.inc(e.activation(rsum[:, :], rsum[:, :], AF.Ln, scale=1.0 / 4096.0,
                            bias=eps2b[:, 0:1]), "d_rln", 1)
        sm.wait(e, "d_rln", 1)
        sm.inc(e.activation(r2[:, :], rsum[:, :], AF.Exp, scale=-0.5), "d_r2", 1)

    @blk.tensor
    def _(e):
        sm.wait(e, "d_wc", 64)
        sm.wait(e, "d_eps", 5)
        for bank in range(2):
            inst = e.matmul(pl[:, bank * 512:(bank + 1) * 512], zer5[:, 0:24],
                            zer5[:, :], start=True, stop=False)
            sm.inc(inst, "d_lg", 1)
        g_idx = 0
        for i in range(8):
            for grp in range(4):
                par = g_idx % 2
                pbank = ptr[par]
                sm.wait(e, "d_xn", i + 1)
                if g_idx >= 2:
                    sm.wait(e, "d_tev", g_idx - 1)
                for k in range(8):
                    dt_ = grp * 8 + k
                    inst = e.transpose(pbank[:, k * 128:(k + 1) * 128],
                                       xn_tok[:, i, dt_ * 128:(dt_ + 1) * 128],
                                       idb[:, :])
                    sm.inc(inst, "d_tr", 1)
                sm.wait(e, "d_tev", g_idx + 1)
                for k in range(8):
                    dm = grp * 8 + k
                    stop = (grp == 3 and k == 7 and i in (3, 7))
                    inst = e.matmul(pl[:, i * 128:(i + 1) * 128], wcatb[:, dm, :],
                                    xnT_g[:, par, k, :], start=False, stop=stop)
                    sm.inc(inst, "d_lg", 1)
                g_idx += 1

    @blk.scalar
    def _(e):
        sm.wait(e, "d_lg", 2 + 256)
        inst = e.copy(lgf[0:24, :], pl[:, :])
        sm.inc(inst, "d_lgf", 1)

    @blk.tensor
    def _(e):
        sm.wait(e, "d_lgf", 1)
        sm.wait(e, "d_tev", 32)
        for i in range(8):
            inst = e.transpose(ptr[i % 2][:, (i // 2) * 128:(i // 2 + 1) * 128],
                               lgf[:, i * 128:(i + 1) * 128], idb[:, :])
            sm.inc(inst, "d_ltr", 1)

    @blk.scalar
    def _(e):
        for i in range(8):
            sm.wait(e, "d_ltr", i + 1)
            inst = e.copy(Lt[:, i, :], ptr[i % 2][:, (i // 2) * 128:(i // 2) * 128 + 24])
            sm.inc(inst, "d_lte", 1)
        sm.wait(e, "d_LL", 8)
        sm.inc(e.activation(EZ[:, :, :], LL[:, :, 0:16], AF.Exp, scale=10.0), "d_EZ", 1)
        sm.wait(e, "d_wc", 64)
        sm.inc(e.activation(g_t[:, :], gateb[:, :], AF.Exp, scale=-1.0), "d_gate", 1)
        sm.inc(e.activation(Hpre[:, :, :], LL[:, :, 16:20], AF.Exp), "d_sfe", 1)
        sm.inc(e.activation(Hpost[:, :, :], LL[:, :, 20:24], AF.Exp), "d_sfe", 1)

    @blk.vector
    def _(e):
        sm.wait(e, "d_lte", 8)
        sm.wait(e, "d_r2", 1)
        sm.wait(e, "d_wc", 64)
        for i in range(8):
            inst = e.scalar_tensor_tensor(LL[:, i, :], Lt[:, i, :], r2[:, i:i + 1],
                                          b24b[:, :], ALU.mult, ALU.add)
            sm.inc(inst, "d_LL", 1)
        sm.wait(e, "d_EZ", 1)
        inst = e.tensor_copy(v44(EZT),
                             v44(EZ).rearrange("p s i j -> p s j i"))
        sm.inc(inst, "d_sk", 1)
        sm.wait(e, "d_sk", 1)
        vc = 1

        def sum2(dst):
            nonlocal vc
            ev = v44(e44)
            inst = e.tensor_add(s4t[:, :, :], ev[:, :, 0, :], ev[:, :, 1, :])
            sm.inc(inst, "d_sk", 1); vc += 1
            inst = e.tensor_add(s4t2[:, :, :], ev[:, :, 2, :], ev[:, :, 3, :])
            sm.inc(inst, "d_sk", 1); vc += 1
            sm.wait(e, "d_sk", vc)
            inst = e.tensor_add(dst[:, :, :], s4t[:, :, :], s4t2[:, :, :])
            sm.inc(inst, "d_sk", 1); vc += 1
            sm.wait(e, "d_sk", vc)

        for it in range(50):
            inst = e.tensor_mul(v44(e44), v44(EZT), b_3(cv))
            sm.inc(inst, "d_sk", 1); vc += 1
            sm.wait(e, "d_sk", vc)
            sum2(s4u)
            inst = e.reciprocal(ru[:, :, :], s4u[:, :, :])
            sm.inc(inst, "d_sk", 1); vc += 1
            sm.wait(e, "d_sk", vc)
            inst = e.tensor_mul(v44(e44), v44(EZ), b_3(ru))
            sm.inc(inst, "d_sk", 1); vc += 1
            sm.wait(e, "d_sk", vc)
            sum2(s4v)
            inst = e.reciprocal(cv[:, :, :], s4v[:, :, :])
            sm.inc(inst, "d_sk", 1); vc += 1
            sm.wait(e, "d_sk", vc)
        inst = e.tensor_mul(v44(H), v44(EZ), b_3(ru))
        sm.inc(inst, "d_sk", 1); vc += 1
        sm.wait(e, "d_sk", vc)
        inst = e.tensor_mul(v44(H), v44(H), b_2(cv))
        sm.inc(inst, "d_sk", 1); vc += 1
        sm.wait(e, "d_sk", vc)
        sm.wait(e, "d_gate", 1)
        sm.inc(e.tensor_scalar_add(g_t[:, :], g_t[:, :], 1.0), "d_g", 1)
        sm.wait(e, "d_g", 1)
        sm.inc(e.reciprocal(g_s[:, :], g_t[:, :]), "d_g", 1)
        sm.wait(e, "d_g", 2)
        sm.inc(e.tensor_scalar(omg[:, :], g_s[:, :], -1.0, 1.0, ALU.mult, ALU.add),
               "d_g", 1)
        sm.wait(e, "d_g", 3)
        sm.inc(e.tensor_scalar_mul(H[:, :, :], H[:, :, :], g_s[:, 0:1]), "d_H", 1)
        sm.wait(e, "d_H", 1)
        for ii in range(4):
            inst = e.tensor_scalar_add(H[:, :, ii * 5:ii * 5 + 1],
                                       H[:, :, ii * 5:ii * 5 + 1], omg[:, 0:1])
            sm.inc(inst, "d_H", 1)
            sm.wait(e, "d_H", 2 + ii)
        for n_, dst in ((1, Hpre), (2, Hpost)):
            sm.wait(e, "d_sfe", n_)
            inst = e.reduce_sum(sred[:, :], dst[:, :, :], AX.X)
            sm.inc(inst, "d_sf", 1)
            sm.wait(e, "d_sf", sm.val("d_sf"))
            inst = e.reciprocal(srec[:, :], sred[:, :])
            sm.inc(inst, "d_sf", 1)
            sm.wait(e, "d_sf", sm.val("d_sf"))
            inst = e.tensor_mul(dst[:, :, :], dst[:, :, :],
                                srec[:, :].rearrange("p (s o) -> p s o", o=1)
                                          .broadcast_to((128, 8, 4)))
            sm.inc(inst, "d_sf", 1)
            sm.wait(e, "d_sf", sm.val("d_sf"))

    @blk.vector
    def _(e):
        sm.wait(e, "d_sf", 3)
        for i in range(8):
            for n in range(4):
                if n == 0:
                    inst = e.tensor_scalar_mul(xp[:, i, :],
                                               xn_tok[:, i, 0:1024],
                                               Hpre[:, i, 0:1])
                else:
                    inst = e.scalar_tensor_tensor(xp[:, i, :],
                                                  xn_tok[:, i, n * 1024:(n + 1) * 1024],
                                                  Hpre[:, i, n:n + 1],
                                                  xp[:, i, :], ALU.mult, ALU.add)
                sm.inc(inst, "d_xp", 1)
                sm.wait(e, "d_xp", sm.val("d_xp"))

    @blk.tensor
    def _(e):
        sm.wait(e, "d_lte", 8)
        for i in range(8):
            par = i % 2
            sm.wait(e, "d_xp", 4 * (i + 1))
            if i >= 2:
                sm.wait(e, "d_xpte", i - 1)
            for ct in range(8):
                inst = e.transpose(ptr[par][:, ct * 128:(ct + 1) * 128],
                                   xp[:, i, ct * 128:(ct + 1) * 128], idb[:, :])
                sm.inc(inst, "d_xpt", 1)

    @blk.scalar
    def _(e):
        for i in range(8):
            par = i % 2
            sm.wait(e, "d_xpt", 8 * (i + 1))
            inst = e.copy(xpT[:, :, i * 128:(i + 1) * 128],
                          ptr[par][:, :].rearrange("p (a b) -> p a b", b=128))
            sm.inc(inst, "d_xpte", 1)

    @blk.tensor
    def _(e):
        sm.wait(e, "d_wbr", 16)
        sm.wait(e, "d_xpte", 8)
        k = 0
        for ct in range(8):
            for tc in range(2):
                pb = py[k % 2]
                if k >= 2:
                    sm.wait(e, "d_yev", k - 1)
                for cdm in range(8):
                    inst = e.matmul(pb[:, :], wbrb[:, cdm, ct * 128:(ct + 1) * 128],
                                    xpT[:, cdm, tc * 512:(tc + 1) * 512],
                                    start=(cdm == 0), stop=(cdm == 7))
                sm.inc(inst, "d_ymm", 1)
                k += 1

    @blk.scalar
    def _(e):
        for k in range(16):
            ct, tc = divmod(k, 2)
            sm.wait(e, "d_ymm", k + 1)
            inst = e.copy(y_f[:, ct, tc * 512:(tc + 1) * 512], py[k % 2][:, :])
            sm.inc(inst, "d_yev", 1)

    @blk.tensor
    def _(e):
        sm.wait(e, "d_yev", 16)
        sm.wait(e, "d_xpte", 8)
        for i in range(8):
            par = i % 2
            if i >= 2:
                sm.wait(e, "d_yte", i - 1)
            for ct in range(8):
                inst = e.transpose(ptr[par][:, ct * 128:(ct + 1) * 128],
                                   y_f[:, ct, i * 128:(i + 1) * 128], idb[:, :])
                sm.inc(inst, "d_ytr", 1)

    @blk.scalar
    def _(e):
        sm.wait(e, "d_xpt", 64)
        for i in range(8):
            par = i % 2
            sm.wait(e, "d_ytr", 8 * (i + 1))
            inst = e.copy(y_tok[:, i, :],
                          ptr[par][:, :].rearrange("p (a b) -> p a b", b=128))
            sm.inc(inst, "d_yte", 1)

    @blk.sync
    def _(e):
        def store(j):
            sm.wait(e, "d_mx", j + 1)
            inst = e.dma_start(out=xf_r[:, j, :], in_=scratch[:, j % 2, :])
            sm.inc(inst, f"d_out_{j % 2}", 16)
        for i in range(8):
            if i < 2:
                sm.wait(e, "d_xn", 8)
            else:
                sm.wait(e, "d_mx", i - 1)
            sm.inc(e.dma_start(out=xt[:, i % 2, :], in_=x_r[:, i, :]), f"d_x3_{i % 2}", 16)
            if i >= 2:
                store(i - 2)
        store(6)
        store(7)
        sm.wait(e, "d_out_0", 64)
        sm.wait(e, "d_out_1", 64)

    @blk.vector
    def _(e):
        sm.wait(e, "d_H", 5)
        sm.wait(e, "d_sf", 6)
        sm.wait(e, "d_yte", 8)
        sm.wait(e, "d_rsq", 8)
        for i in range(8):
            par = i % 2
            acc = scratch[:, par, :]
            sm.wait(e, f"d_x3_{par}", 16 * (i // 2 + 1))
            if i >= 2:
                sm.wait(e, f"d_out_{par}", 16 * (i // 2))
            for ii in range(4):
                a_sl = acc.rearrange("p (n c) -> p n c", c=1024)[:, ii, :]
                for j in range(4):
                    if j == 0:
                        inst = e.tensor_scalar_mul(a_sl, xn_tok[:, i, 0:1024],
                                                   H[:, i, ii * 4:ii * 4 + 1])
                    else:
                        inst = e.scalar_tensor_tensor(
                            a_sl, xn_tok[:, i, j * 1024:(j + 1) * 1024],
                            H[:, i, ii * 4 + j:ii * 4 + j + 1],
                            a_sl, ALU.mult, ALU.add)
                    sm.inc(inst, "d_mv", 1)
                    sm.wait(e, "d_mv", sm.val("d_mv"))
                inst = e.scalar_tensor_tensor(a_sl, y_tok[:, i, :],
                                              Hpost[:, i, ii:ii + 1],
                                              a_sl, ALU.mult, ALU.add)
                sm.inc(inst, "d_mv", 1)
                sm.wait(e, "d_mv", sm.val("d_mv"))
            inst = e.tensor_add(acc, acc, xt[:, par, :])
            sm.inc(inst, "d_mx", 1)
            sm.wait(e, "d_mx", sm.val("d_mx"))


def build_nc_D():
    nc = bass.Bass("TRN2", target_bir_lowering=False)
    x_in = nc.dram_tensor("x_in", [NLOC, D], F32, kind="ExternalInput").ap()
    wcat = nc.dram_tensor("wcat", [128, 32, 24], F32, kind="ExternalInput").ap()
    b24 = nc.dram_tensor("b24", [128, 24], F32, kind="ExternalInput").ap()
    gate_in = nc.dram_tensor("gate_in", [128, 1], F32, kind="ExternalInput").ap()
    wbr = nc.dram_tensor("wbr", [128, 8, 1024], F32, kind="ExternalInput").ap()
    ident = nc.dram_tensor("ident", [128, 128], BF16, kind="ExternalInput").ap()
    xf_out = nc.dram_tensor("xf_out", [NLOC, D], F32, kind="ExternalOutput").ap()
    with contextlib.ExitStack() as stack:
        sm = Sems(nc, stack)
        blk = stack.enter_context(nc.Block())
        emit_launch_D(nc, blk, stack, sm, x_in, wcat, b24, gate_in, wbr, ident, xf_out)
    return nc


def emit_launch_H(nc, blk, stack, sm, x_in, wbase, bbase, wgt, bthr, acat, bcat,
                  selmat, ident, xf_out, tag="h"):
    sb = lambda name, shape, dt: stack.enter_context(nc.sbuf_tensor(tag + name, shape, dt))
    ps = lambda name, shape, dt: stack.enter_context(nc.psum_tensor(tag + name, shape, dt))

    xt = sb("xt", [128, 2, 4096], F32)
    sq = sb("sq", [128, 4096], F32)
    xn_t = sb("xn_t", [128, 2, 4096], BF16)
    xnT = sb("xnT", [128, 32, 1024], BF16)
    sums = sb("sums", [128, 8], F32)
    redt = sb("redt", [128, 1], F32)
    msqt = sb("msqt", [128, 1], F32)
    sumsq = sb("sumsq", [128, 8], F32)
    mvec = sb("mvec", [128, 8], F32)
    varv = sb("varv", [128, 8], F32)
    rstd = sb("rstd", [128, 8], F32)
    epsb = sb("epsb", [128, 1], F32)
    wgtb = sb("wgtb", [128, 32, 9], BF16)
    bthrn = sb("bthrn", [128, 1], F32)
    bthrb = sb("bthrb", [128, 1], F32)
    acatb = sb("acatb", [128, 32, 128], BF16)
    bcatb = sb("bcatb", [128, 4096], BF16)
    selb = sb("selb", [8, 128], BF16)
    onesc = sb("onesc", [1, 128], BF16)
    bbaseb = sb("bbaseb", [1, 4096], BF16)
    lgf = sb("lgf", [128, 1024], BF16)
    G = sb("G", [128, 8, 9], F32)
    thr = sb("thr", [128, 8], F32)
    sred = sb("sred", [128, 8], F32)
    srec = sb("srec", [128, 8], F32)
    gsm = sb("gsm", [128, 8, 8], F32)
    wts = sb("wts", [8, 1024], BF16)
    wtile = sb("wtile", [128, 128], BF16)
    w_exp = sb("w_exp", [128, 1024], BF16)
    hw = sb("hw", [128, 1024], BF16)
    wb = sb("wb", [128, 2, 32, 256], BF16)
    xpc = sb("xpc", [128, 2, 256], F32)
    xo = sb("xo", [128, 2, 256], F32)
    idb = sb("idb", [128, 128], BF16)

    ptr = [ps("ptr0", [128, 1024], BF16), ps("ptr1", [128, 1024], BF16)]
    plg = ps("plg", [9, 1024], F32)
    pb = [ps("pb0", [128, 512], F32), ps("pb1", [128, 512], F32)]

    x_r = x_in.rearrange("(t p) d -> p t d", p=128)
    xf_r = xf_out.rearrange("(t p) d -> p t d", p=128)

    @blk.gpsimd
    def _(e):
        sm.inc(e.dma_start(out=wgtb[:, :, :], in_=wgt[:, :, :]), "h_wc", 16)
        sm.inc(e.dma_start(out=bthrb[:, :], in_=bthr[:, :]), "h_wc", 16)
        sm.inc(e.dma_start(out=acatb[:, :, :], in_=acat[:, :, :]), "h_wc", 16)
        sm.inc(e.dma_start(out=bcatb[:, :], in_=bcat[:, :]), "h_wc", 16)
        sm.inc(e.dma_start(out=selb[:, :], in_=selmat[:, :]), "h_wc", 16)
        sm.inc(e.dma_start(out=bbaseb[:, :], in_=bbase[:, :]), "h_wc", 16)
        sm.inc(e.dma_start(out=idb[:, :], in_=ident[:, :]), "h_wc", 16)
        for oc in range(16):
            if oc >= 2:
                sm.wait(e, "h_bmm", 8 * (oc - 1))
            inst = e.dma_start(out=wb[:, oc % 2, :, :], in_=wbase[oc, :, :, :])
            sm.inc(inst, f"h_w_{oc % 2}", 16)

    @blk.sync
    def _(e):
        for i in range(8):
            if i >= 2:
                sm.wait(e, "h_red", i - 1)
                sm.wait(e, "h_ssq", i - 1)
                sm.wait(e, "h_xn", i - 1)
            sm.inc(e.dma_start(out=xt[:, i % 2, 0:2048], in_=x_r[:, i, 0:2048]),
                   f"h_x1_{i % 2}", 16)

    @blk.vector
    def _(e):
        sm.inc(e.memset(epsb[:, :], LN_EPS), "h_eps", 1)
        sm.inc(e.memset(lgf[:, :], 0.0), "h_eps", 1)
        sm.inc(e.memset(wtile[:, :], 0.0), "h_eps", 1)
        sm.inc(e.memset(onesc[:, :], 1.0), "h_eps", 1)

        def cast(i):
            if i >= 2:
                sm.wait(e, "h_tr", 32 * (i - 1))
            sm.wait(e, "h_rstd", i + 1)
            sm.wait(e, f"h_xs_{i % 2}", 16 * (i // 2 + 1))
            inst = e.tensor_scalar(xn_t[:, i % 2, :], xt[:, i % 2, :],
                                   mvec[:, i:i + 1], rstd[:, i:i + 1],
                                   ALU.subtract, ALU.mult)
            sm.inc(inst, "h_xn", 1)

        for i in range(8):
            sm.wait(e, f"h_x1_{i % 2}", 16 * (i // 2 + 1))
            sm.wait(e, f"h_xs_{i % 2}", 16 * (i // 2 + 1))
            if i >= 1:
                sm.wait(e, "h_cp", 3 * i)
            inst = e.reduce_sum(redt[:, :], xt[:, i % 2, :], AX.X)
            sm.inc(inst, "h_red", 1)
            sm.wait(e, "h_red", i + 1)
            sm.inc(e.tensor_copy(sums[:, i:i + 1], redt[:, :]), "h_cp", 1)
            sm.wait(e, "h_cp", 3 * i + 1)
            sm.inc(e.tensor_scalar_mul(mvec[:, i:i + 1], sums[:, i:i + 1],
                                       1.0 / 4096.0), "h_cp", 1)
            sm.wait(e, "h_cp", 3 * i + 2)
            sm.inc(e.tensor_mul(msqt[:, :], mvec[:, i:i + 1], mvec[:, i:i + 1]),
                   "h_cp", 1)
            if i >= 1:
                cast(i - 1)
            sm.wait(e, "h_cp", 3 * i + 3)
            sm.wait(e, "h_ssq", i + 1)
            inst = e.scalar_tensor_tensor(varv[:, i:i + 1], sumsq[:, i:i + 1],
                                          1.0 / 4096.0, msqt[:, :],
                                          ALU.mult, ALU.subtract)
            sm.inc(inst, "h_var", 1)
        cast(7)

    @blk.scalar
    def _(e):
        def evac(g_idx):
            i_, grp = divmod(g_idx, 4)
            par = g_idx % 2
            sm.wait(e, "h_tr", 8 * (g_idx + 1))
            inst = e.copy(xnT[:, grp * 8:(grp + 1) * 8, i_ * 128:(i_ + 1) * 128],
                          ptr[par][:, :].rearrange("p (a b) -> p a b", b=128))
            sm.inc(inst, "h_tev", 1)

        for i in range(2):
            inst = e.dma_start(out=xt[:, i % 2, 2048:4096],
                               in_=x_r[:, i, 2048:4096])
            sm.inc(inst, f"h_xs_{i % 2}", 16)
        sm.wait(e, "h_eps", 4)
        for i in range(8):
            sm.wait(e, f"h_x1_{i % 2}", 16 * (i // 2 + 1))
            sm.wait(e, f"h_xs_{i % 2}", 16 * (i // 2 + 1))
            if i >= 1:
                sm.wait(e, "h_ssq", i)
            inst = e.activation(sq[:, :], xt[:, i % 2, :], AF.Square,
                                accum_out=sumsq[:, i:i + 1])
            sm.inc(inst, "h_ssq", 1)
            sm.wait(e, "h_var", i + 1)
            sm.inc(e.activation(varv[:, i:i + 1], varv[:, i:i + 1], AF.Ln,
                                bias=epsb[:, 0:1]), "h_ln", 1)
            sm.wait(e, "h_ln", i + 1)
            sm.inc(e.activation(rstd[:, i:i + 1], varv[:, i:i + 1],
                                AF.Exp, scale=-0.5), "h_rstd", 1)
            if i >= 2:
                for g in range(4):
                    evac(4 * (i - 2) + g)
            if i + 2 < 8:
                sm.wait(e, "h_red", i + 1)
                sm.wait(e, "h_xn", i + 1)
                sm.wait(e, "h_ssq", i + 1)
                inst = e.dma_start(out=xt[:, i % 2, 2048:4096],
                                   in_=x_r[:, i + 2, 2048:4096])
                sm.inc(inst, f"h_xs_{i % 2}", 16)
        for g_idx in range(24, 32):
            evac(g_idx)

    @blk.tensor
    def _(e):
        sm.wait(e, "h_wc", 112)
        sm.wait(e, "h_eps", 4)
        for bank in range(2):
            inst = e.matmul(plg[:, bank * 512:(bank + 1) * 512], lgf[:, 0:9],
                            lgf[:, 0:512], start=True, stop=False)
            sm.inc(inst, "h_lg", 1)
        g_idx = 0
        for i in range(8):
            for grp in range(4):
                par = g_idx % 2
                pbank = ptr[par]
                sm.wait(e, "h_xn", i + 1)
                if g_idx >= 2:
                    sm.wait(e, "h_tev", g_idx - 1)
                for k in range(8):
                    dt_ = grp * 8 + k
                    inst = e.transpose(pbank[:, k * 128:(k + 1) * 128],
                                       xn_t[:, i % 2, dt_ * 128:(dt_ + 1) * 128],
                                       idb[:, :])
                    sm.inc(inst, "h_tr", 1)
                g_idx += 1
        sm.wait(e, "h_tev", 32)
        for i in range(8):
            for dm in range(32):
                stop = (dm == 31 and i in (3, 7))
                inst = e.matmul(plg[:, i * 128:(i + 1) * 128], wgtb[:, dm, :],
                                xnT[:, dm, i * 128:(i + 1) * 128],
                                start=False, stop=stop)
                sm.inc(inst, "h_lg", 1)

    @blk.scalar
    def _(e):
        sm.wait(e, "h_lg", 2 + 256)
        inst = e.copy(lgf[0:9, :], plg[:, :])
        sm.inc(inst, "h_lgf", 1)

    @blk.tensor
    def _(e):
        sm.wait(e, "h_lgf", 1)
        sm.wait(e, "h_tev", 32)
        for i in range(8):
            inst = e.transpose(ptr[i % 2][:, (i // 2) * 128:(i // 2 + 1) * 128],
                               lgf[:, i * 128:(i + 1) * 128], idb[:, :])
            sm.inc(inst, "h_ltr", 1)

    @blk.scalar
    def _(e):
        for i in range(8):
            sm.wait(e, "h_ltr", i + 1)
            inst = e.copy(G[:, i, :], ptr[i % 2][:, (i // 2) * 128:(i // 2) * 128 + 9])
            sm.inc(inst, "h_lte", 1)
        sm.wait(e, "h_lte", 8)
        sm.wait(e, "h_negb", 1)
        inst = e.activation(thr[:, :], G[:, :, 8:9].rearrange("p s o -> p (s o)"),
                            AF.Exp, scale=-1.0, bias=bthrn[:, 0:1])
        sm.inc(inst, "h_sg", 1)
        inst = e.activation(gsm[:, :, :], G[:, :, 0:8], AF.Exp)
        sm.inc(inst, "h_ge", 1)

    @blk.vector
    def _(e):
        sm.wait(e, "h_wc", 112)
        sm.inc(e.tensor_scalar_mul(bthrn[:, :], bthrb[:, :], -1.0), "h_negb", 1)
        sm.wait(e, "h_sg", 1)
        sm.inc(e.tensor_scalar_add(thr[:, :], thr[:, :], 1.0), "h_th", 1)
        sm.wait(e, "h_th", 1)
        sm.inc(e.reciprocal(sred[:, :], thr[:, :]), "h_th", 1)
        sm.wait(e, "h_th", 2)
        sm.inc(e.tensor_scalar_mul(thr[:, :], sred[:, :], 0.125), "h_th", 1)
        sm.wait(e, "h_ge", 1)
        sm.wait(e, "h_th", 3)
        sm.inc(e.reduce_sum(sred[:, :], gsm[:, :, :], AX.X), "h_gs", 1)
        sm.wait(e, "h_gs", 1)
        sm.inc(e.reciprocal(srec[:, :], sred[:, :]), "h_gs", 1)
        sm.wait(e, "h_gs", 2)
        inst = e.tensor_mul(gsm[:, :, :], gsm[:, :, :],
                            srec[:, :].rearrange("p (s o) -> p s o", o=1)
                                      .broadcast_to((128, 8, 8)))
        sm.inc(inst, "h_gs", 1)
        sm.wait(e, "h_gs", 3)
        sm.wait(e, "h_th", 3)
        inst = e.tensor_sub(gsm[:, :, :], gsm[:, :, :],
                            thr[:, :].rearrange("p (s o) -> p s o", o=1)
                                     .broadcast_to((128, 8, 8)))
        sm.inc(inst, "h_gs", 1)
        sm.wait(e, "h_gs", 4)
        sm.inc(e.tensor_relu(gsm[:, :, :], gsm[:, :, :]), "h_gs", 1)
        sm.wait(e, "h_gs", 5)
        sm.inc(e.reduce_sum(sred[:, :], gsm[:, :, :], AX.X), "h_gs", 1)
        sm.wait(e, "h_gs", 6)
        sm.inc(e.tensor_single_scalar(srec[:, :], sred[:, :], 0.0, ALU.is_equal),
               "h_gs", 1)
        sm.wait(e, "h_gs", 7)
        sm.inc(e.tensor_add(sred[:, :], sred[:, :], srec[:, :]), "h_gs", 1)
        sm.wait(e, "h_gs", 8)
        sm.inc(e.reciprocal(srec[:, :], sred[:, :]), "h_gs", 1)
        sm.wait(e, "h_gs", 9)
        sm.inc(e.tensor_scalar_mul(srec[:, :], srec[:, :], 2.0), "h_gs", 1)
        sm.wait(e, "h_gs", 10)
        inst = e.tensor_mul(gsm[:, :, :], gsm[:, :, :],
                            srec[:, :].rearrange("p (s o) -> p s o", o=1)
                                      .broadcast_to((128, 8, 8)))
        sm.inc(inst, "h_gs", 1)

    @blk.vector
    def _(e):
        sm.wait(e, "h_gs", 11)
        for i in range(8):
            if i >= 1:
                sm.wait(e, "h_wtr", i)
            inst = e.tensor_copy(wtile[:, 0:8],
                                 gsm[:, i, :].rearrange("p (o c) -> p (o c)", o=1)
                                 if False else gsm[:, i, :])
            sm.inc(inst, "h_wt", 1)
            sm.wait(e, "h_wt", i + 1)

    @blk.tensor
    def _(e):
        sm.wait(e, "h_ltr", 8)
        sm.wait(e, "h_lte", 8)
        for i in range(8):
            sm.wait(e, "h_wt", i + 1)
            inst = e.transpose(ptr[i % 2][:, (i // 2) * 128:(i // 2 + 1) * 128],
                               wtile[:, :], idb[:, :])
            sm.inc(inst, "h_wtr", 1)

    @blk.scalar
    def _(e):
        for i in range(8):
            sm.wait(e, "h_wtr", i + 1)
            inst = e.copy(wts[:, i * 128:(i + 1) * 128],
                          ptr[i % 2][:, (i // 2) * 128:(i // 2) * 128 + 8]
                          .rearrange("p c -> p c") if False else
                          ptr[i % 2][0:8, (i // 2) * 128:(i // 2 + 1) * 128])
            sm.inc(inst, "h_wte", 1)

    @blk.tensor
    def _(e):
        sm.wait(e, "h_wte", 8)
        for tc in range(2):
            inst = e.matmul(pb[tc][:, :], selb[:, :], wts[:, tc * 512:(tc + 1) * 512],
                            start=True, stop=True)
            sm.inc(inst, "h_we", 1)
        sm.wait(e, "h_wee", 2)
        for tc in range(2):
            for dm in range(32):
                inst = e.matmul(pb[tc][:, :], acatb[:, dm, :],
                                xnT[:, dm, tc * 512:(tc + 1) * 512],
                                start=(dm == 0), stop=(dm == 31))
            sm.inc(inst, "h_hmm", 1)

    @blk.scalar
    def _(e):
        for tc in range(2):
            sm.wait(e, "h_we", tc + 1)
            inst = e.copy(w_exp[:, tc * 512:(tc + 1) * 512], pb[tc][:, :])
            sm.inc(inst, "h_wee", 1)

    @blk.vector
    def _(e):
        for tc in range(2):
            sm.wait(e, "h_hmm", tc + 1)
            inst = e.tensor_mul(hw[:, tc * 512:(tc + 1) * 512], pb[tc][:, :],
                                w_exp[:, tc * 512:(tc + 1) * 512])
            sm.inc(inst, "h_hw", 1)

    @blk.tensor
    def _(e):
        sm.wait(e, "h_hw", 2)
        sm.wait(e, "h_eps", 4)
        for k in range(128):
            oc, tt = divmod(k, 8)
            if tt == 0:
                sm.wait(e, f"h_w_{oc % 2}", 16 * (oc // 2 + 1))
            pbk = pb[k % 2]
            if k >= 2:
                sm.wait(e, "h_ev", k - 1)
            for dm in range(32):
                e.matmul(pbk[:, 0:256], xnT[:, dm, tt * 128:(tt + 1) * 128],
                         wb[:, oc % 2, dm, :], start=(dm == 0), stop=False)
            e.matmul(pbk[:, 0:256], onesc[:, :], bbaseb[:, oc * 256:(oc + 1) * 256],
                     start=False, stop=False)
            inst = e.matmul(pbk[:, 0:256], hw[:, tt * 128:(tt + 1) * 128],
                            bcatb[:, oc * 256:(oc + 1) * 256],
                            start=False, stop=True)
            sm.inc(inst, "h_bmm", 1)

    @blk.sync
    def _(e):
        def store(j):
            sm.wait(e, "h_ev", j + 1)
            inst = e.dma_start(out=xf_r[:, j % 8, (j // 8) * 256:(j // 8 + 1) * 256],
                               in_=xo[:, j % 2, :])
            sm.inc(inst, f"h_o_{j % 2}", 16)
        for k in range(128):
            oc, tt = divmod(k, 8)
            if k < 2:
                sm.wait(e, "h_xn", 8)
            else:
                sm.wait(e, "h_ev", k - 1)
            inst = e.dma_start(out=xpc[:, k % 2, :],
                               in_=x_r[:, tt, oc * 256:(oc + 1) * 256])
            sm.inc(inst, f"h_xp_{k % 2}", 16)
            if k >= 2:
                store(k - 2)
        store(126)
        store(127)
        sm.wait(e, "h_o_0", 16 * 64)
        sm.wait(e, "h_o_1", 16 * 64)

    @blk.vector
    def _(e):
        for k in range(128):
            sm.wait(e, "h_bmm", k + 1)
            sm.wait(e, f"h_xp_{k % 2}", 16 * (k // 2 + 1))
            if k >= 2:
                sm.wait(e, f"h_o_{k % 2}", 16 * (k // 2))
            inst = e.tensor_add(xo[:, k % 2, :], pb[k % 2][:, 0:256], xpc[:, k % 2, :])
            sm.inc(inst, "h_ev", 1)


def build_nc_H():
    nc = bass.Bass("TRN2", target_bir_lowering=False)
    x_in = nc.dram_tensor("x_in", [NLOC, D], F32, kind="ExternalInput").ap()
    wbase = nc.dram_tensor("wbase", [16, 128, 32, 256], F32, kind="ExternalInput").ap()
    bbase = nc.dram_tensor("bbase", [1, D], F32, kind="ExternalInput").ap()
    wgt = nc.dram_tensor("wgt", [128, 32, 9], F32, kind="ExternalInput").ap()
    bthr = nc.dram_tensor("bthr", [128, 1], F32, kind="ExternalInput").ap()
    acat = nc.dram_tensor("acat", [128, 32, 128], F32, kind="ExternalInput").ap()
    bcat = nc.dram_tensor("bcat", [128, D], F32, kind="ExternalInput").ap()
    selmat = nc.dram_tensor("selmat", [8, 128], BF16, kind="ExternalInput").ap()
    ident = nc.dram_tensor("ident", [128, 128], BF16, kind="ExternalInput").ap()
    xf_out = nc.dram_tensor("xf_out", [NLOC, D], F32, kind="ExternalOutput").ap()
    with contextlib.ExitStack() as stack:
        sm = Sems(nc, stack)
        blk = stack.enter_context(nc.Block())
        emit_launch_H(nc, blk, stack, sm, x_in, wbase, bbase, wgt, bthr, acat,
                      bcat, selmat, ident, xf_out)
    return nc



def build_nc_fused():
    nc = bass.Bass("TRN2", target_bir_lowering=False)
    ap_ = lambda name, shape, dt, kind: nc.dram_tensor(name, shape, dt, kind=kind).ap()
    EI, EO, IN = "ExternalInput", "ExternalOutput", "Internal"

    x_in = ap_("x_in", [NLOC, D], F32, EI)
    wq0 = ap_("wq0", [16, 128, 32, 256], F32, EI)
    wk0 = ap_("wk0", [4, 128, 32, 256], F32, EI)
    wv0 = ap_("wv0", [4, 128, 32, 256], F32, EI)
    wo0 = ap_("wo0", [8, 128, 32, 512], F32, EI)
    wq1 = ap_("wq1", [16, 128, 32, 256], F32, EI)
    wk1 = ap_("wk1", [4, 128, 32, 256], F32, EI)
    wv1 = ap_("wv1", [4, 128, 32, 256], F32, EI)
    wo1 = ap_("wo1", [8, 128, 32, 512], F32, EI)
    cost = ap_("cost", [64, NLOC], F32, EI)
    sint = ap_("sint", [64, NLOC], F32, EI)
    ident = ap_("ident", [128, 128], BF16, EI)
    trimask = ap_("trimask", [128, 128], BF16, EI)
    wcat = ap_("wcat", [128, 32, 24], F32, EI)
    b24 = ap_("b24", [128, 24], F32, EI)
    gate_in = ap_("gate_in", [128, 1], F32, EI)
    wbr = ap_("wbr", [128, 8, 1024], F32, EI)
    wbase = ap_("wbase", [16, 128, 32, 256], F32, EI)
    bbase = ap_("bbase", [1, D], F32, EI)
    wgt = ap_("wgt", [128, 32, 9], F32, EI)
    bthr = ap_("bthr", [128, 1], F32, EI)
    acat = ap_("acat", [128, 32, 128], F32, EI)
    bcat = ap_("bcat", [128, D], F32, EI)
    selmat = ap_("selmat", [8, 128], BF16, EI)
    bidx = ap_("bidx", [128, 2, 72], mybir.dt.uint32, EI)
    cidx = ap_("cidx", [128, 64], mybir.dt.uint32, EI)
    xf_out = ap_("xf_out", [NLOC, D], F32, EO)

    qe_d = [ap_(f"qe{l}_d", [D, NLOC], BF16, IN) for l in range(2)]
    kt_d = [ap_(f"kt{l}_d", [KVD, NLOC], BF16, IN) for l in range(2)]
    v_d = [ap_(f"v{l}_d", [NLOC, 8 * 129], BF16, IN) for l in range(2)]
    aps_ = lambda name, shape, dt: nc.dram_tensor(name, shape, dt, kind=IN,
                                                  addr_space="Shared").ap()
    qeA = [[aps_(f"qeA{l}h{h_}", [8 * 2048, NLOC], BF16) for h_ in range(2)]
           for l in range(2)]
    ktA = [aps_(f"ktA{l}", [8 * KVD, NLOC], BF16) for l in range(2)]
    vA = [aps_(f"vA{l}", [8 * NLOC, 8 * 129], BF16) for l in range(2)]
    ot_d = [ap_(f"ot{l}_d", [NLOC, D], BF16, IN) for l in range(2)]
    otA_u = [[aps_(f"otA{l}u{u}", [8 * 512, D], BF16) for u in range(2)]
             for l in range(2)]
    x1_d = ap_("x1_d", [NLOC, D], F32, IN)
    x2_d = ap_("x2_d", [NLOC, D], F32, IN)
    x3_d = ap_("x3_d", [NLOC, D], F32, IN)

    with contextlib.ExitStack() as gstack:
        smb = Sems(nc, gstack)
        gp_pool = [gstack.enter_context(nc.semaphore(f"gp{i}")) for i in range(8)]
        ot_pool = [gstack.enter_context(nc.semaphore(f"pl{i}")) for i in range(48)]
        pools = (gp_pool, ot_pool)
        countss = ([0] * 8, [0] * 48)
        blk = gstack.enter_context(nc.Block())
        bixs = gstack.enter_context(nc.sbuf_tensor("bixs", [128, 2, 72], mybir.dt.uint32))
        cixs = gstack.enter_context(nc.sbuf_tensor("cixs", [128, 64], mybir.dt.uint32))

        @blk.gpsimd
        def _(e):
            smb.inc(e.dma_start(out=bixs[:, :, :], in_=bidx[:, :, :]), "ixld", 16)
            smb.inc(e.dma_start(out=cixs[:, :], in_=cidx[:, :]), "ixld", 16)
        ix_hdl = smb["ixld"]

        psm_box = [PoolSems(pools, countss)]

        def psm():
            return psm_box[0]

        def gate():
            cur = psm_box[0]
            waits = [(cur._h(k), cur.base[k] + v)
                     for k, v in cur.n.items() if v > 0]
            for engname in ("sync", "vector", "scalar", "tensor", "gpsimd"):
                dec = getattr(blk, engname)

                def _body(e, w=tuple(waits)):
                    for h, v in w:
                        e.wait_ge(h, v)
                dec(_body)
            psm_box[0] = PoolSems(pools, countss)

        def ag_phase(tagname, pairs):
            @blk.gpsimd
            def _(e):
                for (src, dst) in pairs:
                    inst = e.collective_compute(
                        "AllGather", ALU.bypass,
                        replica_groups=[list(range(8))],
                        ins=[src.opt()], outs=[dst.opt()])
                    smb.inc(inst, tagname, 1)
                e.wait_ge(smb[tagname], smb.val(tagname))
            gate()

        layer_x = [x_in, x2_d]
        layer_w = [(wq0, wk0, wv0, wo0), (wq1, wk1, wv1, wo1)]
        layer_xo = [x1_d, x3_d]

        for l in range(2):
            a_sm = psm()

            def mk_hook(a_sm, l):
                def hook(e, n):
                    if n == 10:
                        e.wait_ge(a_sm["qe_dma_0"], a_sm.base["qe_dma_0"] + 32)
                        e.wait_ge(a_sm["qe_dma_1"], a_sm.base["qe_dma_1"] + 32)
                        inst = e.collective_compute(
                            "AllGather", ALU.bypass,
                            replica_groups=[list(range(8))],
                            ins=[kt_d[l].opt()], outs=[ktA[l].opt()])
                        smb.inc(inst, f"cc{l}", 1)
                    elif n == 12:
                        e.wait_ge(a_sm["v_dma"], a_sm.base["v_dma"] + 16)
                        inst = e.collective_compute(
                            "AllGather", ALU.bypass,
                            replica_groups=[list(range(8))],
                            ins=[v_d[l].opt()], outs=[vA[l].opt()])
                        smb.inc(inst, f"cc{l}", 1)
                    elif n == 18:
                        e.wait_ge(a_sm["qe_dma_0"], a_sm.base["qe_dma_0"] + 96)
                        e.wait_ge(a_sm["qe_dma_1"], a_sm.base["qe_dma_1"] + 96)
                        inst = e.collective_compute(
                            "AllGather", ALU.bypass,
                            replica_groups=[list(range(8))],
                            ins=[qe_d[l][0:2048, :].opt()],
                            outs=[qeA[l][0].opt()])
                        smb.inc(inst, f"cc{l}", 1)
                return hook

            with contextlib.ExitStack() as ps:
                emit_launch_A(nc, blk, ps, a_sm, layer_x[l],
                              layer_w[l][0], layer_w[l][1], layer_w[l][2],
                              cost, sint, ident, qe_d[l], kt_d[l], v_d[l],
                              tag=f"a{l}", gp_hook=mk_hook(a_sm, l))

            @blk.gpsimd
            def _(e, a_sm=a_sm, l=l):
                e.wait_ge(a_sm["qe_dma_0"], a_sm.base["qe_dma_0"] + 160)
                e.wait_ge(a_sm["qe_dma_1"], a_sm.base["qe_dma_1"] + 160)
                inst = e.collective_compute(
                    "AllGather", ALU.bypass,
                    replica_groups=[list(range(8))],
                    ins=[qe_d[l][2048:4096, :].opt()], outs=[qeA[l][1].opt()])
                smb.inc(inst, f"cc{l}", 1)
            gate()
            cc_hdl = smb[f"cc{l}"]
            cc_v1 = smb.val(f"cc{l}")
            cc_v0 = cc_v1 - 1
            with contextlib.ExitStack() as ps:
                smB = psm()
                sh = emit_B_shared(nc, blk, ps, smB, trimask, ident, tag=f"bs{l}")
                emit_launch_B(nc, blk, ps, smB, sh, None, None, None,
                              ot_d[l][0:512, :], tag=f"u0l{l}",
                              fused=dict(qeA0=qeA[l][0], qeA1=qeA[l][1],
                                         ktA=ktA[l], vA=vA[l], u=0,
                                         cc=(cc_hdl, cc_v0, cc_v1),
                                         ixs=bixs, ix_hdl=ix_hdl, ix_val=32))
                emit_launch_B(nc, blk, ps, smB, sh, None, None, None,
                              ot_d[l][512:1024, :], tag=f"u1l{l}", prev=f"u0l{l}",
                              fused=dict(qeA0=qeA[l][0], qeA1=qeA[l][1],
                                         ktA=ktA[l], vA=vA[l], u=1,
                                         cc=(cc_hdl, cc_v0, cc_v1),
                                         ixs=bixs, ix_hdl=ix_hdl, ix_val=32))
            b_sm = psm()

            @blk.gpsimd
            def _(e, b_sm=b_sm, l=l):
                e.wait_ge(b_sm[f"u0l{l}b_odma"], b_sm.base[f"u0l{l}b_odma"] + 16)
                inst = e.collective_compute(
                    "AllGather", ALU.bypass, replica_groups=[list(range(8))],
                    ins=[ot_d[l][0:512, :].opt()], outs=[otA_u[l][0].opt()])
                smb.inc(inst, f"cco{l}", 1)
                e.wait_ge(b_sm[f"u1l{l}b_odma"], b_sm.base[f"u1l{l}b_odma"] + 16)
                inst = e.collective_compute(
                    "AllGather", ALU.bypass, replica_groups=[list(range(8))],
                    ins=[ot_d[l][512:1024, :].opt()], outs=[otA_u[l][1].opt()])
                smb.inc(inst, f"cco{l}", 1)
                e.wait_ge(smb[f"cco{l}"], smb.val(f"cco{l}"))
            gate()
            with contextlib.ExitStack() as ps:
                emit_launch_C(nc, blk, ps, psm(), None,
                              layer_w[l][3], layer_x[l], layer_xo[l],
                              tag=f"c{l}",
                              fused_ot=(otA_u[l], cixs, ix_hdl, 32))
            gate()
            if l == 0:
                with contextlib.ExitStack() as ps:
                    emit_launch_D(nc, blk, ps, psm(), x1_d, wcat,
                                  b24, gate_in, wbr, ident, x2_d, tag="d")
                gate()
            else:
                with contextlib.ExitStack() as ps:
                    emit_launch_H(nc, blk, ps, psm(), x3_d, wbase,
                                  bbase, wgt, bthr, acat, bcat, selmat, ident,
                                  xf_out, tag="h")
                gate()
    return nc


def _wshape(w, hc_cols):
    din, dout = w.shape
    n = dout // hc_cols
    return np.ascontiguousarray(
        w.reshape(32, 128, n, hc_cols).transpose(2, 1, 0, 3))


def _cshape(w):
    return np.ascontiguousarray(w.reshape(32, 128, -1).transpose(1, 0, 2))


def kernel_fused(inputs):
    tri, ident, sel = _consts()
    cosf, sinf = rope_tables()
    x = np.ascontiguousarray(np.asarray(inputs["L0_attn_x"], dtype=np.float32))
    p = perm_interleave()
    wq0 = np.ascontiguousarray(np.asarray(inputs["L0_attn_wq"])[:, np.concatenate([h * HD + p for h in range(NHQ)])])
    wk0 = np.ascontiguousarray(np.asarray(inputs["L0_attn_wk"])[:, np.concatenate([g * HD + p for g in range(NKV)])])
    wcat = _cshape(np.concatenate([np.asarray(inputs["L0_ffn_W_res"]),
                                   np.asarray(inputs["L0_ffn_W_pre"]),
                                   np.asarray(inputs["L0_ffn_W_post"])],
                                  0).T.astype(np.float32))
    b24 = np.concatenate([np.asarray(inputs["L0_ffn_H_res_base"]).reshape(16),
                          np.asarray(inputs["L0_ffn_H_pre_base"]),
                          np.asarray(inputs["L0_ffn_H_post_base"])]).astype(np.float32)
    b24r = np.ascontiguousarray(np.broadcast_to(b24, (128, 24)))
    gater = np.full((128, 1), np.asarray(inputs["L0_ffn_gate"])[0], np.float32)
    wbr0 = np.asarray(inputs["L0_ffn_W_branch"], dtype=np.float32)
    wbr = np.ascontiguousarray(wbr0.reshape(8, 128, 1024).transpose(1, 0, 2))
    wbase = _wshape(np.asarray(inputs["L1_ffn_W_base"]).T.astype(np.float32), 256)
    bbase = np.ascontiguousarray(np.asarray(inputs["L1_ffn_b_base"])[None, :].astype(np.float32))
    wgt = _cshape(np.concatenate([np.asarray(inputs["L1_ffn_W_gate"]).T,
                                  np.asarray(inputs["L1_ffn_W_thr"])[:, None]],
                                 1).astype(np.float32))
    bthr = np.full((128, 1), np.asarray(inputs["L1_ffn_b_thr"])[0], np.float32)
    acat = _cshape(np.asarray(inputs["L1_ffn_A_w"]).reshape(128, D).T.astype(np.float32))
    bcat = np.ascontiguousarray(
        np.asarray(inputs["L1_ffn_B_w"]).transpose(0, 2, 1).reshape(128, D).astype(np.float32))

    def build_bidx(c):
        bq = c // 4
        arr = np.zeros((128, 2, 72), np.uint32)
        p = np.arange(128, dtype=np.uint32)
        for u in range(2):
            gq = (c % 4) * 2 + u
            for n, (kind, j, j2) in enumerate(b_gather_specs()):
                cw, h = chunk_owner(j)
                if kind == "q":
                    row = (bq * 4 + cw) * 2048 + gq * 256 + (j2 % 2) * 128 + p
                    arr[:, u, n] = row * 1024 + h * 512
                elif kind == "k":
                    row = bq * 4096 + cw * 1024 + gq * 128 + p
                    arr[:, u, n] = row * 1024 + h * 512
                else:
                    row = bq * 4096 + cw * 1024 + h * 512 + j2 * 128 + p
                    arr[:, u, n] = row * 1032 + gq * 129
        return arr

    def build_cidx(c):
        bq = c // 4
        cc = c % 4
        arr = np.zeros((128, 64), np.uint32)
        p = np.arange(128, dtype=np.uint32)
        for n, (g, half, a) in enumerate(c_gather_specs()):
            row = (bq * 4 + g // 2) * 512 + a * 128 + p
            col = cc * 512 if half == 0 else 3584 - cc * 512
            arr[:, n] = row * 4096 + col
        return arr

    wq0s = _wshape(wq0, 256)
    wk0s = _wshape(wk0, 256)
    wv0s = _wshape(np.asarray(inputs["L0_attn_wv"], dtype=np.float32), 256)
    wo0s = _wshape(np.asarray(inputs["L0_attn_wo"], dtype=np.float32), 512)
    wq1s = _wshape(np.asarray(inputs["L1_attn_Wq"], dtype=np.float32), 256)
    wk1s = _wshape(np.asarray(inputs["L1_attn_Wk"], dtype=np.float32), 256)
    wv1s = _wshape(np.asarray(inputs["L1_attn_Wv"], dtype=np.float32), 256)
    wo1s = _wshape(np.asarray(inputs["L1_attn_Wo"], dtype=np.float32), 512)

    nc = build_nc_fused()
    in_maps = []
    for c in range(NCORES):
        b = c // 4
        pos = core_positions(c)
        in_maps.append({
            "bidx": build_bidx(c), "cidx": build_cidx(c),
            "x_in": np.ascontiguousarray(x[b, pos, :]),
            "wq0": wq0s, "wk0": wk0s, "wv0": wv0s, "wo0": wo0s,
            "wq1": wq1s, "wk1": wk1s, "wv1": wv1s, "wo1": wo1s,
            "cost": np.ascontiguousarray(cosf[:, pos]).astype(np.float32),
            "sint": np.ascontiguousarray(sinf[:, pos]).astype(np.float32),
            "ident": ident, "trimask": tri,
            "wcat": wcat, "b24": b24r, "gate_in": gater, "wbr": wbr,
            "wbase": wbase, "bbase": bbase, "wgt": wgt, "bthr": bthr,
            "acat": acat, "bcat": bcat, "selmat": sel,
        })
    res = run_bass_kernel_spmd(nc, in_maps, core_ids=list(range(NCORES))).results
    out = np.zeros((B, S, D), np.float32)
    for c in range(NCORES):
        b = c // 4
        pos = core_positions(c)
        out[b, pos, :] = res[c]["xf_out"]
    return out


_TRI = None


def _consts():
    global _TRI
    if _TRI is None:
        tri = np.triu(np.ones((128, 128), np.float32)).astype(BF)
        ident = np.eye(128, dtype=BF)
        sel = np.zeros((8, 128), np.float32)
        for e_ in range(8):
            sel[e_, e_ * 16:(e_ + 1) * 16] = 1.0
        _TRI = (tri, ident, sel.astype(BF))
    return _TRI


def kernel(**inputs):
    return kernel_fused(inputs)


# version 20 (final)
# speedup vs baseline: 1.0343x; 1.0200x over previous
import contextlib
import numpy as np
import ml_dtypes
import concourse.bass as bass
import concourse.mybir as mybir
from concourse.bass_utils import run_bass_kernel_spmd

F32 = mybir.dt.float32
BF16 = mybir.dt.bfloat16
AF = mybir.ActivationFunctionType
ALU = mybir.AluOpType
AX = mybir.AxisListType
BF = ml_dtypes.bfloat16

B, S, D = 2, 4096, 4096
NCORES = 8
NLOC = 1024
CH = 512
NHQ, NKV, HD = 32, 8, 128
KVD = NKV * HD
LN_EPS = 1e-5


def core_chunks(c):
    cc = c % 4
    return (cc, 7 - cc)


def chunk_owner(j):
    return (j, 0) if j < 4 else (7 - j, 1)


def core_positions(c):
    lo, hi = core_chunks(c)
    return np.concatenate([np.arange(lo * CH, (lo + 1) * CH),
                           np.arange(hi * CH, (hi + 1) * CH)])


def b_gather_specs():
    specs = []
    for j in range(8):
        specs.append(("k", j, 0))
        for kt in range(4):
            specs.append(("v", j, kt))
    for j2pair in ((0, 1), (2, 3)):
        for j in range(8):
            for j2 in j2pair:
                specs.append(("q", j, j2))
    return specs


def c_gather_specs():
    return [(g, half, a) for g in range(8) for half in range(2) for a in range(4)]


class Sems:
    def __init__(self, nc, stack):
        self.nc = nc
        self.stack = stack
        self.h = {}
        self.n = {}

    def __getitem__(self, name):
        if name not in self.h:
            self.h[name] = self.stack.enter_context(self.nc.semaphore(name))
            self.n[name] = 0
        return self.h[name]

    def inc(self, inst, name, k=1):
        inst.then_inc(self[name], k)
        self.n[name] += k

    def val(self, name):
        self[name]
        return self.n[name]

    def wait(self, e, name, v):
        e.wait_ge(self[name], v)


class TagSems:

    def __init__(self, base, prefix):
        self.base = base
        self.prefix = prefix

    def __getitem__(self, name):
        return self.base[self.prefix + name]

    def inc(self, inst, name, k=1):
        self.base.inc(inst, self.prefix + name, k)

    def val(self, name):
        return self.base.val(self.prefix + name)


_GP_SEM_TOKENS = ("ldc", "w_ld", "b_ld", "c_ot", "c_w_", "d_wc", "d_wbr",
                  "h_wc", "h_w_")


def _is_gp_sem(name):
    if "ldci" in name:
        return False
    return any(t in name for t in _GP_SEM_TOKENS)


class PoolSems:

    def __init__(self, pools, countss):
        self.pools = pools
        self.countss = countss
        self.map = {}
        self.base = {}
        self.n = {}
        self.nexts = [0, 0]

    def _h(self, name):
        w, s = self.map[name]
        return self.pools[w][s]

    def __getitem__(self, name):
        if name not in self.map:
            w = 0 if _is_gp_sem(name) else 1
            s = self.nexts[w]
            self.map[name] = (w, s)
            self.nexts[w] += 1
            self.base[name] = self.countss[w][s]
            self.n[name] = 0
        return self._h(name)

    def inc(self, inst, name, k=1):
        h = self[name]
        inst.then_inc(h, k)
        self.n[name] += k
        w, s = self.map[name]
        self.countss[w][s] += k

    def val(self, name):
        self[name]
        return self.n[name]

    def wait(self, e, name, v):
        e.wait_ge(self[name], self.base[name] + v)


def emit_gate(nc, blk, smbase, delta):
    for engname in ("sync", "vector", "scalar", "tensor", "gpsimd"):
        dec = getattr(blk, engname)

        def _body(e, d=tuple(delta.items())):
            for k, v in d:
                e.wait_ge(smbase[k], v)
        dec(_body)



def emit_launch_A(nc, blk, stack, sm, x_in, wq, wk, wv, cost, sint, ident,
                  qe_out, kt_out, v_out, tag="a", stage=2, gp_hook=None):
    sb = lambda name, shape, dt: stack.enter_context(nc.sbuf_tensor(tag + name, shape, dt))
    ps = lambda name, shape, dt: stack.enter_context(nc.psum_tensor(tag + name, shape, dt))

    xt = sb("xt", [128, 2, 4096], F32)
    xn_t = sb("xn_t", [128, 2, 4096], BF16)
    sq = sb("sq", [128, 4096], F32)
    xnT = sb("xnT", [128, 32, 1024], BF16)
    sums = sb("sums", [128, 8], F32)
    redt = sb("redt", [128, 1], F32)
    msqt = sb("msqt", [128, 1], F32)
    sumsq = sb("sumsq", [128, 8], F32)
    mvec = sb("mvec", [128, 8], F32)
    varv = sb("varv", [128, 8], F32)
    rstd = sb("rstd", [128, 8], F32)
    epsb = sb("epsb", [128, 1], F32)
    wbuf = sb("wbuf", [128, 2, 32, 256], BF16)
    qraw = sb("qraw", [128, 2, 2, 1024], BF16)
    qe_s = sb("qe_s", [128, 2, 2, 1024], BF16)
    rtmp = sb("rtmp", [128, 2, 1024], BF16)
    vstage = sb("vstage", [128, 8, 8, 129], BF16)
    cosb = sb("cosb", [128, 2, 1024], BF16)
    sinb = sb("sinb", [128, 2, 1024], BF16)
    idb = sb("idb", [128, 128], BF16)

    pst = [ps("pst0", [128, 1024], BF16), ps("pst1", [128, 1024], BF16)]
    pp = [ps("pp0", [128, 512], F32), ps("pp1", [128, 512], F32)]

    x_r = x_in.rearrange("(t p) d -> p t d", p=128)
    NHC = 24

    @blk.gpsimd
    def _(e):
        for half in (slice(0, 64), slice(64, 128)):
            sm.inc(e.dma_start(out=cosb[half, 0, :], in_=cost[:, :]), "ldc", 16)
            sm.inc(e.dma_start(out=cosb[half, 1, :], in_=cost[:, :]), "ldc", 16)
            sm.inc(e.dma_start(out=sinb[half, 0, :], in_=sint[:, :]), "ldc", 16)
            sm.inc(e.dma_start(out=sinb[half, 1, :], in_=sint[:, :]), "ldc", 16)

    @blk.sync
    def _(e):
        sm.inc(e.dma_start(out=idb[:, :], in_=ident[:, :]), "ldci", 16)
        for i in range(8):
            if i >= 2:
                sm.wait(e, "x1done", i - 1)
                sm.wait(e, "ssq", i - 1)
                sm.wait(e, "xn_done", i - 1)
            inst = e.dma_start(out=xt[:, i % 2, 0:2048], in_=x_r[:, i, 0:2048])
            sm.inc(inst, f"ldx1_{i % 2}", 16)

    @blk.vector
    def _(e):
        sm.inc(e.memset(epsb[:, :], LN_EPS), "veps", 1)
        sm.inc(e.memset(vstage[:, :, :, 128:129], 1.0), "vinit", 1)

        def cast(i):
            if i >= 2:
                sm.wait(e, "tr_mm", 32 * (i - 1))
            sm.wait(e, "rstd", i + 1)
            sm.wait(e, f"ldxs_{i % 2}", 16 * (i // 2 + 1))
            inst = e.tensor_scalar(xn_t[:, i % 2, :], xt[:, i % 2, :],
                                   mvec[:, i:i + 1], rstd[:, i:i + 1],
                                   ALU.subtract, ALU.mult)
            sm.inc(inst, "xn_done", 1)

        for i in range(8):
            sm.wait(e, f"ldx1_{i % 2}", 16 * (i // 2 + 1))
            sm.wait(e, f"ldxs_{i % 2}", 16 * (i // 2 + 1))
            if i >= 1:
                sm.wait(e, "vcp", 3 * i)
            inst = e.reduce_sum(redt[:, :], xt[:, i % 2, :], AX.X)
            sm.inc(inst, "x1done", 1)
            sm.wait(e, "x1done", i + 1)
            sm.inc(e.tensor_copy(sums[:, i:i + 1], redt[:, :]), "vcp", 1)
            sm.wait(e, "vcp", 3 * i + 1)
            sm.inc(e.tensor_scalar_mul(mvec[:, i:i + 1], sums[:, i:i + 1],
                                       1.0 / 4096.0), "vcp", 1)
            sm.wait(e, "vcp", 3 * i + 2)
            sm.inc(e.tensor_mul(msqt[:, :], mvec[:, i:i + 1], mvec[:, i:i + 1]),
                   "vcp", 1)
            if i >= 1:
                cast(i - 1)
            sm.wait(e, "vcp", 3 * i + 3)
            sm.wait(e, "ssq", i + 1)
            inst = e.scalar_tensor_tensor(varv[:, i:i + 1], sumsq[:, i:i + 1],
                                          1.0 / 4096.0, msqt[:, :],
                                          ALU.mult, ALU.subtract)
            sm.inc(inst, "vvar", 1)
        cast(7)

    @blk.scalar
    def _(e):
        def evac(g_idx):
            i_, grp = divmod(g_idx, 4)
            pbank = pst[g_idx % 2]
            sm.wait(e, "tr_mm", 8 * (g_idx + 1))
            inst = e.copy(
                xnT[:, grp * 8:(grp + 1) * 8, i_ * 128:(i_ + 1) * 128],
                pbank[:, :].rearrange("p (a b) -> p a b", b=128))
            sm.inc(inst, "tr_ev", 1)

        for i in range(2):
            inst = e.dma_start(out=xt[:, i % 2, 2048:4096],
                               in_=x_r[:, i, 2048:4096])
            sm.inc(inst, f"ldxs_{i % 2}", 16)
        sm.wait(e, "veps", 1)
        for i in range(8):
            sm.wait(e, f"ldx1_{i % 2}", 16 * (i // 2 + 1))
            sm.wait(e, f"ldxs_{i % 2}", 16 * (i // 2 + 1))
            if i >= 1:
                sm.wait(e, "ssq", i)
            inst = e.activation(sq[:, :], xt[:, i % 2, :], AF.Square,
                                accum_out=sumsq[:, i:i + 1])
            sm.inc(inst, "ssq", 1)
            sm.wait(e, "vvar", i + 1)
            sm.inc(e.activation(varv[:, i:i + 1], varv[:, i:i + 1], AF.Ln,
                                bias=epsb[:, 0:1]), "sln", 1)
            sm.wait(e, "sln", i + 1)
            inst = e.activation(rstd[:, i:i + 1], varv[:, i:i + 1],
                                AF.Exp, scale=-0.5)
            sm.inc(inst, "rstd", 1)
            if i >= 2:
                for g in range(4):
                    evac(4 * (i - 2) + g)
            if i + 2 < 8:
                sm.wait(e, "x1done", i + 1)
                sm.wait(e, "xn_done", i + 1)
                sm.wait(e, "ssq", i + 1)
                inst = e.dma_start(out=xt[:, i % 2, 2048:4096],
                                   in_=x_r[:, i + 2, 2048:4096])
                sm.inc(inst, f"ldxs_{i % 2}", 16)
        for g_idx in range(24, 32):
            evac(g_idx)

    if stage < 1:
        @blk.gpsimd
        def _(e):
            sm.wait(e, "rstd", 8)
            sm.wait(e, "vvar", 8)
            sm.wait(e, "vcp", 24)
            sm.wait(e, "ssq", 8)
            qe_r = qe_out.rearrange("(hh p) t -> p hh t", p=128)
            dbg = [sums, sumsq, mvec, varv, rstd]
            if stage >= 0.5:
                pass
            for j, tsr in enumerate(dbg):
                inst = e.dma_start(out=qe_r[:, j, 0:8], in_=tsr[:, :])
                sm.inc(inst, "qe_dma", 16)
            sm.wait(e, "qe_dma", sm.val("qe_dma"))
        return

    @blk.tensor
    def _(e):
        sm.wait(e, "ldci", 16)
        for i in range(8):
            for grp in range(4):
                g_idx = i * 4 + grp
                pbank = pst[g_idx % 2]
                sm.wait(e, "xn_done", i + 1)
                if g_idx >= 2:
                    sm.wait(e, "tr_ev", g_idx - 1)
                for k in range(8):
                    dt = grp * 8 + k
                    inst = e.transpose(pbank[:, k * 128:(k + 1) * 128],
                                       xn_t[:, i % 2, dt * 128:(dt + 1) * 128], idb[:, :])
                    sm.inc(inst, "tr_mm", 1)

    if stage < 2:
        @blk.sync
        def _(e):
            sm.wait(e, "tr_ev", 32)
            qe_r = qe_out.rearrange("(hh p) t -> p hh t", p=128)
            inst = e.dma_start(out=qe_r[:, 0:32, :], in_=xnT[:, :, :])
            sm.inc(inst, "qe_dma", 16)
            sm.wait(e, "qe_dma", sm.val("qe_dma"))
        return

    HCS = [("k", wk, i) for i in range(4)] + [("v", wv, i) for i in range(4)] \
        + [("q", wq, i) for i in list(range(0, 16, 2)) + list(range(1, 16, 2))]
    groups = []
    hc_g0 = []
    rope_of_hc = {}
    rn = 0
    for n, (kind, w, i) in enumerate(HCS):
        hc_g0.append(len(groups))
        if kind == "v":
            for tt in range(8):
                groups.append((n, "v", tt))
        else:
            rope_of_hc[n] = rn
            rn += 1
            for ot in range(2):
                for tc in range(2):
                    groups.append((n, kind, (ot, tc)))
    hc_g0.append(len(groups))
    ev_after = []
    ec_ = 0
    for (n, kind, pl) in groups:
        ec_ += 2 if kind == "v" else 1
        ev_after.append(ec_)
    rope_hcs = [n for n, (kind, w, i) in enumerate(HCS) if kind != "v"]

    @blk.gpsimd
    def _(e):
        for n, (kind, w, i) in enumerate(HCS):
            if n >= 2:
                sm.wait(e, "p_mm", hc_g0[n - 1])
            inst = e.dma_start(out=wbuf[:, n % 2, :, :], in_=w[i, :, :, :])
            sm.inc(inst, f"w_ld_{n % 2}", 16)
            if gp_hook is not None:
                gp_hook(e, n)

    @blk.tensor
    def _(e):
        sm.wait(e, "tr_ev", 32)
        for m, (n, kind, pl) in enumerate(groups):
            if m == hc_g0[n]:
                sm.wait(e, f"w_ld_{n % 2}", 16 * (n // 2 + 1))
            pb = pp[m % 2]
            if m >= 2:
                sm.wait(e, "p_ev", ev_after[m - 2])
            if kind == "v":
                tt = pl
                for dm in range(32):
                    inst = e.matmul(pb[:, 0:256],
                                    xnT[:, dm, tt * 128:(tt + 1) * 128],
                                    wbuf[:, n % 2, dm, :],
                                    start=(dm == 0), stop=(dm == 31))
            else:
                ot, tc = pl
                for dm in range(32):
                    inst = e.matmul(pb[:, :],
                                    wbuf[:, n % 2, dm, ot * 128:(ot + 1) * 128],
                                    xnT[:, dm, tc * 512:(tc + 1) * 512],
                                    start=(dm == 0), stop=(dm == 31))
            sm.inc(inst, "p_mm", 1)

    @blk.scalar
    def _(e):
        for m, (n, kind, pl) in enumerate(groups):
            pb = pp[m % 2]
            sm.wait(e, "p_mm", m + 1)
            if kind == "v":
                tt = pl
                vc = HCS[n][2]
                for gg in range(2):
                    g = vc * 2 + gg
                    inst = e.copy(vstage[:, tt, g, 0:128],
                                  pb[:, gg * 128:(gg + 1) * 128])
                    sm.inc(inst, "p_ev", 1)
            else:
                ot, tc = pl
                r = rope_of_hc[n]
                if pl == (0, 0) and r >= 2:
                    sm.wait(e, "rope_done", r - 1)
                inst = e.copy(qraw[:, r % 2, ot, tc * 512:(tc + 1) * 512],
                              pb[:, :])
                sm.inc(inst, "p_ev", 1)

    @blk.vector
    def _(e):
        sm.wait(e, "ldc", 128)
        for r, n in enumerate(rope_hcs):
            sm.wait(e, "p_ev", ev_after[hc_g0[n + 1] - 1])
            if r >= 2:
                sm.wait(e, f"qe_dma_{r % 2}", 16 * (r // 2))
            qr = qraw[0:64, r % 2, :, :]
            qi = qraw[64:128, r % 2, :, :]
            qeE = qe_s[0:64, r % 2, :, :]
            qeO = qe_s[64:128, r % 2, :, :]
            t_lo = rtmp[0:64, :, :]
            t_hi = rtmp[64:128, :, :]
            if r >= 1:
                sm.wait(e, "vr3", r)
                sm.wait(e, "rope_done", r)
            sm.inc(e.tensor_mul(t_lo, qi, sinb[64:128, :, :]), "vr1", 1)
            sm.inc(e.tensor_mul(qeE, qr, cosb[0:64, :, :]), "vr2", 1)
            sm.wait(e, "vr1", r + 1)
            sm.wait(e, "vr2", r + 1)
            sm.inc(e.tensor_sub(qeE, qeE, t_lo), "vr3", 1)
            sm.inc(e.tensor_mul(t_hi, qr, sinb[0:64, :, :]), "vr4", 1)
            sm.inc(e.tensor_mul(qeO, qi, cosb[64:128, :, :]), "vr5", 1)
            sm.wait(e, "vr4", r + 1)
            sm.wait(e, "vr5", r + 1)
            inst = e.tensor_add(qeO, qeO, t_hi)
            sm.inc(inst, "rope_done", 1)

    @blk.sync
    def _(e):
        qe_r = qe_out.rearrange("(hh p) t -> p hh t", p=128)
        kt_r = kt_out.rearrange("(hh p) t -> p hh t", p=128)
        sm.wait(e, "p_ev", ev_after[hc_g0[8] - 1])
        sm.wait(e, "vinit", 1)
        v_r = v_out.rearrange("(t p) c -> p t c", p=128)
        inst = e.dma_start(out=v_r[:, :, :],
                           in_=vstage[:, :, :, :].rearrange("p t g c -> p t (g c)"))
        sm.inc(inst, "v_dma", 16)
        for r, n in enumerate(rope_hcs):
            kind, w, i = HCS[n]
            sm.wait(e, "rope_done", r + 1)
            sm.wait(e, "vr3", r + 1)
            if kind == "q":
                hh = (i % 2) * 16 + (i // 2) * 2
                dst = qe_r[:, hh:hh + 2, :]
            else:
                dst = kt_r[:, i * 2:(i + 1) * 2, :]
            inst = e.dma_start(out=dst, in_=qe_s[:, r % 2, :, :])
            sm.inc(inst, f"qe_dma_{r % 2}", 16)
        sm.wait(e, "qe_dma_0", sm.val("qe_dma_0"))
        sm.wait(e, "qe_dma_1", sm.val("qe_dma_1"))
        sm.wait(e, "v_dma", 16)


def rope_tables():
    f = np.arange(0, HD, 2, dtype=np.float64) / HD
    inv = 1.0 / (10000.0 ** f)
    t = np.arange(S, dtype=np.float64)
    ang = np.outer(inv, t)
    return (np.cos(ang).astype(np.float32), np.sin(ang).astype(np.float32))


def perm_interleave():
    return np.concatenate([np.arange(0, HD, 2), np.arange(1, HD, 2)])



def emit_B_shared(nc, blk, stack, sm, trimask, ident, tag="bs"):
    sb = lambda name, shape, dt: stack.enter_context(nc.sbuf_tensor(tag + name, shape, dt))
    ps = lambda name, shape, dt: stack.enter_context(nc.psum_tensor(tag + name, shape, dt))
    sh = {}
    sh["msk"] = sb("msk", [128, 128], BF16)
    sh["idb"] = sb("idb", [128, 128], BF16)
    sh["zeros"] = sb("zeros", [128, 128], BF16)
    sh["p_sb"] = sb("p_sb", [128, 4, 2, 512], BF16)
    sh["rl_t"] = sb("rl_t", [128, 1], F32)
    sh["rl_t8"] = sb("rl_t8", [128, 8], F32)
    sh["rl"] = sb("rl", [128, 8, 1], F32)
    sh["o_sb"] = sb("o_sb", [128, 8, 128], BF16)
    sh["psT"] = [ps("psT0", [128, 1024], F32), ps("psT1", [128, 1024], F32)]
    sh["pOl"] = ps("pOl", [128, 1536], F32)
    sh["ptr"] = ps("ptr", [128, 1024], BF16)

    @blk.sync
    def _(e):
        sm.inc(e.dma_start(out=sh["msk"][:, :], in_=trimask[:, :]), "sh_ld", 16)
        sm.inc(e.dma_start(out=sh["idb"][:, :], in_=ident[:, :]), "sh_ld", 16)

    @blk.vector
    def _(e):
        sm.inc(e.memset(sh["zeros"][:, :], 0.0), "sh_ld", 1)
    return sh


def emit_launch_B(nc, blk, stack, sm, sh, q_in, k_in, v_in, ot_out, tag, prev=None, fused=None):
    sb = lambda name, shape, dt: stack.enter_context(nc.sbuf_tensor(tag + name, shape, dt))
    qsb = sb("qsb", [128, 4, 4096], BF16)
    ksb = sb("ksb", [128, 4096], BF16)
    vsb = sb("vsb", [128, 32, 129], BF16)
    ot_sb = sb("ot_sb", [128, 4, 4096], BF16)
    msk, idb, zeros = sh["msk"], sh["idb"], sh["zeros"]
    p_sb, rl_t, rl, o_sb = sh["p_sb"], sh["rl_t"], sh["rl"], sh["o_sb"]
    rl_t8 = sh["rl_t8"]
    psT, pOl, ptr = sh["psT"], sh["pOl"], sh["ptr"]

    def ol_ap(a):
        off = (a // 3) * 512 + (a % 3) * 129
        return pOl[:, off:off + 129]

    SCHED = []
    for hp in range(2):
        for qc in range(8):
            group = [(kc, kt, kc == qc, (kt * 128 if kc == qc else 0))
                     for kc in range(qc + 1) for kt in range(4)]
            SCHED.append((qc, hp, group))

    exp_after, msk_after, pv_after, mm_after = [], [], [], []
    ec = mc = pc = mmc = 0
    for gi, (qc, hp, group) in enumerate(SCHED):
        for (kc, kt, diag, qoff) in group:
            mmc += 2
            ec += 2 if diag else 1
            mc += 2 if diag else 0
            pc += sum(1 for hh in range(2) for qt in range((kt if diag else 0), 4))
            exp_after.append(ec)
            msk_after.append(mc)
            pv_after.append(pc)
            mm_after.append(mmc)
    NGRP = len(SCHED)
    TOT = dict(b_exp=ec, b_msk=mc, b_pv=pc, b_mm=mmc, b_oev=3 * NGRP,
               b_rc=4 * NGRP, b_te=NGRP, b_tr=8 * NGRP)

    def entry_waits(e):
        if prev is not None:
            for s, v in TOT.items():
                sm.wait(e, prev + s, v)

    @blk.sync
    def _(e):
        if fused is None:
            sm.inc(e.dma_start(out=qsb[:, :, :],
                               in_=q_in.rearrange("(j p) t -> p j t", p=128)),
                   tag + "b_ld", 16)
            sm.inc(e.dma_start(out=ksb[:, :], in_=k_in[:, :]), tag + "b_ld", 16)
            sm.inc(e.dma_start(out=vsb[:, :, :],
                               in_=v_in.rearrange("(a p) c -> p a c", p=128)),
                   tag + "b_ld", 16)
        else:
            pass

    if fused is not None:
        @blk.gpsimd
        def _(e):
            import bass_rust as _br
            ktA, vA, u = fused["ktA"], fused["vA"], fused["u"]
            ixs = fused["ixs"]
            cc_hdl, cc_v0, cc_v1 = fused["cc"]
            e.wait_ge(fused["ix_hdl"], fused["ix_val"])
            fl = lambda t: t.rearrange("r t -> (r t)").rearrange("(r o) -> r o", o=1)
            qflat = [fl(fused["qeA0"]), fl(fused["qeA1"])]
            kflat = fl(ktA)
            vflat = fl(vA)
            e.wait_ge(cc_hdl, cc_v0)
            waited1 = [False]
            for n, (kind, j, j2) in enumerate(b_gather_specs()):
                if kind == "q":
                    if j2 >= 2 and not waited1[0]:
                        e.wait_ge(cc_hdl, cc_v1)
                        waited1[0] = True
                    out = qsb[:, j2, j * 512:(j + 1) * 512]
                    flat = qflat[j2 // 2]
                elif kind == "k":
                    out = ksb[:, j * 512:(j + 1) * 512]
                    flat = kflat
                else:
                    out = vsb[:, j * 4 + j2, :]
                    flat = vflat
                inst = e.indirect_dma_start(
                    out, None, flat, _br.IndirectOffsetOnAxis(ixs[:, u, n:n + 1], 0))
                if kind == "q":
                    sm.inc(inst, tag + f"b_ldq{j2 // 2}", 16)
                else:
                    sm.inc(inst, tag + "b_ldkv", 16)

    @blk.vector
    def _(e):
        entry_waits(e)
        s_i = 0
        for g_i, (qc, hp, group) in enumerate(SCHED):
            for (kc, kt, diag, qoff) in group:
                par = s_i % 4
                if diag:
                    sm.wait(e, tag + "b_exp", exp_after[s_i])
                    for hh in range(2):
                        inst = e.tensor_mul(p_sb[:, par, hh, kt * 128:(kt + 1) * 128],
                                            p_sb[:, par, hh, kt * 128:(kt + 1) * 128],
                                            msk[:, :])
                        sm.inc(inst, tag + "b_msk", 1)
                s_i += 1
            sm.wait(e, tag + "b_pv", pv_after[s_i - 1])
            def bank_view(bk, cnt):
                return pOl[:, bk * 512:bk * 512 + cnt * 129] \
                    .rearrange("p (r s) -> p r s", s=129)
            rc = 4 * g_i
            for bk, cnt, a0 in ((0, 3, 0), (1, 3, 3), (2, 2, 6)):
                inst = e.tensor_copy(rl[:, a0:a0 + cnt, :],
                                     bank_view(bk, cnt)[:, :, 128:129])
                sm.inc(inst, tag + "b_rc", 1)
                rc += 1
            sm.wait(e, tag + "b_rc", rc)
            inst = e.reciprocal(rl_t8[:, :], rl[:, :, 0])
            sm.inc(inst, tag + "b_rc", 1)
            sm.wait(e, tag + "b_rc", rc + 1)
            if g_i >= 1:
                sm.wait(e, tag + "b_tr", 8 * g_i)
            for bk, cnt, a0 in ((0, 3, 0), (1, 3, 3), (2, 2, 6)):
                inst = e.tensor_mul(
                    o_sb[:, a0:a0 + cnt, :],
                    bank_view(bk, cnt)[:, :, 0:128],
                    rl_t8[:, a0:a0 + cnt].rearrange("p (r o) -> p r o", o=1)
                        .broadcast_to((128, cnt, 128)))
                sm.inc(inst, tag + "b_oev", 1)

    @blk.tensor
    def _(e):
        sm.wait(e, "sh_ld", 33)
        if fused is None:
            sm.wait(e, tag + "b_ld", 48)
        entry_waits(e)
        s_i = 0
        for g_i, (qc, hp, group) in enumerate(SCHED):
            if fused is not None and qc == 0:
                if hp == 0:
                    sm.wait(e, tag + "b_ldkv", 16 * 40)
                sm.wait(e, tag + f"b_ldq{hp}", 16 * 16)
            if g_i >= 1:
                sm.wait(e, tag + "b_oev", 3 * g_i)
                sm.wait(e, tag + "b_rc", 4 * g_i)
            last_touch = {}
            first_touch = {}
            for li, (kc, kt, diag, qoff) in enumerate(group):
                for hh in range(2):
                    for qt in range((kt if diag else 0), 4):
                        bk = (hh * 4 + qt) // 3
                        last_touch[bk] = (li, hh, qt)
                        if bk not in first_touch:
                            first_touch[bk] = (li, hh, qt)
            base_s = s_i

            def emit_sT(li2):
                s2 = base_s + li2
                (kc2, kt2, diag2, qoff2) = group[li2]
                if s2 >= 2:
                    sm.wait(e, tag + "b_exp", exp_after[s2 - 2])
                for hh in range(2):
                    inst = e.matmul(
                        psT[s2 % 2][:, hh * 512 + qoff2:(hh + 1) * 512],
                        ksb[:, (kc2 * 4 + kt2) * 128:(kc2 * 4 + kt2 + 1) * 128],
                        qsb[:, hp * 2 + hh, qc * 512 + qoff2:(qc + 1) * 512],
                        start=True, stop=True)
                    sm.inc(inst, tag + "b_mm", 1)

            emit_sT(0)
            for li, (kc, kt, diag, qoff) in enumerate(group):
                par = s_i % 4
                if li + 1 < len(group):
                    emit_sT(li + 1)
                sm.wait(e, tag + "b_exp", exp_after[s_i])
                if diag:
                    sm.wait(e, tag + "b_msk", msk_after[s_i])
                for hh in range(2):
                    for qt in range((kt if diag else 0), 4):
                        a = hh * 4 + qt
                        is_last = last_touch[a // 3] == (li, hh, qt)
                        is_first = first_touch[a // 3] == (li, hh, qt)
                        inst = e.matmul(ol_ap(a),
                                        p_sb[:, par, hh, qt * 128:(qt + 1) * 128],
                                        vsb[:, kc * 4 + kt, :],
                                        start=is_first, stop=is_last)
                        sm.inc(inst, tag + "b_pv", 1)
                s_i += 1
            sm.wait(e, tag + "b_oev", 3 * (g_i + 1))
            if g_i >= 1:
                sm.wait(e, tag + "b_te", g_i)
            for a in range(8):
                inst = e.transpose(ptr[:, a * 128:(a + 1) * 128], o_sb[:, a, :],
                                   idb[:, :])
                sm.inc(inst, tag + "b_tr", 1)

    @blk.scalar
    def _(e):
        entry_waits(e)
        s_i = 0
        for g_i, (qc, hp, group) in enumerate(SCHED):
            for (kc, kt, diag, qoff) in group:
                par = s_i % 4
                pb = psT[s_i % 2]
                sm.wait(e, tag + "b_mm", mm_after[s_i])
                if s_i >= 4:
                    sm.wait(e, tag + "b_pv", pv_after[s_i - 4])
                if not diag:
                    inst = e.activation(p_sb[:, par, :, :], pb[:, :], AF.Exp,
                                        scale=1.0 / np.sqrt(128.0))
                    sm.inc(inst, tag + "b_exp", 1)
                else:
                    for hh in range(2):
                        inst = e.activation(p_sb[:, par, hh, qoff:512],
                                            pb[:, hh * 512 + qoff:(hh + 1) * 512],
                                            AF.Exp, scale=1.0 / np.sqrt(128.0))
                        sm.inc(inst, tag + "b_exp", 1)
                s_i += 1
            sm.wait(e, tag + "b_tr", 8 * (g_i + 1))
            inst = e.copy(ot_sb[:, hp * 2:hp * 2 + 2, qc * 512:(qc + 1) * 512],
                          ptr[:, :].rearrange("p (j q) -> p j q", q=512))
            sm.inc(inst, tag + "b_te", 1)

    @blk.sync
    def _(e):
        sm.wait(e, tag + "b_te", NGRP)
        inst = e.dma_start(out=ot_out.rearrange("(j p) t -> p j t", p=128),
                           in_=ot_sb[:, :, :])
        sm.inc(inst, tag + "b_odma", 16)
        sm.wait(e, tag + "b_odma", 16)


def build_nc_B():
    nc = bass.Bass("TRN2", target_bir_lowering=False)
    ins = {}
    for u in range(2):
        ins[f"q_in{u}"] = nc.dram_tensor(f"q_in{u}", [512, 4096], BF16, kind="ExternalInput").ap()
        ins[f"k_in{u}"] = nc.dram_tensor(f"k_in{u}", [128, 4096], BF16, kind="ExternalInput").ap()
        ins[f"v_in{u}"] = nc.dram_tensor(f"v_in{u}", [4096, 129], BF16, kind="ExternalInput").ap()
    trimask = nc.dram_tensor("trimask", [128, 128], BF16, kind="ExternalInput").ap()
    ident = nc.dram_tensor("ident", [128, 128], BF16, kind="ExternalInput").ap()
    outs = {}
    for u in range(2):
        outs[f"ot_out{u}"] = nc.dram_tensor(f"ot_out{u}", [512, 4096], BF16, kind="ExternalOutput").ap()
    with contextlib.ExitStack() as stack:
        sm = Sems(nc, stack)
        blk = stack.enter_context(nc.Block())
        sh = emit_B_shared(nc, blk, stack, sm, trimask, ident)
        emit_launch_B(nc, blk, stack, sm, sh, ins["q_in0"], ins["k_in0"],
                      ins["v_in0"], outs["ot_out0"], tag="u0")
        emit_launch_B(nc, blk, stack, sm, sh, ins["q_in1"], ins["k_in1"],
                      ins["v_in1"], outs["ot_out1"], tag="u1", prev="u0")
    return nc



def emit_launch_C(nc, blk, stack, sm, ot_in, wo, x_in, xo_out, tag="c", fused_ot=None):
    sb = lambda name, shape, dt: stack.enter_context(nc.sbuf_tensor(tag + name, shape, dt))
    ps = lambda name, shape, dt: stack.enter_context(nc.psum_tensor(tag + name, shape, dt))

    otb = sb("otb", [128, 32, 1024], BF16)
    wbuf = sb("wbuf", [128, 2, 32, 512], BF16)
    xpc = sb("xpc", [128, 2, 512], F32)
    xo = sb("xo", [128, 2, 512], F32)
    pp = [ps("pp0", [128, 512], F32), ps("pp1", [128, 512], F32)]

    x_r = x_in.rearrange("(t p) d -> p t d", p=128)
    xo_r = xo_out.rearrange("(t p) d -> p t d", p=128)

    @blk.sync
    def _(e):
        if fused_ot is None:
            sm.inc(e.dma_start(out=otb[:, :, :],
                               in_=ot_in.rearrange("(a p) t -> p a t", p=128)),
                   "c_ot", 16)

        def store(j):
            sm.wait(e, "c_ev", j + 1)
            inst = e.dma_start(out=xo_r[:, j % 8, (j // 8) * 512:(j // 8 + 1) * 512],
                               in_=xo[:, j % 2, :])
            sm.inc(inst, f"c_o_{j % 2}", 16)
        for k in range(64):
            ec, tt = divmod(k, 8)
            if k >= 2:
                sm.wait(e, "c_ev", k - 1)
            inst = e.dma_start(out=xpc[:, k % 2, :],
                               in_=x_r[:, tt, ec * 512:(ec + 1) * 512])
            sm.inc(inst, f"c_x_{k % 2}", 16)
            if k >= 2:
                store(k - 2)
        store(62)
        store(63)

    if fused_ot is not None:
        @blk.gpsimd
        def _(e):
            import bass_rust as _br
            otA_aps, ixs, ix_hdl, ix_val = fused_ot
            e.wait_ge(ix_hdl, ix_val)
            oflats = [a_.rearrange("r t -> (r t)").rearrange("(r o) -> r o", o=1)
                      for a_ in otA_aps]
            for n, (g, half, a) in enumerate(c_gather_specs()):
                inst = e.indirect_dma_start(
                    otb[:, g * 4 + a, half * 512:(half + 1) * 512], None,
                    oflats[g % 2],
                    _br.IndirectOffsetOnAxis(ixs[:, n:n + 1], 0))
                sm.inc(inst, "c_ot", 16)

    @blk.gpsimd
    def _(e):
        for ec in range(8):
            if ec >= 2:
                sm.wait(e, "c_mm", 8 * (ec - 1))
            inst = e.dma_start(out=wbuf[:, ec % 2, :, :], in_=wo[ec, :, :, :])
            sm.inc(inst, f"c_w_{ec % 2}", 16)

    @blk.tensor
    def _(e):
        sm.wait(e, "c_ot", 16 if fused_ot is None else 16 * 64)
        for k in range(64):
            ec, tt = divmod(k, 8)
            if tt == 0:
                sm.wait(e, f"c_w_{ec % 2}", 16 * (ec // 2 + 1))
            pb = pp[k % 2]
            if k >= 2:
                sm.wait(e, "c_ev", k - 1)
            for dm in range(32):
                inst = e.matmul(pb[:, :], otb[:, dm, tt * 128:(tt + 1) * 128],
                                wbuf[:, ec % 2, dm, :],
                                start=(dm == 0), stop=(dm == 31))
            sm.inc(inst, "c_mm", 1)

    @blk.vector
    def _(e):
        for k in range(64):
            sm.wait(e, "c_mm", k + 1)
            sm.wait(e, f"c_x_{k % 2}", 16 * (k // 2 + 1))
            if k >= 2:
                sm.wait(e, f"c_o_{k % 2}", 16 * (k // 2))
            inst = e.tensor_add(xo[:, k % 2, :], pp[k % 2][:, :], xpc[:, k % 2, :])
            sm.inc(inst, "c_ev", 1)

    @blk.sync
    def _(e):
        sm.wait(e, "c_o_0", 16 * 32)
        sm.wait(e, "c_o_1", 16 * 32)


def build_nc_C():
    nc = bass.Bass("TRN2", target_bir_lowering=False)
    ot_in = nc.dram_tensor("ot_in", [4096, 1024], BF16, kind="ExternalInput").ap()
    wo = nc.dram_tensor("wo", [4096, 4096], F32, kind="ExternalInput").ap()
    x_in = nc.dram_tensor("x_in", [1024, 4096], F32, kind="ExternalInput").ap()
    xo_out = nc.dram_tensor("xo_out", [1024, 4096], F32, kind="ExternalOutput").ap()
    with contextlib.ExitStack() as stack:
        sm = Sems(nc, stack)
        blk = stack.enter_context(nc.Block())
        emit_launch_C(nc, blk, stack, sm, ot_in, wo, x_in, xo_out)
    return nc
RMS_EPS = 1.1920929e-07


def emit_launch_D(nc, blk, stack, sm, x_in, wcat, b24, gate_in, wbr, ident,
                  xf_out, tag="d"):
    sb = lambda name, shape, dt: stack.enter_context(nc.sbuf_tensor(tag + name, shape, dt))
    ps = lambda name, shape, dt: stack.enter_context(nc.psum_tensor(tag + name, shape, dt))

    xt = sb("xt", [128, 2, 4096], F32)
    scratch = sb("scratch", [128, 2, 4096], F32)
    xn_tok = sb("xn_tok", [128, 8, 4096], BF16)
    sums = sb("sums", [128, 8], F32)
    redt = sb("redt", [128, 1], F32)
    msqt = sb("msqt", [128, 1], F32)
    sumsq = sb("sumsq", [128, 8], F32)
    mvec = sb("mvec", [128, 8], F32)
    varv = sb("varv", [128, 8], F32)
    rstd = sb("rstd", [128, 8], F32)
    rsum = sb("rsum", [128, 8], F32)
    r2 = sb("r2", [128, 8], F32)
    epsb = sb("epsb", [128, 1], F32)
    eps2b = sb("eps2b", [128, 1], F32)
    xnT_g = sb("xnT_g", [128, 2, 8, 128], BF16)
    wcatb = sb("wcatb", [128, 32, 24], BF16)
    b24b = sb("b24b", [128, 24], F32)
    gateb = sb("gateb", [128, 1], F32)
    zer5 = sb("zer5", [128, 512], BF16)
    lgf = sb("lgf", [128, 1024], BF16)
    Lt = sb("Lt", [128, 8, 24], F32)
    LL = sb("LL", [128, 8, 24], F32)
    EZ = sb("EZ", [128, 8, 16], F32)
    EZT = sb("EZT", [128, 8, 16], F32)
    e44 = sb("e44", [128, 8, 16], F32)
    s4u = sb("s4u", [128, 8, 4], F32)
    s4v = sb("s4v", [128, 8, 4], F32)
    ru = sb("ru", [128, 8, 4], F32)
    cv = sb("cv", [128, 8, 4], F32)
    H = sb("H", [128, 8, 16], F32)
    Hpre = sb("Hpre", [128, 8, 4], F32)
    Hpost = sb("Hpost", [128, 8, 4], F32)
    sred = sb("sred", [128, 8], F32)
    srec = sb("srec", [128, 8], F32)
    g_s = sb("g_s", [128, 1], F32)
    g_t = sb("g_t", [128, 1], F32)
    omg = sb("omg", [128, 1], F32)
    xp = sb("xp", [128, 8, 1024], BF16)
    xpT = sb("xpT", [128, 8, 1024], BF16)
    wbrb = sb("wbrb", [128, 8, 1024], BF16)
    y_f = sb("y_f", [128, 8, 1024], BF16)
    y_tok = xp
    idb = sb("idb", [128, 128], BF16)

    ptr = [ps("ptr0", [128, 1024], BF16), ps("ptr1", [128, 1024], BF16)]
    pl = ps("pl", [24, 1024], F32)
    py = [ps("py0", [128, 512], F32), ps("py1", [128, 512], F32)]

    x_r = x_in.rearrange("(t p) d -> p t d", p=128)
    xf_r = xf_out.rearrange("(t p) d -> p t d", p=128)

    def v44(t):
        return t[:, :, :].rearrange("p s (i j) -> p s i j", j=4)

    def b_2(ap3):
        return ap3.rearrange("p s (o j) -> p s o j", o=1).broadcast_to((128, 8, 4, 4))

    def b_3(ap3):
        return ap3.rearrange("p s (i o) -> p s i o", o=1).broadcast_to((128, 8, 4, 4))

    @blk.gpsimd
    def _(e):
        sm.inc(e.dma_start(out=wcatb[:, :, :], in_=wcat[:, :, :]), "d_wc", 16)
        sm.inc(e.dma_start(out=b24b[:, :], in_=b24[:, :]), "d_wc", 16)
        sm.inc(e.dma_start(out=gateb[:, :], in_=gate_in[:, :]), "d_wc", 16)
        sm.inc(e.dma_start(out=idb[:, :], in_=ident[:, :]), "d_wc", 16)
        sm.inc(e.dma_start(out=wbrb[:, :, :], in_=wbr[:, :, :]), "d_wbr", 16)

    @blk.sync
    def _(e):
        for i in range(8):
            if i >= 2:
                sm.wait(e, "d_red", i - 1)
                sm.wait(e, "d_ssq", i - 1)
                sm.wait(e, "d_xn", i - 1)
            sm.inc(e.dma_start(out=xt[:, i % 2, 0:2048], in_=x_r[:, i, 0:2048]),
                   f"d_x1_{i % 2}", 16)

    @blk.vector
    def _(e):
        sm.inc(e.memset(epsb[:, :], LN_EPS), "d_eps", 1)
        sm.inc(e.memset(eps2b[:, :], RMS_EPS), "d_eps", 1)
        sm.inc(e.memset(lgf[:, :], 0.0), "d_eps", 1)
        sm.inc(e.memset(zer5[:, :], 0.0), "d_eps", 1)
        sm.inc(e.memset(cv[:, :, :], 1.0), "d_eps", 1)

        def cast(i):
            sm.wait(e, "d_rstd", i + 1)
            sm.wait(e, f"d_xs_{i % 2}", 16 * (i // 2 + 1))
            inst = e.tensor_scalar(xn_tok[:, i, :], xt[:, i % 2, :],
                                   mvec[:, i:i + 1], rstd[:, i:i + 1],
                                   ALU.subtract, ALU.mult)
            sm.inc(inst, "d_xn", 1)

        for i in range(8):
            sm.wait(e, f"d_x1_{i % 2}", 16 * (i // 2 + 1))
            sm.wait(e, f"d_xs_{i % 2}", 16 * (i // 2 + 1))
            if i >= 1:
                sm.wait(e, "d_cp", 3 * i)
            inst = e.reduce_sum(redt[:, :], xt[:, i % 2, :], AX.X)
            sm.inc(inst, "d_red", 1)
            sm.wait(e, "d_red", i + 1)
            sm.inc(e.tensor_copy(sums[:, i:i + 1], redt[:, :]), "d_cp", 1)
            sm.wait(e, "d_cp", 3 * i + 1)
            sm.inc(e.tensor_scalar_mul(mvec[:, i:i + 1], sums[:, i:i + 1],
                                       1.0 / 4096.0), "d_cp", 1)
            sm.wait(e, "d_cp", 3 * i + 2)
            sm.inc(e.tensor_mul(msqt[:, :], mvec[:, i:i + 1], mvec[:, i:i + 1]),
                   "d_cp", 1)
            if i >= 1:
                cast(i - 1)
            sm.wait(e, "d_cp", 3 * i + 3)
            sm.wait(e, "d_ssq", i + 1)
            inst = e.scalar_tensor_tensor(varv[:, i:i + 1], sumsq[:, i:i + 1],
                                          1.0 / 4096.0, msqt[:, :],
                                          ALU.mult, ALU.subtract)
            sm.inc(inst, "d_var", 1)
        cast(7)

    @blk.scalar
    def _(e):
        def evac(g_idx):
            par = g_idx % 2
            sm.wait(e, "d_tr", 8 * (g_idx + 1))
            if g_idx >= 2:
                sm.wait(e, "d_lg", 2 + 8 * (g_idx - 1))
            inst = e.copy(xnT_g[:, par, :, :],
                          ptr[par][:, :].rearrange("p (a b) -> p a b", b=128))
            sm.inc(inst, "d_tev", 1)

        for i in range(2):
            inst = e.dma_start(out=xt[:, i % 2, 2048:4096],
                               in_=x_r[:, i, 2048:4096])
            sm.inc(inst, f"d_xs_{i % 2}", 16)
        sm.wait(e, "d_eps", 5)
        for i in range(8):
            sm.wait(e, f"d_x1_{i % 2}", 16 * (i // 2 + 1))
            sm.wait(e, f"d_xs_{i % 2}", 16 * (i // 2 + 1))
            if i >= 1:
                sm.wait(e, "d_ssq", i)
            inst = e.activation(scratch[:, 0, :], xt[:, i % 2, :], AF.Square,
                                accum_out=sumsq[:, i:i + 1])
            sm.inc(inst, "d_ssq", 1)
            sm.wait(e, "d_var", i + 1)
            sm.inc(e.activation(varv[:, i:i + 1], varv[:, i:i + 1], AF.Ln,
                                bias=epsb[:, 0:1]), "d_ln", 1)
            sm.wait(e, "d_ln", i + 1)
            sm.inc(e.activation(rstd[:, i:i + 1], varv[:, i:i + 1],
                                AF.Exp, scale=-0.5), "d_rstd", 1)
            if i >= 2:
                for g in range(4):
                    evac(4 * (i - 2) + g)
            if i + 2 < 8:
                sm.wait(e, "d_red", i + 1)
                sm.wait(e, "d_xn", i + 1)
                sm.wait(e, "d_ssq", i + 1)
                inst = e.dma_start(out=xt[:, i % 2, 2048:4096],
                                   in_=x_r[:, i + 2, 2048:4096])
                sm.inc(inst, f"d_xs_{i % 2}", 16)
        for g_idx in range(24, 32):
            evac(g_idx)

    @blk.scalar
    def _(e):
        for i in range(8):
            sm.wait(e, "d_xn", i + 1)
            if i == 0:
                sm.wait(e, "d_ssq", 8)
            else:
                sm.wait(e, "d_rsq", i)
            inst = e.activation(scratch[:, 0, :], xn_tok[:, i, :], AF.Square,
                                accum_out=rsum[:, i:i + 1])
            sm.inc(inst, "d_rsq", 1)
        sm.wait(e, "d_rsq", 8)
        sm.wait(e, "d_eps", 5)
        sm.inc(e.activation(rsum[:, :], rsum[:, :], AF.Ln, scale=1.0 / 4096.0,
                            bias=eps2b[:, 0:1]), "d_rln", 1)
        sm.wait(e, "d_rln", 1)
        sm.inc(e.activation(r2[:, :], rsum[:, :], AF.Exp, scale=-0.5), "d_r2", 1)

    @blk.tensor
    def _(e):
        sm.wait(e, "d_wc", 64)
        sm.wait(e, "d_eps", 5)
        for bank in range(2):
            inst = e.matmul(pl[:, bank * 512:(bank + 1) * 512], zer5[:, 0:24],
                            zer5[:, :], start=True, stop=False)
            sm.inc(inst, "d_lg", 1)
        g_idx = 0
        for i in range(8):
            for grp in range(4):
                par = g_idx % 2
                pbank = ptr[par]
                sm.wait(e, "d_xn", i + 1)
                if g_idx >= 2:
                    sm.wait(e, "d_tev", g_idx - 1)
                for k in range(8):
                    dt_ = grp * 8 + k
                    inst = e.transpose(pbank[:, k * 128:(k + 1) * 128],
                                       xn_tok[:, i, dt_ * 128:(dt_ + 1) * 128],
                                       idb[:, :])
                    sm.inc(inst, "d_tr", 1)
                sm.wait(e, "d_tev", g_idx + 1)
                for k in range(8):
                    dm = grp * 8 + k
                    stop = (grp == 3 and k == 7 and i in (3, 7))
                    inst = e.matmul(pl[:, i * 128:(i + 1) * 128], wcatb[:, dm, :],
                                    xnT_g[:, par, k, :], start=False, stop=stop)
                    sm.inc(inst, "d_lg", 1)
                g_idx += 1

    @blk.scalar
    def _(e):
        sm.wait(e, "d_lg", 2 + 256)
        inst = e.copy(lgf[0:24, :], pl[:, :])
        sm.inc(inst, "d_lgf", 1)

    @blk.tensor
    def _(e):
        sm.wait(e, "d_lgf", 1)
        sm.wait(e, "d_tev", 32)
        for i in range(8):
            inst = e.transpose(ptr[i % 2][:, (i // 2) * 128:(i // 2 + 1) * 128],
                               lgf[:, i * 128:(i + 1) * 128], idb[:, :])
            sm.inc(inst, "d_ltr", 1)

    @blk.scalar
    def _(e):
        for i in range(8):
            sm.wait(e, "d_ltr", i + 1)
            inst = e.copy(Lt[:, i, :], ptr[i % 2][:, (i // 2) * 128:(i // 2) * 128 + 24])
            sm.inc(inst, "d_lte", 1)
        sm.wait(e, "d_LL", 8)
        sm.inc(e.activation(EZ[:, :, :], LL[:, :, 0:16], AF.Exp, scale=10.0), "d_EZ", 1)
        sm.wait(e, "d_wc", 64)
        sm.inc(e.activation(g_t[:, :], gateb[:, :], AF.Exp, scale=-1.0), "d_gate", 1)
        sm.inc(e.activation(Hpre[:, :, :], LL[:, :, 16:20], AF.Exp), "d_sfe", 1)
        sm.inc(e.activation(Hpost[:, :, :], LL[:, :, 20:24], AF.Exp), "d_sfe", 1)

    @blk.vector
    def _(e):
        sm.wait(e, "d_lte", 8)
        sm.wait(e, "d_r2", 1)
        sm.wait(e, "d_wc", 64)
        for i in range(8):
            inst = e.scalar_tensor_tensor(LL[:, i, :], Lt[:, i, :], r2[:, i:i + 1],
                                          b24b[:, :], ALU.mult, ALU.add)
            sm.inc(inst, "d_LL", 1)
        sm.wait(e, "d_EZ", 1)
        inst = e.tensor_copy(v44(EZT),
                             v44(EZ).rearrange("p s i j -> p s j i"))
        sm.inc(inst, "d_sk", 1)
        sm.wait(e, "d_sk", 1)
        vc = 1
        for it in range(50):
            inst = e.tensor_mul(v44(e44), v44(EZ), b_2(cv))
            sm.inc(inst, "d_sk", 1); vc += 1
            sm.wait(e, "d_sk", vc)
            inst = e.reduce_sum(s4u[:, :, :], v44(e44), AX.X)
            sm.inc(inst, "d_sk", 1); vc += 1
            sm.wait(e, "d_sk", vc)
            inst = e.reciprocal(ru[:, :, :], s4u[:, :, :])
            sm.inc(inst, "d_sk", 1); vc += 1
            sm.wait(e, "d_sk", vc)
            inst = e.tensor_mul(v44(e44), v44(EZT), b_2(ru))
            sm.inc(inst, "d_sk", 1); vc += 1
            sm.wait(e, "d_sk", vc)
            inst = e.reduce_sum(s4v[:, :, :], v44(e44), AX.X)
            sm.inc(inst, "d_sk", 1); vc += 1
            sm.wait(e, "d_sk", vc)
            inst = e.reciprocal(cv[:, :, :], s4v[:, :, :])
            sm.inc(inst, "d_sk", 1); vc += 1
            sm.wait(e, "d_sk", vc)
        inst = e.tensor_mul(v44(H), v44(EZ), b_3(ru))
        sm.inc(inst, "d_sk", 1); vc += 1
        sm.wait(e, "d_sk", vc)
        inst = e.tensor_mul(v44(H), v44(H), b_2(cv))
        sm.inc(inst, "d_sk", 1); vc += 1
        sm.wait(e, "d_sk", vc)
        sm.wait(e, "d_gate", 1)
        sm.inc(e.tensor_scalar_add(g_t[:, :], g_t[:, :], 1.0), "d_g", 1)
        sm.wait(e, "d_g", 1)
        sm.inc(e.reciprocal(g_s[:, :], g_t[:, :]), "d_g", 1)
        sm.wait(e, "d_g", 2)
        sm.inc(e.tensor_scalar(omg[:, :], g_s[:, :], -1.0, 1.0, ALU.mult, ALU.add),
               "d_g", 1)
        sm.wait(e, "d_g", 3)
        sm.inc(e.tensor_scalar_mul(H[:, :, :], H[:, :, :], g_s[:, 0:1]), "d_H", 1)
        sm.wait(e, "d_H", 1)
        for ii in range(4):
            inst = e.tensor_scalar_add(H[:, :, ii * 5:ii * 5 + 1],
                                       H[:, :, ii * 5:ii * 5 + 1], omg[:, 0:1])
            sm.inc(inst, "d_H", 1)
            sm.wait(e, "d_H", 2 + ii)
        for n_, dst in ((1, Hpre), (2, Hpost)):
            sm.wait(e, "d_sfe", n_)
            inst = e.reduce_sum(sred[:, :], dst[:, :, :], AX.X)
            sm.inc(inst, "d_sf", 1)
            sm.wait(e, "d_sf", sm.val("d_sf"))
            inst = e.reciprocal(srec[:, :], sred[:, :])
            sm.inc(inst, "d_sf", 1)
            sm.wait(e, "d_sf", sm.val("d_sf"))
            inst = e.tensor_mul(dst[:, :, :], dst[:, :, :],
                                srec[:, :].rearrange("p (s o) -> p s o", o=1)
                                          .broadcast_to((128, 8, 4)))
            sm.inc(inst, "d_sf", 1)
            sm.wait(e, "d_sf", sm.val("d_sf"))

    @blk.vector
    def _(e):
        sm.wait(e, "d_sf", 3)
        for i in range(8):
            for n in range(4):
                if n == 0:
                    inst = e.tensor_scalar_mul(xp[:, i, :],
                                               xn_tok[:, i, 0:1024],
                                               Hpre[:, i, 0:1])
                else:
                    inst = e.scalar_tensor_tensor(xp[:, i, :],
                                                  xn_tok[:, i, n * 1024:(n + 1) * 1024],
                                                  Hpre[:, i, n:n + 1],
                                                  xp[:, i, :], ALU.mult, ALU.add)
                sm.inc(inst, "d_xp", 1)
                sm.wait(e, "d_xp", sm.val("d_xp"))

    @blk.tensor
    def _(e):
        sm.wait(e, "d_lte", 8)
        for i in range(8):
            par = i % 2
            sm.wait(e, "d_xp", 4 * (i + 1))
            if i >= 2:
                sm.wait(e, "d_xpte", i - 1)
            for ct in range(8):
                inst = e.transpose(ptr[par][:, ct * 128:(ct + 1) * 128],
                                   xp[:, i, ct * 128:(ct + 1) * 128], idb[:, :])
                sm.inc(inst, "d_xpt", 1)

    @blk.scalar
    def _(e):
        for i in range(8):
            par = i % 2
            sm.wait(e, "d_xpt", 8 * (i + 1))
            inst = e.copy(xpT[:, :, i * 128:(i + 1) * 128],
                          ptr[par][:, :].rearrange("p (a b) -> p a b", b=128))
            sm.inc(inst, "d_xpte", 1)

    @blk.tensor
    def _(e):
        sm.wait(e, "d_wbr", 16)
        sm.wait(e, "d_xpte", 8)
        k = 0
        for ct in range(8):
            for tc in range(2):
                pb = py[k % 2]
                if k >= 2:
                    sm.wait(e, "d_yev", k - 1)
                for cdm in range(8):
                    inst = e.matmul(pb[:, :], wbrb[:, cdm, ct * 128:(ct + 1) * 128],
                                    xpT[:, cdm, tc * 512:(tc + 1) * 512],
                                    start=(cdm == 0), stop=(cdm == 7))
                sm.inc(inst, "d_ymm", 1)
                k += 1

    @blk.scalar
    def _(e):
        for k in range(16):
            ct, tc = divmod(k, 2)
            sm.wait(e, "d_ymm", k + 1)
            inst = e.copy(y_f[:, ct, tc * 512:(tc + 1) * 512], py[k % 2][:, :])
            sm.inc(inst, "d_yev", 1)

    @blk.tensor
    def _(e):
        sm.wait(e, "d_yev", 16)
        sm.wait(e, "d_xpte", 8)
        for i in range(8):
            par = i % 2
            if i >= 2:
                sm.wait(e, "d_yte", i - 1)
            for ct in range(8):
                inst = e.transpose(ptr[par][:, ct * 128:(ct + 1) * 128],
                                   y_f[:, ct, i * 128:(i + 1) * 128], idb[:, :])
                sm.inc(inst, "d_ytr", 1)

    @blk.scalar
    def _(e):
        sm.wait(e, "d_xpt", 64)
        for i in range(8):
            par = i % 2
            sm.wait(e, "d_ytr", 8 * (i + 1))
            inst = e.copy(y_tok[:, i, :],
                          ptr[par][:, :].rearrange("p (a b) -> p a b", b=128))
            sm.inc(inst, "d_yte", 1)

    @blk.sync
    def _(e):
        def store(j):
            sm.wait(e, "d_mx", j + 1)
            inst = e.dma_start(out=xf_r[:, j, :], in_=scratch[:, j % 2, :])
            sm.inc(inst, f"d_out_{j % 2}", 16)
        for i in range(8):
            if i < 2:
                sm.wait(e, "d_xn", 8)
            else:
                sm.wait(e, "d_mx", i - 1)
            sm.inc(e.dma_start(out=xt[:, i % 2, :], in_=x_r[:, i, :]), f"d_x3_{i % 2}", 16)
            if i >= 2:
                store(i - 2)
        store(6)
        store(7)
        sm.wait(e, "d_out_0", 64)
        sm.wait(e, "d_out_1", 64)

    @blk.vector
    def _(e):
        sm.wait(e, "d_H", 5)
        sm.wait(e, "d_sf", 6)
        sm.wait(e, "d_yte", 8)
        sm.wait(e, "d_rsq", 8)
        for i in range(8):
            par = i % 2
            acc = scratch[:, par, :]
            sm.wait(e, f"d_x3_{par}", 16 * (i // 2 + 1))
            if i >= 2:
                sm.wait(e, f"d_out_{par}", 16 * (i // 2))
            for ii in range(4):
                a_sl = acc.rearrange("p (n c) -> p n c", c=1024)[:, ii, :]
                for j in range(4):
                    if j == 0:
                        inst = e.tensor_scalar_mul(a_sl, xn_tok[:, i, 0:1024],
                                                   H[:, i, ii * 4:ii * 4 + 1])
                    else:
                        inst = e.scalar_tensor_tensor(
                            a_sl, xn_tok[:, i, j * 1024:(j + 1) * 1024],
                            H[:, i, ii * 4 + j:ii * 4 + j + 1],
                            a_sl, ALU.mult, ALU.add)
                    sm.inc(inst, "d_mv", 1)
                    sm.wait(e, "d_mv", sm.val("d_mv"))
                inst = e.scalar_tensor_tensor(a_sl, y_tok[:, i, :],
                                              Hpost[:, i, ii:ii + 1],
                                              a_sl, ALU.mult, ALU.add)
                sm.inc(inst, "d_mv", 1)
                sm.wait(e, "d_mv", sm.val("d_mv"))
            inst = e.tensor_add(acc, acc, xt[:, par, :])
            sm.inc(inst, "d_mx", 1)
            sm.wait(e, "d_mx", sm.val("d_mx"))


def build_nc_D():
    nc = bass.Bass("TRN2", target_bir_lowering=False)
    x_in = nc.dram_tensor("x_in", [NLOC, D], F32, kind="ExternalInput").ap()
    wcat = nc.dram_tensor("wcat", [128, 32, 24], F32, kind="ExternalInput").ap()
    b24 = nc.dram_tensor("b24", [128, 24], F32, kind="ExternalInput").ap()
    gate_in = nc.dram_tensor("gate_in", [128, 1], F32, kind="ExternalInput").ap()
    wbr = nc.dram_tensor("wbr", [128, 8, 1024], F32, kind="ExternalInput").ap()
    ident = nc.dram_tensor("ident", [128, 128], BF16, kind="ExternalInput").ap()
    xf_out = nc.dram_tensor("xf_out", [NLOC, D], F32, kind="ExternalOutput").ap()
    with contextlib.ExitStack() as stack:
        sm = Sems(nc, stack)
        blk = stack.enter_context(nc.Block())
        emit_launch_D(nc, blk, stack, sm, x_in, wcat, b24, gate_in, wbr, ident, xf_out)
    return nc


def emit_launch_H(nc, blk, stack, sm, x_in, wbase, bbase, wgt, bthr, acat, bcat,
                  selmat, ident, xf_out, tag="h"):
    sb = lambda name, shape, dt: stack.enter_context(nc.sbuf_tensor(tag + name, shape, dt))
    ps = lambda name, shape, dt: stack.enter_context(nc.psum_tensor(tag + name, shape, dt))

    xt = sb("xt", [128, 2, 4096], F32)
    sq = sb("sq", [128, 4096], F32)
    xn_t = sb("xn_t", [128, 2, 4096], BF16)
    xnT = sb("xnT", [128, 32, 1024], BF16)
    sums = sb("sums", [128, 8], F32)
    redt = sb("redt", [128, 1], F32)
    msqt = sb("msqt", [128, 1], F32)
    sumsq = sb("sumsq", [128, 8], F32)
    mvec = sb("mvec", [128, 8], F32)
    varv = sb("varv", [128, 8], F32)
    rstd = sb("rstd", [128, 8], F32)
    epsb = sb("epsb", [128, 1], F32)
    wgtb = sb("wgtb", [128, 32, 9], BF16)
    bthrn = sb("bthrn", [128, 1], F32)
    bthrb = sb("bthrb", [128, 1], F32)
    acatb = sb("acatb", [128, 32, 128], BF16)
    bcatb = sb("bcatb", [128, 4096], BF16)
    selb = sb("selb", [8, 128], BF16)
    onesc = sb("onesc", [1, 128], BF16)
    bbaseb = sb("bbaseb", [1, 4096], BF16)
    lgf = sb("lgf", [128, 1024], BF16)
    G = sb("G", [128, 8, 9], F32)
    thr = sb("thr", [128, 8], F32)
    sred = sb("sred", [128, 8], F32)
    srec = sb("srec", [128, 8], F32)
    gsm = sb("gsm", [128, 8, 8], F32)
    wts = sb("wts", [8, 1024], BF16)
    wtile = sb("wtile", [128, 128], BF16)
    w_exp = sb("w_exp", [128, 1024], BF16)
    hw = sb("hw", [128, 1024], BF16)
    wb = sb("wb", [128, 2, 32, 256], BF16)
    xpc = sb("xpc", [128, 2, 256], F32)
    xo = sb("xo", [128, 2, 256], F32)
    idb = sb("idb", [128, 128], BF16)

    ptr = [ps("ptr0", [128, 1024], BF16), ps("ptr1", [128, 1024], BF16)]
    plg = ps("plg", [9, 1024], F32)
    pb = [ps("pb0", [128, 512], F32), ps("pb1", [128, 512], F32)]

    x_r = x_in.rearrange("(t p) d -> p t d", p=128)
    xf_r = xf_out.rearrange("(t p) d -> p t d", p=128)

    @blk.gpsimd
    def _(e):
        sm.inc(e.dma_start(out=wgtb[:, :, :], in_=wgt[:, :, :]), "h_wc", 16)
        sm.inc(e.dma_start(out=bthrb[:, :], in_=bthr[:, :]), "h_wc", 16)
        sm.inc(e.dma_start(out=acatb[:, :, :], in_=acat[:, :, :]), "h_wc", 16)
        sm.inc(e.dma_start(out=bcatb[:, :], in_=bcat[:, :]), "h_wc", 16)
        sm.inc(e.dma_start(out=selb[:, :], in_=selmat[:, :]), "h_wc", 16)
        sm.inc(e.dma_start(out=bbaseb[:, :], in_=bbase[:, :]), "h_wc", 16)
        sm.inc(e.dma_start(out=idb[:, :], in_=ident[:, :]), "h_wc", 16)
        for oc in range(16):
            if oc >= 2:
                sm.wait(e, "h_bmm", 8 * (oc - 1))
            inst = e.dma_start(out=wb[:, oc % 2, :, :], in_=wbase[oc, :, :, :])
            sm.inc(inst, f"h_w_{oc % 2}", 16)

    @blk.sync
    def _(e):
        for i in range(8):
            if i >= 2:
                sm.wait(e, "h_red", i - 1)
                sm.wait(e, "h_ssq", i - 1)
                sm.wait(e, "h_xn", i - 1)
            sm.inc(e.dma_start(out=xt[:, i % 2, 0:2048], in_=x_r[:, i, 0:2048]),
                   f"h_x1_{i % 2}", 16)

    @blk.vector
    def _(e):
        sm.inc(e.memset(epsb[:, :], LN_EPS), "h_eps", 1)
        sm.inc(e.memset(lgf[:, :], 0.0), "h_eps", 1)
        sm.inc(e.memset(wtile[:, :], 0.0), "h_eps", 1)
        sm.inc(e.memset(onesc[:, :], 1.0), "h_eps", 1)

        def cast(i):
            if i >= 2:
                sm.wait(e, "h_tr", 32 * (i - 1))
            sm.wait(e, "h_rstd", i + 1)
            sm.wait(e, f"h_xs_{i % 2}", 16 * (i // 2 + 1))
            inst = e.tensor_scalar(xn_t[:, i % 2, :], xt[:, i % 2, :],
                                   mvec[:, i:i + 1], rstd[:, i:i + 1],
                                   ALU.subtract, ALU.mult)
            sm.inc(inst, "h_xn", 1)

        for i in range(8):
            sm.wait(e, f"h_x1_{i % 2}", 16 * (i // 2 + 1))
            sm.wait(e, f"h_xs_{i % 2}", 16 * (i // 2 + 1))
            if i >= 1:
                sm.wait(e, "h_cp", 3 * i)
            inst = e.reduce_sum(redt[:, :], xt[:, i % 2, :], AX.X)
            sm.inc(inst, "h_red", 1)
            sm.wait(e, "h_red", i + 1)
            sm.inc(e.tensor_copy(sums[:, i:i + 1], redt[:, :]), "h_cp", 1)
            sm.wait(e, "h_cp", 3 * i + 1)
            sm.inc(e.tensor_scalar_mul(mvec[:, i:i + 1], sums[:, i:i + 1],
                                       1.0 / 4096.0), "h_cp", 1)
            sm.wait(e, "h_cp", 3 * i + 2)
            sm.inc(e.tensor_mul(msqt[:, :], mvec[:, i:i + 1], mvec[:, i:i + 1]),
                   "h_cp", 1)
            if i >= 1:
                cast(i - 1)
            sm.wait(e, "h_cp", 3 * i + 3)
            sm.wait(e, "h_ssq", i + 1)
            inst = e.scalar_tensor_tensor(varv[:, i:i + 1], sumsq[:, i:i + 1],
                                          1.0 / 4096.0, msqt[:, :],
                                          ALU.mult, ALU.subtract)
            sm.inc(inst, "h_var", 1)
        cast(7)

    @blk.scalar
    def _(e):
        def evac(g_idx):
            i_, grp = divmod(g_idx, 4)
            par = g_idx % 2
            sm.wait(e, "h_tr", 8 * (g_idx + 1))
            inst = e.copy(xnT[:, grp * 8:(grp + 1) * 8, i_ * 128:(i_ + 1) * 128],
                          ptr[par][:, :].rearrange("p (a b) -> p a b", b=128))
            sm.inc(inst, "h_tev", 1)

        for i in range(2):
            inst = e.dma_start(out=xt[:, i % 2, 2048:4096],
                               in_=x_r[:, i, 2048:4096])
            sm.inc(inst, f"h_xs_{i % 2}", 16)
        sm.wait(e, "h_eps", 4)
        for i in range(8):
            sm.wait(e, f"h_x1_{i % 2}", 16 * (i // 2 + 1))
            sm.wait(e, f"h_xs_{i % 2}", 16 * (i // 2 + 1))
            if i >= 1:
                sm.wait(e, "h_ssq", i)
            inst = e.activation(sq[:, :], xt[:, i % 2, :], AF.Square,
                                accum_out=sumsq[:, i:i + 1])
            sm.inc(inst, "h_ssq", 1)
            sm.wait(e, "h_var", i + 1)
            sm.inc(e.activation(varv[:, i:i + 1], varv[:, i:i + 1], AF.Ln,
                                bias=epsb[:, 0:1]), "h_ln", 1)
            sm.wait(e, "h_ln", i + 1)
            sm.inc(e.activation(rstd[:, i:i + 1], varv[:, i:i + 1],
                                AF.Exp, scale=-0.5), "h_rstd", 1)
            if i >= 2:
                for g in range(4):
                    evac(4 * (i - 2) + g)
            if i + 2 < 8:
                sm.wait(e, "h_red", i + 1)
                sm.wait(e, "h_xn", i + 1)
                sm.wait(e, "h_ssq", i + 1)
                inst = e.dma_start(out=xt[:, i % 2, 2048:4096],
                                   in_=x_r[:, i + 2, 2048:4096])
                sm.inc(inst, f"h_xs_{i % 2}", 16)
        for g_idx in range(24, 32):
            evac(g_idx)

    @blk.tensor
    def _(e):
        sm.wait(e, "h_wc", 112)
        sm.wait(e, "h_eps", 4)
        for bank in range(2):
            inst = e.matmul(plg[:, bank * 512:(bank + 1) * 512], lgf[:, 0:9],
                            lgf[:, 0:512], start=True, stop=False)
            sm.inc(inst, "h_lg", 1)
        g_idx = 0
        for i in range(8):
            for grp in range(4):
                par = g_idx % 2
                pbank = ptr[par]
                sm.wait(e, "h_xn", i + 1)
                if g_idx >= 2:
                    sm.wait(e, "h_tev", g_idx - 1)
                for k in range(8):
                    dt_ = grp * 8 + k
                    inst = e.transpose(pbank[:, k * 128:(k + 1) * 128],
                                       xn_t[:, i % 2, dt_ * 128:(dt_ + 1) * 128],
                                       idb[:, :])
                    sm.inc(inst, "h_tr", 1)
                g_idx += 1
        sm.wait(e, "h_tev", 32)
        for i in range(8):
            for dm in range(32):
                stop = (dm == 31 and i in (3, 7))
                inst = e.matmul(plg[:, i * 128:(i + 1) * 128], wgtb[:, dm, :],
                                xnT[:, dm, i * 128:(i + 1) * 128],
                                start=False, stop=stop)
                sm.inc(inst, "h_lg", 1)

    @blk.scalar
    def _(e):
        sm.wait(e, "h_lg", 2 + 256)
        inst = e.copy(lgf[0:9, :], plg[:, :])
        sm.inc(inst, "h_lgf", 1)

    @blk.tensor
    def _(e):
        sm.wait(e, "h_lgf", 1)
        sm.wait(e, "h_tev", 32)
        for i in range(8):
            inst = e.transpose(ptr[i % 2][:, (i // 2) * 128:(i // 2 + 1) * 128],
                               lgf[:, i * 128:(i + 1) * 128], idb[:, :])
            sm.inc(inst, "h_ltr", 1)

    @blk.scalar
    def _(e):
        for i in range(8):
            sm.wait(e, "h_ltr", i + 1)
            inst = e.copy(G[:, i, :], ptr[i % 2][:, (i // 2) * 128:(i // 2) * 128 + 9])
            sm.inc(inst, "h_lte", 1)
        sm.wait(e, "h_lte", 8)
        sm.wait(e, "h_negb", 1)
        inst = e.activation(thr[:, :], G[:, :, 8:9].rearrange("p s o -> p (s o)"),
                            AF.Exp, scale=-1.0, bias=bthrn[:, 0:1])
        sm.inc(inst, "h_sg", 1)
        inst = e.activation(gsm[:, :, :], G[:, :, 0:8], AF.Exp)
        sm.inc(inst, "h_ge", 1)

    @blk.vector
    def _(e):
        sm.wait(e, "h_wc", 112)
        sm.inc(e.tensor_scalar_mul(bthrn[:, :], bthrb[:, :], -1.0), "h_negb", 1)
        sm.wait(e, "h_sg", 1)
        sm.inc(e.tensor_scalar_add(thr[:, :], thr[:, :], 1.0), "h_th", 1)
        sm.wait(e, "h_th", 1)
        sm.inc(e.reciprocal(sred[:, :], thr[:, :]), "h_th", 1)
        sm.wait(e, "h_th", 2)
        sm.inc(e.tensor_scalar_mul(thr[:, :], sred[:, :], 0.125), "h_th", 1)
        sm.wait(e, "h_ge", 1)
        sm.wait(e, "h_th", 3)
        sm.inc(e.reduce_sum(sred[:, :], gsm[:, :, :], AX.X), "h_gs", 1)
        sm.wait(e, "h_gs", 1)
        sm.inc(e.reciprocal(srec[:, :], sred[:, :]), "h_gs", 1)
        sm.wait(e, "h_gs", 2)
        inst = e.tensor_mul(gsm[:, :, :], gsm[:, :, :],
                            srec[:, :].rearrange("p (s o) -> p s o", o=1)
                                      .broadcast_to((128, 8, 8)))
        sm.inc(inst, "h_gs", 1)
        sm.wait(e, "h_gs", 3)
        sm.wait(e, "h_th", 3)
        inst = e.tensor_sub(gsm[:, :, :], gsm[:, :, :],
                            thr[:, :].rearrange("p (s o) -> p s o", o=1)
                                     .broadcast_to((128, 8, 8)))
        sm.inc(inst, "h_gs", 1)
        sm.wait(e, "h_gs", 4)
        sm.inc(e.tensor_relu(gsm[:, :, :], gsm[:, :, :]), "h_gs", 1)
        sm.wait(e, "h_gs", 5)
        sm.inc(e.reduce_sum(sred[:, :], gsm[:, :, :], AX.X), "h_gs", 1)
        sm.wait(e, "h_gs", 6)
        sm.inc(e.tensor_single_scalar(srec[:, :], sred[:, :], 0.0, ALU.is_equal),
               "h_gs", 1)
        sm.wait(e, "h_gs", 7)
        sm.inc(e.tensor_add(sred[:, :], sred[:, :], srec[:, :]), "h_gs", 1)
        sm.wait(e, "h_gs", 8)
        sm.inc(e.reciprocal(srec[:, :], sred[:, :]), "h_gs", 1)
        sm.wait(e, "h_gs", 9)
        sm.inc(e.tensor_scalar_mul(srec[:, :], srec[:, :], 2.0), "h_gs", 1)
        sm.wait(e, "h_gs", 10)
        inst = e.tensor_mul(gsm[:, :, :], gsm[:, :, :],
                            srec[:, :].rearrange("p (s o) -> p s o", o=1)
                                      .broadcast_to((128, 8, 8)))
        sm.inc(inst, "h_gs", 1)

    @blk.vector
    def _(e):
        sm.wait(e, "h_gs", 11)
        for i in range(8):
            if i >= 1:
                sm.wait(e, "h_wtr", i)
            inst = e.tensor_copy(wtile[:, 0:8],
                                 gsm[:, i, :].rearrange("p (o c) -> p (o c)", o=1)
                                 if False else gsm[:, i, :])
            sm.inc(inst, "h_wt", 1)
            sm.wait(e, "h_wt", i + 1)

    @blk.tensor
    def _(e):
        sm.wait(e, "h_ltr", 8)
        sm.wait(e, "h_lte", 8)
        for i in range(8):
            sm.wait(e, "h_wt", i + 1)
            inst = e.transpose(ptr[i % 2][:, (i // 2) * 128:(i // 2 + 1) * 128],
                               wtile[:, :], idb[:, :])
            sm.inc(inst, "h_wtr", 1)

    @blk.scalar
    def _(e):
        for i in range(8):
            sm.wait(e, "h_wtr", i + 1)
            inst = e.copy(wts[:, i * 128:(i + 1) * 128],
                          ptr[i % 2][:, (i // 2) * 128:(i // 2) * 128 + 8]
                          .rearrange("p c -> p c") if False else
                          ptr[i % 2][0:8, (i // 2) * 128:(i // 2 + 1) * 128])
            sm.inc(inst, "h_wte", 1)

    @blk.tensor
    def _(e):
        sm.wait(e, "h_wte", 8)
        for tc in range(2):
            inst = e.matmul(pb[tc][:, :], selb[:, :], wts[:, tc * 512:(tc + 1) * 512],
                            start=True, stop=True)
            sm.inc(inst, "h_we", 1)
        sm.wait(e, "h_wee", 2)
        for tc in range(2):
            for dm in range(32):
                inst = e.matmul(pb[tc][:, :], acatb[:, dm, :],
                                xnT[:, dm, tc * 512:(tc + 1) * 512],
                                start=(dm == 0), stop=(dm == 31))
            sm.inc(inst, "h_hmm", 1)

    @blk.scalar
    def _(e):
        for tc in range(2):
            sm.wait(e, "h_we", tc + 1)
            inst = e.copy(w_exp[:, tc * 512:(tc + 1) * 512], pb[tc][:, :])
            sm.inc(inst, "h_wee", 1)

    @blk.vector
    def _(e):
        for tc in range(2):
            sm.wait(e, "h_hmm", tc + 1)
            inst = e.tensor_mul(hw[:, tc * 512:(tc + 1) * 512], pb[tc][:, :],
                                w_exp[:, tc * 512:(tc + 1) * 512])
            sm.inc(inst, "h_hw", 1)

    @blk.tensor
    def _(e):
        sm.wait(e, "h_hw", 2)
        sm.wait(e, "h_eps", 4)
        for k in range(128):
            oc, tt = divmod(k, 8)
            if tt == 0:
                sm.wait(e, f"h_w_{oc % 2}", 16 * (oc // 2 + 1))
            pbk = pb[k % 2]
            if k >= 2:
                sm.wait(e, "h_ev", k - 1)
            for dm in range(32):
                e.matmul(pbk[:, 0:256], xnT[:, dm, tt * 128:(tt + 1) * 128],
                         wb[:, oc % 2, dm, :], start=(dm == 0), stop=False)
            e.matmul(pbk[:, 0:256], onesc[:, :], bbaseb[:, oc * 256:(oc + 1) * 256],
                     start=False, stop=False)
            inst = e.matmul(pbk[:, 0:256], hw[:, tt * 128:(tt + 1) * 128],
                            bcatb[:, oc * 256:(oc + 1) * 256],
                            start=False, stop=True)
            sm.inc(inst, "h_bmm", 1)

    @blk.sync
    def _(e):
        def store(j):
            sm.wait(e, "h_ev", j + 1)
            inst = e.dma_start(out=xf_r[:, j % 8, (j // 8) * 256:(j // 8 + 1) * 256],
                               in_=xo[:, j % 2, :])
            sm.inc(inst, f"h_o_{j % 2}", 16)
        for k in range(128):
            oc, tt = divmod(k, 8)
            if k < 2:
                sm.wait(e, "h_xn", 8)
            else:
                sm.wait(e, "h_ev", k - 1)
            inst = e.dma_start(out=xpc[:, k % 2, :],
                               in_=x_r[:, tt, oc * 256:(oc + 1) * 256])
            sm.inc(inst, f"h_xp_{k % 2}", 16)
            if k >= 2:
                store(k - 2)
        store(126)
        store(127)
        sm.wait(e, "h_o_0", 16 * 64)
        sm.wait(e, "h_o_1", 16 * 64)

    @blk.vector
    def _(e):
        for k in range(128):
            sm.wait(e, "h_bmm", k + 1)
            sm.wait(e, f"h_xp_{k % 2}", 16 * (k // 2 + 1))
            if k >= 2:
                sm.wait(e, f"h_o_{k % 2}", 16 * (k // 2))
            inst = e.tensor_add(xo[:, k % 2, :], pb[k % 2][:, 0:256], xpc[:, k % 2, :])
            sm.inc(inst, "h_ev", 1)


def build_nc_H():
    nc = bass.Bass("TRN2", target_bir_lowering=False)
    x_in = nc.dram_tensor("x_in", [NLOC, D], F32, kind="ExternalInput").ap()
    wbase = nc.dram_tensor("wbase", [16, 128, 32, 256], F32, kind="ExternalInput").ap()
    bbase = nc.dram_tensor("bbase", [1, D], F32, kind="ExternalInput").ap()
    wgt = nc.dram_tensor("wgt", [128, 32, 9], F32, kind="ExternalInput").ap()
    bthr = nc.dram_tensor("bthr", [128, 1], F32, kind="ExternalInput").ap()
    acat = nc.dram_tensor("acat", [128, 32, 128], F32, kind="ExternalInput").ap()
    bcat = nc.dram_tensor("bcat", [128, D], F32, kind="ExternalInput").ap()
    selmat = nc.dram_tensor("selmat", [8, 128], BF16, kind="ExternalInput").ap()
    ident = nc.dram_tensor("ident", [128, 128], BF16, kind="ExternalInput").ap()
    xf_out = nc.dram_tensor("xf_out", [NLOC, D], F32, kind="ExternalOutput").ap()
    with contextlib.ExitStack() as stack:
        sm = Sems(nc, stack)
        blk = stack.enter_context(nc.Block())
        emit_launch_H(nc, blk, stack, sm, x_in, wbase, bbase, wgt, bthr, acat,
                      bcat, selmat, ident, xf_out)
    return nc



def build_nc_fused():
    nc = bass.Bass("TRN2", target_bir_lowering=False)
    ap_ = lambda name, shape, dt, kind: nc.dram_tensor(name, shape, dt, kind=kind).ap()
    EI, EO, IN = "ExternalInput", "ExternalOutput", "Internal"

    x_in = ap_("x_in", [NLOC, D], F32, EI)
    wq0 = ap_("wq0", [16, 128, 32, 256], F32, EI)
    wk0 = ap_("wk0", [4, 128, 32, 256], F32, EI)
    wv0 = ap_("wv0", [4, 128, 32, 256], F32, EI)
    wo0 = ap_("wo0", [8, 128, 32, 512], F32, EI)
    wq1 = ap_("wq1", [16, 128, 32, 256], F32, EI)
    wk1 = ap_("wk1", [4, 128, 32, 256], F32, EI)
    wv1 = ap_("wv1", [4, 128, 32, 256], F32, EI)
    wo1 = ap_("wo1", [8, 128, 32, 512], F32, EI)
    cost = ap_("cost", [64, NLOC], F32, EI)
    sint = ap_("sint", [64, NLOC], F32, EI)
    ident = ap_("ident", [128, 128], BF16, EI)
    trimask = ap_("trimask", [128, 128], BF16, EI)
    wcat = ap_("wcat", [128, 32, 24], F32, EI)
    b24 = ap_("b24", [128, 24], F32, EI)
    gate_in = ap_("gate_in", [128, 1], F32, EI)
    wbr = ap_("wbr", [128, 8, 1024], F32, EI)
    wbase = ap_("wbase", [16, 128, 32, 256], F32, EI)
    bbase = ap_("bbase", [1, D], F32, EI)
    wgt = ap_("wgt", [128, 32, 9], F32, EI)
    bthr = ap_("bthr", [128, 1], F32, EI)
    acat = ap_("acat", [128, 32, 128], F32, EI)
    bcat = ap_("bcat", [128, D], F32, EI)
    selmat = ap_("selmat", [8, 128], BF16, EI)
    bidx = ap_("bidx", [128, 2, 72], mybir.dt.uint32, EI)
    cidx = ap_("cidx", [128, 64], mybir.dt.uint32, EI)
    xf_out = ap_("xf_out", [NLOC, D], F32, EO)

    qe_d = [ap_(f"qe{l}_d", [D, NLOC], BF16, IN) for l in range(2)]
    kt_d = [ap_(f"kt{l}_d", [KVD, NLOC], BF16, IN) for l in range(2)]
    v_d = [ap_(f"v{l}_d", [NLOC, 8 * 129], BF16, IN) for l in range(2)]
    aps_ = lambda name, shape, dt: nc.dram_tensor(name, shape, dt, kind=IN,
                                                  addr_space="Shared").ap()
    qeA = [[aps_(f"qeA{l}h{h_}", [8 * 2048, NLOC], BF16) for h_ in range(2)]
           for l in range(2)]
    ktA = [aps_(f"ktA{l}", [8 * KVD, NLOC], BF16) for l in range(2)]
    vA = [aps_(f"vA{l}", [8 * NLOC, 8 * 129], BF16) for l in range(2)]
    ot_d = [ap_(f"ot{l}_d", [NLOC, D], BF16, IN) for l in range(2)]
    otA_u = [[aps_(f"otA{l}u{u}", [8 * 512, D], BF16) for u in range(2)]
             for l in range(2)]
    x1_d = ap_("x1_d", [NLOC, D], F32, IN)
    x2_d = ap_("x2_d", [NLOC, D], F32, IN)
    x3_d = ap_("x3_d", [NLOC, D], F32, IN)

    with contextlib.ExitStack() as gstack:
        smb = Sems(nc, gstack)
        gp_pool = [gstack.enter_context(nc.semaphore(f"gp{i}")) for i in range(8)]
        ot_pool = [gstack.enter_context(nc.semaphore(f"pl{i}")) for i in range(48)]
        pools = (gp_pool, ot_pool)
        countss = ([0] * 8, [0] * 48)
        blk = gstack.enter_context(nc.Block())
        bixs = gstack.enter_context(nc.sbuf_tensor("bixs", [128, 2, 72], mybir.dt.uint32))
        cixs = gstack.enter_context(nc.sbuf_tensor("cixs", [128, 64], mybir.dt.uint32))

        @blk.gpsimd
        def _(e):
            smb.inc(e.dma_start(out=bixs[:, :, :], in_=bidx[:, :, :]), "ixld", 16)
            smb.inc(e.dma_start(out=cixs[:, :], in_=cidx[:, :]), "ixld", 16)
        ix_hdl = smb["ixld"]

        psm_box = [PoolSems(pools, countss)]

        def psm():
            return psm_box[0]

        def gate():
            cur = psm_box[0]
            waits = [(cur._h(k), cur.base[k] + v)
                     for k, v in cur.n.items() if v > 0]
            for engname in ("sync", "vector", "scalar", "tensor", "gpsimd"):
                dec = getattr(blk, engname)

                def _body(e, w=tuple(waits)):
                    for h, v in w:
                        e.wait_ge(h, v)
                dec(_body)
            psm_box[0] = PoolSems(pools, countss)

        def ag_phase(tagname, pairs):
            @blk.gpsimd
            def _(e):
                for (src, dst) in pairs:
                    inst = e.collective_compute(
                        "AllGather", ALU.bypass,
                        replica_groups=[list(range(8))],
                        ins=[src.opt()], outs=[dst.opt()])
                    smb.inc(inst, tagname, 1)
                e.wait_ge(smb[tagname], smb.val(tagname))
            gate()

        layer_x = [x_in, x2_d]
        layer_w = [(wq0, wk0, wv0, wo0), (wq1, wk1, wv1, wo1)]
        layer_xo = [x1_d, x3_d]

        for l in range(2):
            a_sm = psm()

            def mk_hook(a_sm, l):
                def hook(e, n):
                    if n == 10:
                        e.wait_ge(a_sm["qe_dma_0"], a_sm.base["qe_dma_0"] + 32)
                        e.wait_ge(a_sm["qe_dma_1"], a_sm.base["qe_dma_1"] + 32)
                        inst = e.collective_compute(
                            "AllGather", ALU.bypass,
                            replica_groups=[list(range(8))],
                            ins=[kt_d[l].opt()], outs=[ktA[l].opt()])
                        smb.inc(inst, f"cc{l}", 1)
                    elif n == 12:
                        e.wait_ge(a_sm["v_dma"], a_sm.base["v_dma"] + 16)
                        inst = e.collective_compute(
                            "AllGather", ALU.bypass,
                            replica_groups=[list(range(8))],
                            ins=[v_d[l].opt()], outs=[vA[l].opt()])
                        smb.inc(inst, f"cc{l}", 1)
                    elif n == 18:
                        e.wait_ge(a_sm["qe_dma_0"], a_sm.base["qe_dma_0"] + 96)
                        e.wait_ge(a_sm["qe_dma_1"], a_sm.base["qe_dma_1"] + 96)
                        inst = e.collective_compute(
                            "AllGather", ALU.bypass,
                            replica_groups=[list(range(8))],
                            ins=[qe_d[l][0:2048, :].opt()],
                            outs=[qeA[l][0].opt()])
                        smb.inc(inst, f"cc{l}", 1)
                return hook

            with contextlib.ExitStack() as ps:
                emit_launch_A(nc, blk, ps, a_sm, layer_x[l],
                              layer_w[l][0], layer_w[l][1], layer_w[l][2],
                              cost, sint, ident, qe_d[l], kt_d[l], v_d[l],
                              tag=f"a{l}", gp_hook=mk_hook(a_sm, l))

            @blk.gpsimd
            def _(e, a_sm=a_sm, l=l):
                e.wait_ge(a_sm["qe_dma_0"], a_sm.base["qe_dma_0"] + 160)
                e.wait_ge(a_sm["qe_dma_1"], a_sm.base["qe_dma_1"] + 160)
                inst = e.collective_compute(
                    "AllGather", ALU.bypass,
                    replica_groups=[list(range(8))],
                    ins=[qe_d[l][2048:4096, :].opt()], outs=[qeA[l][1].opt()])
                smb.inc(inst, f"cc{l}", 1)
            gate()
            cc_hdl = smb[f"cc{l}"]
            cc_v1 = smb.val(f"cc{l}")
            cc_v0 = cc_v1 - 1
            with contextlib.ExitStack() as ps:
                smB = psm()
                sh = emit_B_shared(nc, blk, ps, smB, trimask, ident, tag=f"bs{l}")
                emit_launch_B(nc, blk, ps, smB, sh, None, None, None,
                              ot_d[l][0:512, :], tag=f"u0l{l}",
                              fused=dict(qeA0=qeA[l][0], qeA1=qeA[l][1],
                                         ktA=ktA[l], vA=vA[l], u=0,
                                         cc=(cc_hdl, cc_v0, cc_v1),
                                         ixs=bixs, ix_hdl=ix_hdl, ix_val=32))
                emit_launch_B(nc, blk, ps, smB, sh, None, None, None,
                              ot_d[l][512:1024, :], tag=f"u1l{l}", prev=f"u0l{l}",
                              fused=dict(qeA0=qeA[l][0], qeA1=qeA[l][1],
                                         ktA=ktA[l], vA=vA[l], u=1,
                                         cc=(cc_hdl, cc_v0, cc_v1),
                                         ixs=bixs, ix_hdl=ix_hdl, ix_val=32))
            b_sm = psm()

            @blk.gpsimd
            def _(e, b_sm=b_sm, l=l):
                e.wait_ge(b_sm[f"u0l{l}b_odma"], b_sm.base[f"u0l{l}b_odma"] + 16)
                inst = e.collective_compute(
                    "AllGather", ALU.bypass, replica_groups=[list(range(8))],
                    ins=[ot_d[l][0:512, :].opt()], outs=[otA_u[l][0].opt()])
                smb.inc(inst, f"cco{l}", 1)
                e.wait_ge(b_sm[f"u1l{l}b_odma"], b_sm.base[f"u1l{l}b_odma"] + 16)
                inst = e.collective_compute(
                    "AllGather", ALU.bypass, replica_groups=[list(range(8))],
                    ins=[ot_d[l][512:1024, :].opt()], outs=[otA_u[l][1].opt()])
                smb.inc(inst, f"cco{l}", 1)
                e.wait_ge(smb[f"cco{l}"], smb.val(f"cco{l}"))
            gate()
            with contextlib.ExitStack() as ps:
                emit_launch_C(nc, blk, ps, psm(), None,
                              layer_w[l][3], layer_x[l], layer_xo[l],
                              tag=f"c{l}",
                              fused_ot=(otA_u[l], cixs, ix_hdl, 32))
            gate()
            if l == 0:
                with contextlib.ExitStack() as ps:
                    emit_launch_D(nc, blk, ps, psm(), x1_d, wcat,
                                  b24, gate_in, wbr, ident, x2_d, tag="d")
                gate()
            else:
                with contextlib.ExitStack() as ps:
                    emit_launch_H(nc, blk, ps, psm(), x3_d, wbase,
                                  bbase, wgt, bthr, acat, bcat, selmat, ident,
                                  xf_out, tag="h")
                gate()
    return nc


def _wshape(w, hc_cols):
    din, dout = w.shape
    n = dout // hc_cols
    return np.ascontiguousarray(
        w.reshape(32, 128, n, hc_cols).transpose(2, 1, 0, 3))


def _cshape(w):
    return np.ascontiguousarray(w.reshape(32, 128, -1).transpose(1, 0, 2))


def kernel_fused(inputs):
    tri, ident, sel = _consts()
    cosf, sinf = rope_tables()
    x = np.ascontiguousarray(np.asarray(inputs["L0_attn_x"], dtype=np.float32))
    p = perm_interleave()
    wq0 = np.ascontiguousarray(np.asarray(inputs["L0_attn_wq"])[:, np.concatenate([h * HD + p for h in range(NHQ)])])
    wk0 = np.ascontiguousarray(np.asarray(inputs["L0_attn_wk"])[:, np.concatenate([g * HD + p for g in range(NKV)])])
    wcat = _cshape(np.concatenate([np.asarray(inputs["L0_ffn_W_res"]),
                                   np.asarray(inputs["L0_ffn_W_pre"]),
                                   np.asarray(inputs["L0_ffn_W_post"])],
                                  0).T.astype(np.float32))
    b24 = np.concatenate([np.asarray(inputs["L0_ffn_H_res_base"]).reshape(16),
                          np.asarray(inputs["L0_ffn_H_pre_base"]),
                          np.asarray(inputs["L0_ffn_H_post_base"])]).astype(np.float32)
    b24r = np.ascontiguousarray(np.broadcast_to(b24, (128, 24)))
    gater = np.full((128, 1), np.asarray(inputs["L0_ffn_gate"])[0], np.float32)
    wbr0 = np.asarray(inputs["L0_ffn_W_branch"], dtype=np.float32)
    wbr = np.ascontiguousarray(wbr0.reshape(8, 128, 1024).transpose(1, 0, 2))
    wbase = _wshape(np.asarray(inputs["L1_ffn_W_base"]).T.astype(np.float32), 256)
    bbase = np.ascontiguousarray(np.asarray(inputs["L1_ffn_b_base"])[None, :].astype(np.float32))
    wgt = _cshape(np.concatenate([np.asarray(inputs["L1_ffn_W_gate"]).T,
                                  np.asarray(inputs["L1_ffn_W_thr"])[:, None]],
                                 1).astype(np.float32))
    bthr = np.full((128, 1), np.asarray(inputs["L1_ffn_b_thr"])[0], np.float32)
    acat = _cshape(np.asarray(inputs["L1_ffn_A_w"]).reshape(128, D).T.astype(np.float32))
    bcat = np.ascontiguousarray(
        np.asarray(inputs["L1_ffn_B_w"]).transpose(0, 2, 1).reshape(128, D).astype(np.float32))

    def build_bidx(c):
        bq = c // 4
        arr = np.zeros((128, 2, 72), np.uint32)
        p = np.arange(128, dtype=np.uint32)
        for u in range(2):
            gq = (c % 4) * 2 + u
            for n, (kind, j, j2) in enumerate(b_gather_specs()):
                cw, h = chunk_owner(j)
                if kind == "q":
                    row = (bq * 4 + cw) * 2048 + gq * 256 + (j2 % 2) * 128 + p
                    arr[:, u, n] = row * 1024 + h * 512
                elif kind == "k":
                    row = bq * 4096 + cw * 1024 + gq * 128 + p
                    arr[:, u, n] = row * 1024 + h * 512
                else:
                    row = bq * 4096 + cw * 1024 + h * 512 + j2 * 128 + p
                    arr[:, u, n] = row * 1032 + gq * 129
        return arr

    def build_cidx(c):
        bq = c // 4
        cc = c % 4
        arr = np.zeros((128, 64), np.uint32)
        p = np.arange(128, dtype=np.uint32)
        for n, (g, half, a) in enumerate(c_gather_specs()):
            row = (bq * 4 + g // 2) * 512 + a * 128 + p
            col = cc * 512 if half == 0 else 3584 - cc * 512
            arr[:, n] = row * 4096 + col
        return arr

    wq0s = _wshape(wq0, 256)
    wk0s = _wshape(wk0, 256)
    wv0s = _wshape(np.asarray(inputs["L0_attn_wv"], dtype=np.float32), 256)
    wo0s = _wshape(np.asarray(inputs["L0_attn_wo"], dtype=np.float32), 512)
    wq1s = _wshape(np.asarray(inputs["L1_attn_Wq"], dtype=np.float32), 256)
    wk1s = _wshape(np.asarray(inputs["L1_attn_Wk"], dtype=np.float32), 256)
    wv1s = _wshape(np.asarray(inputs["L1_attn_Wv"], dtype=np.float32), 256)
    wo1s = _wshape(np.asarray(inputs["L1_attn_Wo"], dtype=np.float32), 512)

    nc = build_nc_fused()
    in_maps = []
    for c in range(NCORES):
        b = c // 4
        pos = core_positions(c)
        in_maps.append({
            "bidx": build_bidx(c), "cidx": build_cidx(c),
            "x_in": np.ascontiguousarray(x[b, pos, :]),
            "wq0": wq0s, "wk0": wk0s, "wv0": wv0s, "wo0": wo0s,
            "wq1": wq1s, "wk1": wk1s, "wv1": wv1s, "wo1": wo1s,
            "cost": np.ascontiguousarray(cosf[:, pos]).astype(np.float32),
            "sint": np.ascontiguousarray(sinf[:, pos]).astype(np.float32),
            "ident": ident, "trimask": tri,
            "wcat": wcat, "b24": b24r, "gate_in": gater, "wbr": wbr,
            "wbase": wbase, "bbase": bbase, "wgt": wgt, "bthr": bthr,
            "acat": acat, "bcat": bcat, "selmat": sel,
        })
    res = run_bass_kernel_spmd(nc, in_maps, core_ids=list(range(NCORES))).results
    out = np.zeros((B, S, D), np.float32)
    for c in range(NCORES):
        b = c // 4
        pos = core_positions(c)
        out[b, pos, :] = res[c]["xf_out"]
    return out


_TRI = None


def _consts():
    global _TRI
    if _TRI is None:
        tri = np.triu(np.ones((128, 128), np.float32)).astype(BF)
        ident = np.eye(128, dtype=BF)
        sel = np.zeros((8, 128), np.float32)
        for e_ in range(8):
            sel[e_, e_ * 16:(e_ + 1) * 16] = 1.0
        _TRI = (tri, ident, sel.astype(BF))
    return _TRI


def kernel(**inputs):
    return kernel_fused(inputs)
